# version 03
# baseline (speedup 1.0000x reference)
import math
import numpy as np
import ml_dtypes
import concourse.bass as bass
import concourse.mybir as mybir
from concourse.bass_utils import run_bass_kernel_spmd

BFNP = ml_dtypes.bfloat16
F32 = mybir.dt.float32
BF16 = mybir.dt.bfloat16
AF = mybir.ActivationFunctionType
ALU = mybir.AluOpType
AX = mybir.AxisListType

DM = 2048
NCH = DM // 128
NH, HD = 16, 128
TOWN = 1024
TPRE = 3072
SGTOK = 1024
B, SEQ = 2, 4096
EPS_LN = 1e-5
EPS_RMS = 1e-6
GLD_C = 1.0 / np.e
GLD_L = 0.5 - math.log(4.0 / 3.0)
GLD_U = 0.5
NEXP = 8
INTER = 8192
SCALE_H = 64.0


def _tiles_km(w_km, m_order=None):
    K, M = w_km.shape
    nK, nM = K // 128, M // 128
    t = w_km.reshape(nK, 128, nM, 128).transpose(2, 0, 1, 3)
    if m_order is not None:
        t = t[np.asarray(m_order)]
    return np.ascontiguousarray(t).reshape(-1, 128)


def _const_tiles():
    c = {}
    c["c_id"] = np.eye(128, dtype=np.float32)
    m = np.arange(128, dtype=np.float32)
    c["c_arrq"] = np.broadcast_to(m[None, :] + 1.0, (128, 128)).copy()
    c["c_arrk"] = np.broadcast_to(127.0 - m[None, :], (128, 128)).copy()
    d = m[None, :] - m[:, None]
    c["c_delta"] = np.where(d >= 0, d, 1e9).astype(np.float32)
    c["c_ones_col"] = np.ones((128, 1), dtype=np.float32)
    c["c_ones_row"] = np.ones((1, 128), dtype=np.float32)
    return c


class _E:

    def __init__(self, nc, sem_names):
        self.nc = nc
        self._cm = []
        self.sems = {}
        for n in sem_names:
            cm = nc.semaphore(n)
            self.sems[n] = cm.__enter__()
            self._cm.append(cm)
        self.cnt = {n: 0 for n in sem_names}
        self.hw = {}

    def close(self):
        for cm in reversed(self._cm):
            cm.__exit__(None, None, None)

    def inc(self, instr, sem, k=1):
        instr.then_inc(self.sems[sem], k)
        self.cnt[sem] += k
        return self.cnt[sem]

    def wait(self, eng, sem, val):
        if val <= 0:
            return
        key = (eng.engine, sem)
        if self.hw.get(key, 0) >= val:
            return
        eng.wait_ge(self.sems[sem], val)
        self.hw[key] = val

    def ss(self, eng, instr, sem):
        self.inc(instr, sem, 1)
        self.wait(eng, sem, self.cnt[sem])


class _Arena:

    ALIGN = 16

    def __init__(self, nc, cols):
        self._cm = nc.sbuf_tensor("arena", [128, cols], BF16)
        self.t = self._cm.__enter__()
        self.cols = cols
        self.off = 0
        self.marks = []
        self.peak = 0

    def close(self):
        self._cm.__exit__(None, None, None)

    def mark(self):
        self.marks.append(self.off)

    def release(self):
        self.off = self.marks.pop()

    def bf(self, cols):
        cols = -(-cols // self.ALIGN) * self.ALIGN
        assert self.off + cols <= self.cols, (self.off, cols, self.cols)
        ap = self.t.ap()[:, self.off:self.off + cols]
        self.off += cols
        self.peak = max(self.peak, self.off)
        return ap

    def f32(self, cols):
        return self.bf(2 * cols).bitcast(F32)


class _WQ:

    def __init__(self, E, src_ap, n_tiles, slab_tiles, buf_ap):
        self.E = E
        self.src = src_ap
        self.n = n_tiles
        self.st = slab_tiles
        self.buf = buf_ap
        self.nslab = -(-n_tiles // slab_tiles)
        self.wq_mark = []
        self.sfree0 = E.cnt["sfree"]
        self.cur = -1

    def emit_loads(self, g, nslab=None):
        E = self.E
        for s in range(len(self.wq_mark), min(x for x in [self.nslab, nslab]
                                              if x is not None)):
            E.wait(g, "sfree", self.sfree0 + max(0, s - 1))
            t0 = s * self.st
            nt = min(self.st, self.n - t0)
            src = self.src[t0 * 128:(t0 + nt) * 128, :].rearrange(
                "(t p) m -> p t m", p=128)
            dst = self.buf[:, (s % 2) * self.st * 128:
                           (s % 2) * self.st * 128 + nt * 128].rearrange(
                "p (t m) -> p t m", m=128)
            E.inc(g.dma_start(out=dst, in_=src), "wq", 16)
            self.wq_mark.append(E.cnt["wq"])

    def tile(self, t_eng, i):
        s = i // self.st
        if s != self.cur:
            self.E.wait(t_eng, "wq", self.wq_mark[s])
            self.cur = s
        off = (s % 2) * self.st * 128 + (i - s * self.st) * 128
        return self.buf[:, off:off + 128]

    def ends_slab(self, i):
        s = i // self.st
        return i == min(self.n, (s + 1) * self.st) - 1



SEMS = ["wq", "xq", "oq", "sfree", "pe", "act", "dve", "pool"]


class _Prog:

    def __init__(self, arena_cols=100352):
        self.nc = bass.Bass("TRN2", target_bir_lowering=False, debug=False,
                            num_devices=8)
        self.dram = {}
        self.E = _E(self.nc, SEMS)
        self.arena = _Arena(self.nc, arena_cols)
        self.psum_cm = []
        self.ps = {}
        for name, shape, dt in [
            ("g0", [128, 512], F32), ("g1", [128, 512], F32),
            ("g2", [128, 512], F32), ("g3", [128, 512], F32),
            ("sm", [128, 512], F32), ("tp", [128, 1024], BF16),
            ("row", [1, 1024], F32),
        ]:
            cm = self.nc.psum_tensor(name, shape, dt)
            self.ps[name] = cm.__enter__()
            self.psum_cm.append(cm)
        self.block_cm = self.nc.Block()
        self.block = self.block_cm.__enter__()
        self.gbanks = [self.ps[k].ap() for k in ("g0", "g1", "g2", "g3")]
        self.bstate = {}
        self.dummy = self.arena.bf(16)[0:1, 0:1]

    def dcl(self, name, shape, dt, kind="ExternalInput"):
        if name not in self.dram:
            self.dram[name] = self.nc.dram_tensor(name, shape, dt,
                                                  kind=kind).ap()
        return self.dram[name]

    def bw(self, eng, key):
        st = self.bstate.get(key)
        if st:
            self.E.wait(eng, st[0], st[1])

    def bs(self, key, sem, val):
        self.bstate[key] = (sem, val)

    def finish(self):
        self.block_cm.__exit__(None, None, None)
        for cm in reversed(self.psum_cm):
            cm.__exit__(None, None, None)
        self.arena.close()
        self.E.close()
        return self.nc


def _mm_pipeline(P, n_groups, emit_mms, emit_evac, evac="act", depth=4):
    E, block = P.E, P.block
    pe0 = E.cnt["pe"]
    ev0 = E.cnt[evac]

    @block.tensor
    def _(t):
        for j in range(n_groups):
            bi = j % depth
            if j >= depth:
                E.wait(t, evac, ev0 + j - depth + 1)
            else:
                P.bw(t, ("g", bi))
            last = emit_mms(t, j, P.gbanks[bi])
            E.inc(last, "pe", 1)

    sec = block.scalar if evac == "act" else block.vector

    @sec
    def _(e):
        for j in range(n_groups):
            E.wait(e, "pe", pe0 + j + 1)
            last = emit_evac(e, j, P.gbanks[j % depth])
            E.inc(last, evac, 1)

    for j in range(max(0, n_groups - depth), n_groups):
        P.bs(("g", j % depth), evac, ev0 + j + 1)


def _emit_consts(P, dram):
    nc, E, block, ar = P.nc, P.E, P.block, P.arena
    C = {}
    C["id_bf"] = ar.bf(128)
    C["ones_col_bf"] = ar.bf(16)[:, 0:1]
    C["ones_row_bf"] = ar.bf(128)[0:1, :]
    C["arrq"] = ar.f32(128)
    C["arrk"] = ar.f32(128)
    C["delta"] = ar.f32(128)
    C["slope"] = ar.f32(16)
    C["sneg"] = ar.f32(16)
    C["temp"] = ar.f32(16)
    C["rtemp"] = ar.f32(16)
    C["nw"] = ar.f32(16)
    C["bdec"] = ar.f32(16)
    C["qdec_bf"] = ar.bf(NH * 128)
    C["kdec_bf"] = ar.bf(NH * 128)
    C["dt_bf"] = ar.bf(NH * 128)
    C["id_f32"] = ar.f32(128)
    C["sel8_bf"] = ar.bf(8 * 128)
    C["rwE_bf"] = ar.bf(1024)

    @block.gpsimd
    def _(g):
        E.inc(g.dma_start(out=C["id_bf"], in_=dram["c_id"]), "wq", 16)
        E.inc(g.dma_start(out=C["ones_col_bf"], in_=dram["c_ones_col"]), "wq", 16)
        E.inc(g.dma_start(out=C["ones_row_bf"], in_=dram["c_ones_row"]), "wq", 16)
    wq_c = E.cnt["wq"]

    @block.sync
    def _(sy):
        for k, src in [("arrq", "c_arrq"), ("arrk", "c_arrk"),
                       ("delta", "c_delta"), ("slope", "c_slope"),
                       ("nw", "c_nw")]:
            E.inc(sy.dma_start(out=C[k], in_=dram[src]), "xq", 16)
        E.inc(sy.dma_start(out=C["temp"][:, 0:1], in_=dram["c_temp"]), "xq", 16)
        E.inc(sy.dma_start(out=C["id_f32"], in_=dram["c_id"]), "xq", 16)
    xq_c = E.cnt["xq"]

    @block.vector
    def _(v):
        E.wait(v, "xq", xq_c)
        v.tensor_scalar_mul(C["sneg"], C["slope"], -1.0)
        v.tensor_scalar_mul(C["bdec"], C["slope"], -128.0)
        E.inc(v.reciprocal(C["rtemp"][:, 0:1], C["temp"][:, 0:1]), "dve", 1)
    dve_c = E.cnt["dve"]

    @block.scalar
    def _(sc):
        E.wait(sc, "dve", dve_c)
        sc.activation(C["bdec"], C["bdec"], AF.Exp)
        last = None
        for h in range(NH):
            s = C["sneg"][:, h:h + 1]
            sc.activation(C["qdec_bf"][:, h * 128:(h + 1) * 128], C["arrq"],
                          AF.Exp, scale=s)
            sc.activation(C["kdec_bf"][:, h * 128:(h + 1) * 128], C["arrk"],
                          AF.Exp, scale=s)
            last = sc.activation(C["dt_bf"][:, h * 128:(h + 1) * 128],
                                 C["delta"], AF.Exp, scale=s)
        E.inc(last, "act", 1)
    C["ready_act"] = E.cnt["act"]
    C["ready_wq"] = wq_c
    return C


def _emit_ln(P, C, x_src_dram, src_t0, t_total, xn_dst, dst_t_total, dst_t0, S):
    E, block = P.E, P.block
    xbf, x2b, tmpf = S["xbf"], S["x2b"], S["tmpf"]
    rowm, rowr = S["rowm"], S["rowr"]
    rowm_bf, rowr_bf = S["rowm_bf"], S["rowr_bf"]
    mbc, rbc = S["mbc"], S["rbc"]
    row = P.ps["row"].ap()

    for grp in range(t_total // 512):
        t0 = src_t0 + grp * 512
        dve_now = E.cnt["dve"]

        @block.gpsimd
        def _(g):
            E.wait(g, "dve", dve_now)
            for ch in range(NCH):
                E.inc(g.dma_start(
                    out=xbf[:, ch * 512:(ch + 1) * 512],
                    in_=x_src_dram[ch * 128:(ch + 1) * 128, t0:t0 + 512]),
                    "xq", 16)
        xq0 = E.cnt["xq"] - 16 * NCH

        pe0 = E.cnt["pe"]
        act0 = E.cnt["act"]

        @block.scalar
        def _(sc):
            for ch in range(NCH):
                E.wait(sc, "xq", xq0 + 16 * (ch + 1))
                if ch < 2:
                    E.wait(sc, "pe", pe0)
                else:
                    E.wait(sc, "pe", pe0 + ch - 1)
                E.inc(sc.activation(x2b[:, (ch % 2) * 512:(ch % 2 + 1) * 512],
                                    xbf[:, ch * 512:(ch + 1) * 512],
                                    AF.Square), "act", 1)

        @block.tensor
        def _(t):
            P.bw(t, "row")
            for ch in range(NCH):
                E.wait(t, "act", act0 + ch + 1)
                t.matmul(row[0:1, 0:512], C["ones_col_bf"],
                         xbf[:, ch * 512:(ch + 1) * 512],
                         start=(ch == 0), stop=(ch == NCH - 1))
                last = t.matmul(row[0:1, 512:1024], C["ones_col_bf"],
                                x2b[:, (ch % 2) * 512:(ch % 2 + 1) * 512],
                                start=(ch == 0), stop=(ch == NCH - 1))
                E.inc(last, "pe", 1)
        pe_stats = E.cnt["pe"]

        act_now = E.cnt["act"]

        @block.vector
        def _(v):
            E.wait(v, "pe", pe_stats)
            E.wait(v, "act", act_now)
            E.ss(v, v.tensor_scalar_mul(rowm, row[0:1, 0:512], 1.0 / DM),
                 "dve")
            E.ss(v, v.tensor_tensor(rowr, rowm, rowm, ALU.mult), "dve")
            E.ss(v, v.scalar_tensor_tensor(rowr, row[0:1, 512:1024], 1.0 / DM,
                                           rowr, ALU.mult, ALU.subtract),
                 "dve")
            E.ss(v, v.tensor_scalar_add(rowr, rowr, EPS_LN), "dve")
            E.inc(v.reciprocal(rowr, rowr), "dve", 1)
        dve_rows = E.cnt["dve"]
        P.bs("row", "dve", dve_rows)

        @block.scalar
        def _(sc):
            E.wait(sc, "dve", dve_rows)
            sc.activation(rowr, rowr, AF.Sqrt)
            sc.copy(rowm_bf, rowm)
            E.inc(sc.copy(rowr_bf, rowr), "act", 1)
        act_rows = E.cnt["act"]

        @block.tensor
        def _(t):
            E.wait(t, "act", act_rows)
            P.bw(t, ("g", 0))
            P.bw(t, ("g", 1))
            t.matmul(P.gbanks[0], C["ones_row_bf"], rowm_bf,
                     start=True, stop=True)
            E.inc(t.matmul(P.gbanks[1], C["ones_row_bf"], rowr_bf,
                           start=True, stop=True), "pe", 1)
        pe_bc = E.cnt["pe"]

        @block.vector
        def _(v):
            E.wait(v, "pe", pe_bc)
            v.tensor_copy(mbc, P.gbanks[0])
            v.tensor_copy(rbc, P.gbanks[1])
            for ch in range(NCH):
                v.tensor_tensor(tmpf, xbf[:, ch * 512:(ch + 1) * 512],
                                mbc, ALU.subtract)
                last = v.tensor_tensor(
                    xn_dst[:, ch * dst_t_total + dst_t0 + grp * 512:
                           ch * dst_t_total + dst_t0 + grp * 512 + 512],
                    tmpf, rbc, ALU.mult)
            E.inc(last, "dve", 1)
        P.bs(("g", 0), "dve", E.cnt["dve"])
        P.bs(("g", 1), "dve", E.cnt["dve"])


def _emit_proj(P, C, wq, xn_src, t_total, n_m, dst_of, func, tile0,
               scale_of=None, bias_of=None, evac_cb=None, pre_waits=(),
               post=None, post_pre=(), n_k=NCH, xn_row=None):
    halves = t_total // 512
    groups = [(mi, h) for mi in range(n_m) for h in range(halves)]
    E = P.E
    dve0 = E.cnt["dve"]
    xn_row = xn_row if xn_row is not None else t_total

    frees = {}

    def mms(t, j, bank):
        if j == 0:
            for sem, val in pre_waits:
                P.E.wait(t, sem, val)
        mi, half = groups[j]
        last = None
        for k in range(n_k):
            idx = tile0 + mi * n_k + k
            w = wq.tile(t, idx)
            last = t.matmul(bank, w,
                            xn_src[:, k * xn_row + half * 512:
                                   k * xn_row + half * 512 + 512],
                            start=(k == 0), stop=(k == n_k - 1))
            if half == halves - 1 and wq.ends_slab(idx):
                frees[j] = frees.get(j, 0) + 1
        return last

    def evac(e, j, bank):
        mi, half = groups[j]
        if post is not None and j >= 4:
            E.wait(e, "dve", dve0 + j - 3)
        kw = {}
        if scale_of is not None:
            kw["scale"] = scale_of(mi)
        if bias_of is not None:
            kw["bias"] = bias_of(mi)
        last = e.activation(dst_of(mi, half), bank, func, **kw)
        if evac_cb is not None:
            last = evac_cb(e, mi, half, bank) or last
        if j in frees:
            P.E.inc(e.copy(P.dummy, P.dummy), "sfree", frees[j])
        return last

    _mm_pipeline(P, len(groups), mms, evac, evac="act")

    if post is not None:
        ev0 = E.cnt["act"] - len(groups)

        @P.block.vector
        def _(v):
            for j, (mi, half) in enumerate(groups):
                E.wait(v, "act", ev0 + j + 1)
                if j == 0:
                    for sem, val in post_pre:
                        E.wait(v, sem, val)
                last = post(v, mi, half)
                E.inc(last, "dve", 1)
    return tile0 + n_m * n_k


def _emit_kv_prefix_chunk(P, C, kp, vp, tcols, ch, kv, L):
    E, block = P.E, P.block
    tp = P.ps["tp"].ap()
    for w in range(4):
        hs = [4 * w + i for i in range(4)]

        @block.vector
        def _(v):
            E.wait(v, "act", L["src_act"])
            P.bw(v, "lt_tmpkp")
            last = None
            for i, h in enumerate(hs):
                last = v.tensor_tensor(
                    L["tmp_kp"][:, i * 128:(i + 1) * 128],
                    kp[:, h * tcols + ch * 128: h * tcols + ch * 128 + 128],
                    C["kdec_bf"][:, h * 128:(h + 1) * 128], ALU.mult)
            E.inc(last, "dve", 1)
        dve_kp = E.cnt["dve"]

        @block.tensor
        def _(t):
            E.wait(t, "dve", dve_kp)
            P.bw(t, "tp")
            last = None
            for i, h in enumerate(hs):
                t.transpose(tp[:, i * 256:i * 256 + 128],
                            L["tmp_kp"][:, i * 128:(i + 1) * 128], C["id_bf"])
                last = t.transpose(
                    tp[:, i * 256 + 128:i * 256 + 256],
                    vp[:, h * tcols + ch * 128: h * tcols + ch * 128 + 128],
                    C["id_bf"])
            E.inc(last, "pe", 1)
        P.bs("lt_tmpkp", "pe", E.cnt["pe"])
        pe_tr = E.cnt["pe"]

        @block.scalar
        def _(sc):
            E.wait(sc, "pe", pe_tr)
            P.bw(sc, "lt_ktvt")
            last = None
            for i in range(4):
                sc.copy(L["kT"][:, i * 128:(i + 1) * 128],
                        tp[:, i * 256:i * 256 + 128])
                last = sc.copy(L["vT"][:, i * 128:(i + 1) * 128],
                               tp[:, i * 256 + 128:i * 256 + 256])
            E.inc(last, "act", 1)
        P.bs("tp", "act", E.cnt["act"])
        act_ev = E.cnt["act"]

        @block.tensor
        def _(t):
            E.wait(t, "act", act_ev)
            P.bw(t, ("g", 3))
            last = None
            for i in range(4):
                last = t.matmul(P.gbanks[3][:, i * 128:(i + 1) * 128],
                                L["kT"][:, i * 128:(i + 1) * 128],
                                L["vT"][:, i * 128:(i + 1) * 128],
                                start=True, stop=True)
            E.inc(last, "pe", 1)
        P.bs("lt_ktvt", "pe", E.cnt["pe"])
        pe_u = E.cnt["pe"]

        @block.vector
        def _(v):
            E.wait(v, "pe", pe_u)
            last = None
            for i, h in enumerate(hs):
                last = v.scalar_tensor_tensor(
                    kv[:, h * 128:(h + 1) * 128], kv[:, h * 128:(h + 1) * 128],
                    C["bdec"][:, h:h + 1],
                    P.gbanks[3][:, i * 128:(i + 1) * 128], ALU.mult, ALU.add)
            E.inc(last, "dve", 1)
        P.bs(("g", 3), "dve", E.cnt["dve"])
    L["kv_dve"] = E.cnt["dve"]


def _emit_light_own_chunk(P, C, q_sb, k_sb, v_sb, hg, ch, kv, kv_bf, o_sb, L):
    E, block = P.E, P.block
    tp = P.ps["tp"].ap()
    row = P.ps["row"].ap()
    n_w = HG // 4
    for w in range(n_w):
        hls = [4 * w + i for i in range(4)]

        @block.tensor
        def _(t):
            E.wait(t, "act", L["src_act"])
            P.bw(t, ("g", 2))
            last = None
            for i, hl in enumerate(hls):
                last = t.matmul(
                    P.gbanks[2][:, i * 128:(i + 1) * 128],
                    k_sb[:, hl * TOWN + ch * 128: hl * TOWN + ch * 128 + 128],
                    q_sb[:, hl * TOWN + ch * 128: hl * TOWN + ch * 128 + 128],
                    start=True, stop=True)
            E.inc(last, "pe", 1)
        pe_s = E.cnt["pe"]

        @block.vector
        def _(v):
            E.wait(v, "pe", pe_s)
            P.bw(v, "lt_tmpkp")
            last = None
            for i, hl in enumerate(hls):
                h = hg * HG + hl
                v.tensor_tensor(L["E"][:, i * 128:(i + 1) * 128],
                                P.gbanks[2][:, i * 128:(i + 1) * 128],
                                C["dt_bf"][:, h * 128:(h + 1) * 128], ALU.mult)
                v.tensor_tensor(
                    L["tmp_kp"][:, i * 128:(i + 1) * 128],
                    k_sb[:, hl * TOWN + ch * 128: hl * TOWN + ch * 128 + 128],
                    C["kdec_bf"][:, h * 128:(h + 1) * 128], ALU.mult)
                last = v.tensor_tensor(
                    L["tmp_qp"][:, i * 128:(i + 1) * 128],
                    q_sb[:, hl * TOWN + ch * 128: hl * TOWN + ch * 128 + 128],
                    C["qdec_bf"][:, h * 128:(h + 1) * 128], ALU.mult)
            E.inc(last, "dve", 1)
        P.bs(("g", 2), "dve", E.cnt["dve"])
        dve_e = E.cnt["dve"]

        @block.tensor
        def _(t):
            E.wait(t, "dve", dve_e)
            P.bw(t, "tp")
            last = None
            for i, hl in enumerate(hls):
                t.transpose(tp[:, i * 256:i * 256 + 128],
                            L["tmp_kp"][:, i * 128:(i + 1) * 128], C["id_bf"])
                last = t.transpose(
                    tp[:, i * 256 + 128:i * 256 + 256],
                    v_sb[:, hl * TOWN + ch * 128: hl * TOWN + ch * 128 + 128],
                    C["id_bf"])
            E.inc(last, "pe", 1)
        P.bs("lt_tmpkp", "pe", E.cnt["pe"])
        pe_tr = E.cnt["pe"]

        @block.scalar
        def _(sc):
            E.wait(sc, "pe", pe_tr)
            E.wait(sc, "dve", L["kv_dve"])
            P.bw(sc, "lt_ktvt")
            P.bw(sc, "lt_kvbf")
            last = None
            for i, hl in enumerate(hls):
                h = hg * HG + hl
                sc.copy(L["kT"][:, i * 128:(i + 1) * 128],
                        tp[:, i * 256:i * 256 + 128])
                sc.copy(L["vT"][:, i * 128:(i + 1) * 128],
                        tp[:, i * 256 + 128:i * 256 + 256])
                last = sc.copy(kv_bf[:, i * 128:(i + 1) * 128],
                               kv[:, h * 128:(h + 1) * 128])
            E.inc(last, "act", 1)
        P.bs("tp", "act", E.cnt["act"])
        act_ev = E.cnt["act"]

        @block.tensor
        def _(t):
            E.wait(t, "act", act_ev)
            P.bw(t, ("g", 1))
            P.bw(t, ("g", 3))
            last = None
            for i, hl in enumerate(hls):
                t.matmul(P.gbanks[1][:, i * 128:(i + 1) * 128],
                         L["vT"][:, i * 128:(i + 1) * 128],
                         L["E"][:, i * 128:(i + 1) * 128],
                         start=True, stop=False)
                t.matmul(P.gbanks[1][:, i * 128:(i + 1) * 128],
                         kv_bf[:, i * 128:(i + 1) * 128],
                         L["tmp_qp"][:, i * 128:(i + 1) * 128],
                         start=False, stop=True)
                last = t.matmul(P.gbanks[3][:, i * 128:(i + 1) * 128],
                                L["kT"][:, i * 128:(i + 1) * 128],
                                L["vT"][:, i * 128:(i + 1) * 128],
                                start=True, stop=True)
            E.inc(last, "pe", 1)
        P.bs("lt_ktvt", "pe", E.cnt["pe"])
        P.bs("lt_kvbf", "pe", E.cnt["pe"])
        pe_o = E.cnt["pe"]

        @block.vector
        def _(v):
            E.wait(v, "pe", pe_o)
            last = None
            for i, hl in enumerate(hls):
                h = hg * HG + hl
                last = v.scalar_tensor_tensor(
                    kv[:, h * 128:(h + 1) * 128], kv[:, h * 128:(h + 1) * 128],
                    C["bdec"][:, h:h + 1],
                    P.gbanks[3][:, i * 128:(i + 1) * 128], ALU.mult, ALU.add)
            E.inc(last, "dve", 1)
        P.bs(("g", 3), "dve", E.cnt["dve"])
        if w == n_w - 1:
            L["kv_dve"] = E.cnt["dve"]

        @block.scalar
        def _(sc):
            E.wait(sc, "pe", pe_o)
            P.bw(sc, "lt_o2")
            last = None
            for i, hl in enumerate(hls):
                h = hg * HG + hl
                sc.copy(o_sb[:, h * TOWN + ch * 128: h * TOWN + ch * 128 + 128],
                        P.gbanks[1][:, i * 128:(i + 1) * 128])
                last = sc.activation(L["o2"][:, i * 128:(i + 1) * 128],
                                     P.gbanks[1][:, i * 128:(i + 1) * 128],
                                     AF.Square)
            E.inc(last, "act", 1)
        P.bs(("g", 1), "act", E.cnt["act"])
        act_o = E.cnt["act"]

        @block.tensor
        def _(t):
            E.wait(t, "act", act_o)
            P.bw(t, "row")
            last = None
            for i in range(4):
                last = t.matmul(row[0:1, ch * 128:(ch + 1) * 128],
                                C["ones_col_bf"],
                                L["o2"][:, i * 128:(i + 1) * 128],
                                start=(w == 0 and i == 0),
                                stop=(w == n_w - 1 and i == 3))
            E.inc(last, "pe", 1)
        P.bs("lt_o2", "pe", E.cnt["pe"])


def _emit_ffn0(P, C, dram, rwE_bf, rwE_act, accum, h_sb, xnst, wbuf, gsl, usl,
               wbc, xn0_d, guard_oq=0):
    E, block = P.E, P.block
    base = dram["ffn0_t"]
    for e in range(NEXP):
        ebase = e * 768 * 128
        dve_bc0 = E.cnt["dve"]

        @block.tensor
        def _(t, e=e):
            E.wait(t, "act", rwE_act)
            P.bw(t, ("g", 2))
            P.bw(t, ("g", 3))
            t.matmul(P.gbanks[2], C["sel8_bf"][0:8, e * 128:(e + 1) * 128],
                     rwE_bf[0:8, 0:512], start=True, stop=True)
            E.inc(t.matmul(P.gbanks[3],
                           C["sel8_bf"][0:8, e * 128:(e + 1) * 128],
                           rwE_bf[0:8, 512:1024], start=True, stop=True),
                  "pe", 1)
        pe_bc = E.cnt["pe"]

        @block.vector
        def _(v):
            E.wait(v, "pe", pe_bc)
            v.tensor_copy(wbc[:, 0:512], P.gbanks[2])
            E.inc(v.tensor_copy(wbc[:, 512:1024], P.gbanks[3]), "dve", 1)
        P.bs(("g", 2), "dve", E.cnt["dve"])
        P.bs(("g", 3), "dve", E.cnt["dve"])
        wbc_dve = E.cnt["dve"]

        for half in range(2):
            @block.sync
            def _(sy, half=half):
                E.wait(sy, "oq", guard_oq)
                E.wait(sy, "pe", E.cnt["pe"])
                src = xn0_d[:, :].rearrange(
                    "p (k t) -> p k t", t=TOWN)[:, :, half * 512:
                                                half * 512 + 512]
                E.inc(sy.dma_start(
                    out=xnst.rearrange("p (k t) -> p k t", t=512), in_=src),
                    "xq", 16)
            xq_xn = E.cnt["xq"]

            wq = _WQ(E, base[ebase:ebase + 512 * 128, :], 512, 16, wbuf)

            @block.gpsimd
            def _(g):
                wq.emit_loads(g)

            pe0 = E.cnt["pe"]
            act0 = E.cnt["act"]
            dve0 = E.cnt["dve"]

            @block.tensor
            def _(t, half=half, wq=wq, pe0=pe0, act0=act0):
                E.wait(t, "xq", xq_xn)
                for F in range(16):
                    pair = F % 2
                    if F >= 2:
                        E.wait(t, "act", act0 + F - 1)
                    else:
                        P.bw(t, ("g", 2 * pair))
                        P.bw(t, ("g", 2 * pair + 1))
                    for k in range(NCH):
                        t.matmul(P.gbanks[2 * pair], wq.tile(t, F * 32 + k),
                                 xnst[:, k * 512:(k + 1) * 512],
                                 start=(k == 0), stop=(k == NCH - 1))
                    last = None
                    for k in range(NCH):
                        last = t.matmul(P.gbanks[2 * pair + 1],
                                        wq.tile(t, F * 32 + 16 + k),
                                        xnst[:, k * 512:(k + 1) * 512],
                                        start=(k == 0), stop=(k == NCH - 1))
                    E.inc(last, "pe", 1)

            @block.scalar
            def _(sc, pe0=pe0, dve0=dve0, wq=wq):
                for F in range(16):
                    pair = F % 2
                    E.wait(sc, "pe", pe0 + F + 1)
                    if F >= 2:
                        E.wait(sc, "dve", dve0 + F - 1)
                    sc.activation(gsl[:, pair * 512:(pair + 1) * 512],
                                  P.gbanks[2 * pair], AF.Silu)
                    last = sc.activation(usl[:, pair * 512:(pair + 1) * 512],
                                         P.gbanks[2 * pair + 1], AF.Copy)
                    E.inc(last, "act", 1)
                    E.inc(sc.copy(P.dummy, P.dummy), "sfree", 2)

            @block.vector
            def _(v, half=half, act0=act0, wbc_dve=wbc_dve):
                E.wait(v, "oq", guard_oq)
                E.wait(v, "dve", wbc_dve)
                for F in range(16):
                    pair = F % 2
                    E.wait(v, "act", act0 + F + 1)
                    v.tensor_tensor(gsl[:, pair * 512:(pair + 1) * 512],
                                    gsl[:, pair * 512:(pair + 1) * 512],
                                    usl[:, pair * 512:(pair + 1) * 512],
                                    ALU.mult)
                    last = v.tensor_tensor(
                        h_sb[:, F * TOWN + half * 512: F * TOWN + half * 512 + 512],
                        gsl[:, pair * 512:(pair + 1) * 512],
                        wbc[:, half * 512:(half + 1) * 512], ALU.mult)
                    E.inc(last, "dve", 1)
            for bi in range(4):
                P.bs(("g", bi), "act", E.cnt["act"])

        h_dve = E.cnt["dve"]

        wdq = _WQ(E, base[ebase + 512 * 128:ebase + 768 * 128, :], 256, 16,
                  wbuf)

        @block.gpsimd
        def _(g):
            wdq.emit_loads(g)

        def mms(t, j, bank, wdq=wdq, h_dve=h_dve):
            if j == 0:
                E.wait(t, "dve", h_dve)
            mi, half = divmod(j, 2)
            last = None
            for k in range(NCH):
                idx = mi * NCH + k
                last = t.matmul(bank, wdq.tile(t, idx),
                                h_sb[:, k * TOWN + half * 512:
                                     k * TOWN + half * 512 + 512],
                                start=(k == 0), stop=(k == NCH - 1))
                if half == 1 and wdq.ends_slab(idx):
                    frees_b[j] = frees_b.get(j, 0) + 1
            return last

        def evac(v, j, bank, e=e):
            mi, half = divmod(j, 2)
            dst = accum[:, mi * TOWN + half * 512: mi * TOWN + half * 512 + 512]
            if e == 0:
                last = v.tensor_copy(dst, bank)
            else:
                last = v.tensor_tensor(dst, dst, bank, ALU.add)
            if j in frees_b:
                P.E.inc(v.tensor_copy(P.dummy, P.dummy), "sfree", frees_b[j])
            return last

        frees_b = {}
        _mm_pipeline(P, 32, mms, evac, evac="dve")


def _emit_p5b(P, C, dram, accum, s0_d, x1_d, xn1_d, T5, guard_oq=0):
    E, block = P.E, P.block
    row = P.ps["row"].ap()
    xs, ss, tb = T5["xs"], T5["ss"], T5["tb"]
    upd, x1b, x1sq, xn1s = T5["upd"], T5["x1b"], T5["x1sq"], T5["xn1s"]
    dve_a = E.cnt["dve"]
    dve_x1 = {}
    pe_st = {}
    acts = {}
    pe0 = E.cnt["pe"]

    for m in range(NCH):
        @block.sync
        def _(sy, m=m):
            if m < 2:
                E.wait(sy, "dve", dve_a)
            else:
                E.wait(sy, "dve", dve_x1[m - 2])
            sl = xs[:, (m % 2) * 1024:(m % 2) * 1024 + 1024]
            E.inc(sy.dma_start(out=sl, in_=dram["xT"][m * 128:(m + 1) * 128, :]),
                  "xq", 16)
            E.inc(sy.dma_start(out=ss[:, (m % 2) * 1024:(m % 2) * 1024 + 1024],
                               in_=s0_d[:, m * TOWN:(m + 1) * TOWN]), "xq", 16)
        xq_m = E.cnt["xq"]

        @block.scalar
        def _(sc, m=m):
            E.wait(sc, "dve", dve_a)
            if m >= 2:
                E.wait(sc, "dve", dve_x1[m - 2])
            E.inc(sc.activation(tb[:, (m % 2) * 1024:(m % 2) * 1024 + 1024],
                                accum[:, m * TOWN:(m + 1) * TOWN], AF.Silu),
                  "act", 1)
        act_tb = E.cnt["act"]

        @block.vector
        def _(v, m=m, xq_m=xq_m, act_tb=act_tb):
            E.wait(v, "oq", guard_oq)
            E.wait(v, "xq", xq_m)
            E.wait(v, "act", act_tb)
            v.tensor_tensor(upd, ss[:, (m % 2) * 1024:(m % 2) * 1024 + 1024],
                            tb[:, (m % 2) * 1024:(m % 2) * 1024 + 1024],
                            ALU.mult)
            E.inc(v.tensor_tensor(accum[:, m * TOWN:(m + 1) * TOWN],
                                  xs[:, (m % 2) * 1024:(m % 2) * 1024 + 1024],
                                  upd, ALU.add), "dve", 1)
        dve_x1[m] = E.cnt["dve"]

        @block.sync
        def _(sy, m=m):
            E.wait(sy, "dve", dve_x1[m])
            E.inc(sy.dma_start(out=x1_d[:, m * TOWN:(m + 1) * TOWN],
                               in_=accum[:, m * TOWN:(m + 1) * TOWN]), "oq", 16)

        @block.scalar
        def _(sc, m=m):
            E.wait(sc, "dve", dve_x1[m])
            if m >= 2:
                E.wait(sc, "pe", pe_st[m - 2])
            sc.copy(x1b[:, (m % 2) * 1024:(m % 2) * 1024 + 1024],
                    accum[:, m * TOWN:(m + 1) * TOWN])
            E.inc(sc.activation(x1sq[:, (m % 2) * 1024:(m % 2) * 1024 + 1024],
                                accum[:, m * TOWN:(m + 1) * TOWN], AF.Square),
                  "act", 1)
        acts[m] = E.cnt["act"]

        @block.tensor
        def _(t, m=m):
            E.wait(t, "act", acts[m])
            if m == 0:
                P.bw(t, "row")
                P.bw(t, ("g", 0))
                P.bw(t, ("g", 1))
            b0 = (m % 2) * 1024
            t.matmul(row[0:1, 0:512], C["ones_col_bf"], x1b[:, b0:b0 + 512],
                     start=(m == 0), stop=(m == NCH - 1))
            t.matmul(row[0:1, 512:1024], C["ones_col_bf"],
                     x1b[:, b0 + 512:b0 + 1024],
                     start=(m == 0), stop=(m == NCH - 1))
            t.matmul(P.gbanks[0][0:1, :], C["ones_col_bf"],
                     x1sq[:, b0:b0 + 512],
                     start=(m == 0), stop=(m == NCH - 1))
            last = t.matmul(P.gbanks[1][0:1, :], C["ones_col_bf"],
                            x1sq[:, b0 + 512:b0 + 1024],
                            start=(m == 0), stop=(m == NCH - 1))
            E.inc(last, "pe", 1)
        pe_st[m] = E.cnt["pe"]

    pe_stats = E.cnt["pe"]
    mrow = T5["m1row"]
    vrow = T5["v1row"]
    mrow_bf, vrow_bf = T5["m1_bf"], T5["v1_bf"]

    @block.vector
    def _(v):
        E.wait(v, "pe", pe_stats)
        E.ss(v, v.tensor_scalar_mul(mrow, row[0:1, 0:1024], 1.0 / DM), "dve")
        E.ss(v, v.tensor_tensor(vrow, mrow, mrow, ALU.mult), "dve")
        v.scalar_tensor_tensor(vrow[0:1, 0:512], P.gbanks[0][0:1, :],
                               1.0 / DM, vrow[0:1, 0:512],
                               ALU.mult, ALU.subtract)
        E.ss(v, v.scalar_tensor_tensor(vrow[0:1, 512:1024],
                                       P.gbanks[1][0:1, :],
                                       1.0 / DM, vrow[0:1, 512:1024],
                                       ALU.mult, ALU.subtract), "dve")
        E.ss(v, v.tensor_scalar_add(vrow, vrow, EPS_LN), "dve")
        E.inc(v.reciprocal(vrow, vrow), "dve", 1)
    dve_rows = E.cnt["dve"]
    P.bs("row", "dve", dve_rows)
    P.bs(("g", 0), "dve", dve_rows)
    P.bs(("g", 1), "dve", dve_rows)

    @block.scalar
    def _(sc):
        E.wait(sc, "dve", dve_rows)
        sc.activation(vrow, vrow, AF.Sqrt)
        sc.copy(mrow_bf, mrow)
        E.inc(sc.copy(vrow_bf, vrow), "act", 1)
    act_rows = E.cnt["act"]

    m1bc, r1bc = T5["m1bc"], T5["r1bc"]

    @block.tensor
    def _(t):
        E.wait(t, "act", act_rows)
        P.bw(t, ("g", 2))
        P.bw(t, ("g", 3))
        t.matmul(P.gbanks[2], C["ones_row_bf"], mrow_bf[0:1, 0:512],
                 start=True, stop=True)
        E.inc(t.matmul(P.gbanks[3], C["ones_row_bf"], mrow_bf[0:1, 512:1024],
                       start=True, stop=True), "pe", 1)
    pe_b1 = E.cnt["pe"]

    @block.vector
    def _(v):
        E.wait(v, "pe", pe_b1)
        v.tensor_copy(m1bc[:, 0:512], P.gbanks[2])
        E.inc(v.tensor_copy(m1bc[:, 512:1024], P.gbanks[3]), "dve", 1)
    dve_m1 = E.cnt["dve"]
    P.bs(("g", 2), "dve", dve_m1)
    P.bs(("g", 3), "dve", dve_m1)

    @block.tensor
    def _(t):
        E.wait(t, "dve", dve_m1)
        t.matmul(P.gbanks[2], C["ones_row_bf"], vrow_bf[0:1, 0:512],
                 start=True, stop=True)
        E.inc(t.matmul(P.gbanks[3], C["ones_row_bf"], vrow_bf[0:1, 512:1024],
                       start=True, stop=True), "pe", 1)
    pe_b2 = E.cnt["pe"]

    @block.vector
    def _(v):
        E.wait(v, "pe", pe_b2)
        v.tensor_copy(r1bc[:, 0:512], P.gbanks[2])
        E.inc(v.tensor_copy(r1bc[:, 512:1024], P.gbanks[3]), "dve", 1)
    P.bs(("g", 2), "dve", E.cnt["dve"])
    P.bs(("g", 3), "dve", E.cnt["dve"])

    oq_xn = {}
    for m in range(NCH):
        @block.vector
        def _(v, m=m):
            if m >= 2:
                E.wait(v, "oq", oq_xn[m - 2])
            v.tensor_tensor(upd, accum[:, m * TOWN:(m + 1) * TOWN], m1bc,
                            ALU.subtract)
            E.inc(v.tensor_tensor(xn1s[:, (m % 2) * 1024:(m % 2) * 1024 + 1024],
                                  upd, r1bc, ALU.mult), "dve", 1)
        dve_xn = E.cnt["dve"]

        @block.sync
        def _(sy, m=m, dve_xn=dve_xn):
            E.wait(sy, "dve", dve_xn)
            E.inc(sy.dma_start(out=xn1_d[:, m * TOWN:(m + 1) * TOWN],
                               in_=xn1s[:, (m % 2) * 1024:(m % 2) * 1024 + 1024]),
                  "oq", 16)
        oq_xn[m] = E.cnt["oq"]



NPRECH = TPRE // 128
PSG = 512
NSG = TPRE // PSG
HG = 4


def _emit_rms_rows(P, C, var_acc, rstd_bc, rb_row):
    E, block = P.E, P.block
    dve_v = E.cnt["dve"]

    @block.vector
    def _(v):
        E.ss(v, v.tensor_scalar(var_acc[0:1, :], var_acc[0:1, :], 1.0 / DM,
                                EPS_RMS, ALU.mult, ALU.add), "dve")
        E.inc(v.reciprocal(var_acc[0:1, :], var_acc[0:1, :]), "dve", 1)
    dve_r = E.cnt["dve"]

    @block.scalar
    def _(sc):
        E.wait(sc, "dve", dve_r)
        sc.activation(var_acc[0:1, :], var_acc[0:1, :], AF.Sqrt)
        E.inc(sc.copy(rb_row, var_acc[0:1, :]), "act", 1)
    act_r = E.cnt["act"]

    @block.tensor
    def _(t):
        E.wait(t, "act", act_r)
        P.bw(t, ("g", 0))
        P.bw(t, ("g", 1))
        t.matmul(P.gbanks[0], C["ones_row_bf"], rb_row[0:1, 0:512],
                 start=True, stop=True)
        E.inc(t.matmul(P.gbanks[1], C["ones_row_bf"], rb_row[0:1, 512:1024],
                       start=True, stop=True), "pe", 1)
    pe_b = E.cnt["pe"]

    @block.vector
    def _(v):
        E.wait(v, "pe", pe_b)
        v.tensor_copy(rstd_bc[:, 0:512], P.gbanks[0])
        E.inc(v.tensor_copy(rstd_bc[:, 512:1024], P.gbanks[1]), "dve", 1)
    P.bs(("g", 0), "dve", E.cnt["dve"])
    P.bs(("g", 1), "dve", E.cnt["dve"])
    return E.cnt["dve"]


def _emit_router(P, C, dram, xn_own, wr_sb, rwE_bf, R, wq_ready=0):
    E, block = P.E, P.block
    row = P.ps["row"]

    pe0 = E.cnt["pe"]

    @block.tensor
    def _(t):
        E.wait(t, "wq", wq_ready)
        P.bw(t, ("g", 2))
        P.bw(t, ("g", 3))
        for half in range(2):
            for k in range(NCH):
                last = t.matmul(P.gbanks[2 + half][0:8, 0:512],
                                wr_sb[:, k * 8:(k + 1) * 8],
                                xn_own[:, k * TOWN + half * 512:
                                       k * TOWN + half * 512 + 512],
                                start=(k == 0), stop=(k == NCH - 1))
        E.inc(last, "pe", 1)
    pe_l = E.cnt["pe"]

    @block.vector
    def _(v):
        E.wait(v, "pe", pe_l)
        v.tensor_scalar_mul(R["inh"][0:8, 0:512], P.gbanks[2][0:8, 0:512],
                            C["rtemp"][0:8, 0:1])
        E.inc(v.tensor_scalar_mul(R["inh"][0:8, 512:1024],
                                  P.gbanks[3][0:8, 0:512],
                                  C["rtemp"][0:8, 0:1]), "dve", 1)
    dve_l = E.cnt["dve"]
    P.bs(("g", 2), "dve", dve_l)
    P.bs(("g", 3), "dve", dve_l)

    @block.scalar
    def _(sc):
        E.wait(sc, "dve", dve_l)
        E.inc(sc.activation(R["inh"], R["inh"], AF.Sigmoid), "act", 1)
    act_l = E.cnt["act"]

    @block.tensor
    def _(t):
        E.wait(t, "act", act_l)
        P.bw(t, ("g", 3))
        for tt in range(8):
            last = t.transpose(P.gbanks[3][:, tt * 8:(tt + 1) * 8],
                               R["inh"][0:8, tt * 128:(tt + 1) * 128],
                               C["id_f32"][0:8, 0:8])
        E.inc(last, "pe", 1)
    pe_t = E.cnt["pe"]

    @block.vector
    def _(v):
        E.wait(v, "pe", pe_t)
        E.inc(v.tensor_copy(R["tok"], P.gbanks[3][:, 0:64]), "dve", 1)
    dve_t = E.cnt["dve"]
    P.bs(("g", 3), "dve", dve_t)

    @block.scalar
    def _(sc):
        E.wait(sc, "dve", dve_t)
        E.inc(sc.activation(R["dist"], R["tok"], AF.Abs,
                            bias=R["negC"][:, 0:1]), "act", 1)
    act_d = E.cnt["act"]

    @block.vector
    def _(v):
        E.wait(v, "act", act_d)
        v.tensor_scalar_mul(R["e1"], R["dist"], -10.0)
        E.inc(v.tensor_scalar_mul(R["e2"], R["dist"], -10.0 / 3.0), "dve", 1)
    dve_d = E.cnt["dve"]

    @block.scalar
    def _(sc):
        E.wait(sc, "dve", dve_d)
        sc.activation(R["e1"], R["e1"], AF.Exp)
        E.inc(sc.activation(R["e2"], R["e2"], AF.Exp), "act", 1)
    act_e = E.cnt["act"]

    @block.vector
    def _(v):
        E.wait(v, "act", act_e)
        tok, e1, e2 = R["tok"], R["e1"], R["e2"]
        t1, t2, t3 = R["t1"], R["t2"], R["t3"]
        cw, cne, cni = R["cw"], R["cne"], R["cni"]
        cm, cs, cr = R["cm"], R["cs"], R["cr"]

        def r3(x):
            return x.rearrange("p (t e) -> p t e", e=8)

        v.tensor_single_scalar(t1, tok, GLD_L, ALU.is_ge)
        E.ss(v, v.tensor_single_scalar(t2, tok, GLD_U, ALU.is_le), "dve")
        E.ss(v, v.tensor_tensor(t1, t1, t2, ALU.mult), "dve")
        E.ss(v, v.tensor_tensor(t1, t1, e1, ALU.mult), "dve")
        E.ss(v, v.reduce_sum(cw, r3(t1), axis=AX.X), "dve")
        v.tensor_single_scalar(cne, cw, 1e-8, ALU.is_lt)
        E.ss(v, v.reduce_max(cm, r3(e2), axis=AX.X), "dve")
        for tt in range(8):
            v.tensor_single_scalar(t2[:, tt * 8:(tt + 1) * 8],
                                   e2[:, tt * 8:(tt + 1) * 8],
                                   cm[:, tt:tt + 1], ALU.is_ge)
        E.ss(v, v.tensor_scalar(cni, cne, -1.0, 1.0, ALU.mult, ALU.add),
             "dve")
        E.ss(v, v.scalar_tensor_tensor(t3, t2, -1e30, e2, ALU.mult, ALU.add),
             "dve")
        E.ss(v, v.reduce_max(cm, r3(t3), axis=AX.X), "dve")
        for tt in range(8):
            v.tensor_single_scalar(t3[:, tt * 8:(tt + 1) * 8],
                                   t3[:, tt * 8:(tt + 1) * 8],
                                   cm[:, tt:tt + 1], ALU.is_ge)
        E.ss(v, v.tensor_copy(P.dummy.bitcast(F32), P.dummy.bitcast(F32))
             if False else v.tensor_scalar_mul(cs, cw, 1.0), "dve")
        E.ss(v, v.tensor_tensor(t2, t2, t3, ALU.add), "dve")
        E.ss(v, v.tensor_tensor(t2, e2, t2, ALU.mult), "dve")
        E.ss(v, v.reduce_sum(cs, r3(t2), axis=AX.X), "dve")
        E.ss(v, v.tensor_single_scalar(cs, cs, 1e-8, ALU.max), "dve")
        E.ss(v, v.reciprocal(cr, cs), "dve")
        for tt in range(8):
            v.tensor_single_scalar(t2[:, tt * 8:(tt + 1) * 8],
                                   t2[:, tt * 8:(tt + 1) * 8],
                                   cr[:, tt:tt + 1], ALU.mult)
        for tt in range(8):
            v.tensor_single_scalar(t2[:, tt * 8:(tt + 1) * 8],
                                   t2[:, tt * 8:(tt + 1) * 8],
                                   cne[:, tt:tt + 1], ALU.mult)
        for tt in range(8):
            v.tensor_single_scalar(t1[:, tt * 8:(tt + 1) * 8],
                                   t1[:, tt * 8:(tt + 1) * 8],
                                   cni[:, tt:tt + 1], ALU.mult)
        E.ss(v, v.tensor_tensor(t1, t1, t2, ALU.add), "dve")
        E.ss(v, v.reduce_sum(cs, r3(t1), axis=AX.X), "dve")
        E.ss(v, v.tensor_single_scalar(cs, cs, 1e-8, ALU.max), "dve")
        E.ss(v, v.reciprocal(cr, cs), "dve")
        last = None
        for tt in range(8):
            last = v.tensor_single_scalar(R["wf"][:, tt * 8:(tt + 1) * 8],
                                          t1[:, tt * 8:(tt + 1) * 8],
                                          cr[:, tt:tt + 1], ALU.mult)
        E.inc(last, "dve", 1)
    dve_w = E.cnt["dve"]

    for rnd in range(2):
        @block.tensor
        def _(t, rnd=rnd):
            E.wait(t, "dve", dve_w)
            P.bw(t, ("g", 3))
            for i in range(4):
                tt = rnd * 4 + i
                last = t.transpose(P.gbanks[3][0:8, i * 128:(i + 1) * 128],
                                   R["wf"][:, tt * 8:(tt + 1) * 8],
                                   C["id_f32"])
            E.inc(last, "pe", 1)
        pe_w = E.cnt["pe"]

        @block.scalar
        def _(sc, rnd=rnd, pe_w=pe_w):
            E.wait(sc, "pe", pe_w)
            last = None
            for i in range(4):
                tt = rnd * 4 + i
                last = sc.copy(rwE_bf[0:8, tt * 128:(tt + 1) * 128],
                               P.gbanks[3][0:8, i * 128:(i + 1) * 128])
            E.inc(last, "act", 1)
        P.bs(("g", 3), "act", E.cnt["act"])
    return E.cnt["act"]


def _build_A(debug=None):
    P = _Prog()
    _emit_partA(P, fused=False, debug=debug)
    return P.finish(), P.dram, P.outs


def _emit_partA(P, fused, debug=None):
    nc, E, block, ar = P.nc, P.E, P.block, P.arena
    ar.mark()

    dram = P.dram
    for n, shape, dt in [
        ("c_id", [128, 128], F32), ("c_arrq", [128, 128], F32),
        ("c_arrk", [128, 128], F32), ("c_delta", [128, 128], F32),
        ("c_ones_col", [128, 1], F32), ("c_ones_row", [1, 128], F32),
        ("c_slope", [128, 16], F32), ("c_temp", [128, 1], F32),
        ("c_nw", [128, 16], F32),
        ("xT", [DM, TOWN], F32), ("xTpre", [DM, TPRE], F32),
        ("wqkv_t", [48 * NCH * 128, 128], F32),
        ("wkv_t", [32 * NCH * 128, 128], F32),
        ("wgate_t", [16 * NCH * 128, 128], F32),
        ("wout_t", [16 * NCH * 128, 128], F32),
        ("wr_t", [NCH * 128, 8], F32),
        ("c_sel8", [8, 8 * 128], F32),
        ("ffn0_t", [NEXP * 768 * 128, 128], F32),
    ]:
        P.dcl(n, shape, dt)

    bkind = "Internal" if fused else "ExternalOutput"
    xn0_d = nc.dram_tensor("xn0_d", [128, NCH * TOWN], BF16).ap()
    s0_d = nc.dram_tensor("s0_d", [128, NCH * TOWN], BF16).ap()
    x1_d = P.dcl("x1_d", [128, NCH * TOWN], F32, kind=bkind)
    xn1_d = P.dcl("xn1_d", [128, NCH * TOWN], BF16, kind=bkind)
    outs = P.outs = {}
    outs["x1_d"] = x1_d
    outs["xn1_d"] = xn1_d
    if debug == "p3":
        outs["o_d"] = nc.dram_tensor("o_d", [128, NH * TOWN], BF16,
                                     kind="ExternalOutput").ap()
        outs["var_d"] = nc.dram_tensor("var_d", [1, TOWN], F32,
                                       kind="ExternalOutput").ap()
        outs["kv_d"] = nc.dram_tensor("kv_d", [128, NH * 128], F32,
                                      kind="ExternalOutput").ap()
        for n in ("q_d", "k_d", "v_d"):
            outs[n] = nc.dram_tensor(n, [128, NH * TOWN], BF16,
                                     kind="ExternalOutput").ap()

    C = _emit_consts(P, dram)

    wbuf = ar.bf(2 * 16 * 128)
    ar.mark()
    xn_own = ar.bf(NCH * TOWN)
    kv = ar.f32(NH * 128)
    kv_bf = ar.bf(4 * 128)
    o_sb = ar.bf(NH * TOWN)
    var_acc = ar.f32(1024)
    L = {"tmp_kp": ar.bf(512), "tmp_qp": ar.bf(512), "E": ar.bf(512),
         "kT": ar.bf(512), "vT": ar.bf(512), "o2": ar.bf(512),
         "kv_dve": 0}
    S = {}
    S["xbf"] = ar.bf(NCH * 512)
    S["x2b"] = ar.bf(2 * 512)
    S["tmpf"] = ar.f32(512)
    r4 = ar.f32(1024)
    S["rowm"], S["rowr"] = r4[0:1, 0:512], r4[0:1, 512:1024]
    rb = ar.bf(1024)
    S["rowm_bf"], S["rowr_bf"] = rb[0:1, 0:512], rb[0:1, 512:1024]
    S["mbc"] = ar.f32(512)
    S["rbc"] = ar.f32(512)

    @block.vector
    def _(v):
        E.inc(v.memset(kv, 0.0), "dve", 1)

    ar.mark()
    xn_pre = ar.bf(NCH * PSG)
    kp = ar.bf(NH * PSG)
    vp = ar.bf(NH * PSG)
    for sg in range(NSG):
        _emit_ln(P, C, dram["xTpre"], sg * PSG, PSG, xn_pre, PSG, 0, S)
        xnp_ready = E.cnt["dve"]
        wkv_wq = _WQ(E, dram["wkv_t"], 32 * NCH, 16, wbuf)

        @block.gpsimd
        def _(g):
            if sg == 0:
                E.wait(g, "act", C["ready_act"])
            wkv_wq.emit_loads(g)

        def dst_kv(mi, half, kp=kp, vp=vp):
            h, typ = divmod(mi, 2)
            buf = (kp, vp)[typ]
            return buf[:, h * PSG + half * 512: h * PSG + half * 512 + 512]
        _emit_proj(P, C, wkv_wq, xn_pre, PSG, 32, dst_kv, AF.Silu, 0,
                   pre_waits=[("dve", xnp_ready)])
        L["src_act"] = E.cnt["act"]
        for ch in range(PSG // 128):
            _emit_kv_prefix_chunk(P, C, kp, vp, PSG, ch, kv, L)
    ar.release()

    _emit_ln(P, C, dram["xT"], 0, TOWN, xn_own, TOWN, 0, S)
    xn_own_ready = E.cnt["dve"]

    @block.sync
    def _(sy):
        E.wait(sy, "dve", xn_own_ready)
        E.inc(sy.dma_start(out=xn0_d, in_=xn_own), "oq", 16)

    ar.mark()
    q_sb = ar.bf(HG * TOWN)
    k_sb = ar.bf(HG * TOWN)
    v_sb = ar.bf(HG * TOWN)

    qkv_wq = _WQ(E, dram["wqkv_t"], 48 * NCH, 16, wbuf)

    @block.gpsimd
    def _(g):
        qkv_wq.emit_loads(g)

    tile0 = 0
    n_hg = NH // HG
    for hg in range(n_hg):
        def dst_of(mi, half, q_sb=q_sb, k_sb=k_sb, v_sb=v_sb):
            h_local, typ = divmod(mi, 3)
            buf = (q_sb, k_sb, v_sb)[typ]
            return buf[:, h_local * TOWN + half * 512:
                       h_local * TOWN + half * 512 + 512]
        pw = [("dve", xn_own_ready)]
        if debug == "p3":
            pw.append(("oq", E.cnt["oq"]))
        tile0 = _emit_proj(P, C, qkv_wq, xn_own, TOWN, 3 * HG, dst_of,
                           AF.Silu, tile0, pre_waits=pw)
        L["src_act"] = E.cnt["act"]
        if debug == "p3":
            act_q = E.cnt["act"]

            @block.sync
            def _(sy):
                E.wait(sy, "act", act_q)
                for nm, buf in (("q_d", q_sb), ("k_d", k_sb), ("v_d", v_sb)):
                    E.inc(sy.dma_start(
                        out=outs[nm][:, hg * HG * TOWN:(hg + 1) * HG * TOWN],
                        in_=buf), "oq", 16)
        for ch in range(TOWN // 128):
            _emit_light_own_chunk(P, C, q_sb, k_sb, v_sb, hg, ch,
                                  kv, kv_bf, o_sb, L)
        pe_var = E.cnt["pe"]

        @block.vector
        def _(v):
            E.wait(v, "pe", pe_var)
            if hg == 0:
                last = v.tensor_copy(var_acc[0:1, :], P.ps["row"].ap()[0:1, :])
            else:
                last = v.tensor_tensor(var_acc[0:1, :], var_acc[0:1, :],
                                       P.ps["row"].ap()[0:1, :], ALU.add)
            E.inc(last, "dve", 1)
        P.bs("row", "dve", E.cnt["dve"])
    ar.release()

    if debug == "p3":
        dve_f = E.cnt["dve"]
        act_f = E.cnt["act"]

        @block.sync
        def _(sy):
            E.wait(sy, "dve", dve_f)
            E.wait(sy, "act", act_f)
            E.inc(sy.dma_start(out=outs["o_d"], in_=o_sb), "oq", 16)
            E.inc(sy.dma_start(out=outs["var_d"], in_=var_acc[0:1, :]),
                  "oq", 16)
            E.inc(sy.dma_start(out=outs["kv_d"], in_=kv), "oq", 16)
            E.wait(sy, "oq", E.cnt["oq"])
        ar.release()
        return

    ar.mark()
    rb_row = ar.bf(1024)[0:1, :]
    rstd_bc = ar.f32(1024)
    gsl4 = ar.bf(4 * 512)
    ztmp = ar.f32(512)
    s_sb = ar.bf(NCH * TOWN)
    wr_sb = ar.bf(NCH * 8)
    R = {"inh": ar.f32(1024)[0:8, :], "tok": ar.f32(64), "dist": ar.f32(64),
         "e1": ar.f32(64), "e2": ar.f32(64), "t1": ar.f32(64),
         "t2": ar.f32(64), "t3": ar.f32(64), "cw": ar.f32(8),
         "cne": ar.f32(8), "cni": ar.f32(8), "cm": ar.f32(8),
         "cs": ar.f32(8), "cr": ar.f32(8), "wf": ar.f32(64),
         "negC": ar.f32(16)}

    @block.vector
    def _(v):
        E.inc(v.memset(R["negC"][:, 0:1], -GLD_C), "dve", 1)

    rms_done = _emit_rms_rows(P, C, var_acc, rstd_bc, rb_row)

    wgate_wq = _WQ(E, dram["wgate_t"], 16 * NCH, 16, wbuf)

    @block.gpsimd
    def _(g):
        wgate_wq.emit_loads(g)
        E.inc(g.dma_start(out=wr_sb.rearrange("p (k m) -> p k m", m=8),
                          in_=dram["wr_t"].rearrange("(k p) m -> p k m",
                                                     p=128)), "wq", 16)
        E.inc(g.dma_start(out=C["sel8_bf"][0:8, :], in_=dram["c_sel8"]),
              "wq", 16)
    wq_p4 = E.cnt["wq"]

    def gate_dst(mi, half):
        j = (2 * mi + half) % 4
        return gsl4[:, j * 512:(j + 1) * 512]

    def z_post(v, mi, half):
        oc = o_sb[:, mi * TOWN + half * 512: mi * TOWN + half * 512 + 512]
        v.scalar_tensor_tensor(ztmp, oc, C["nw"][:, mi:mi + 1],
                               gate_dst(mi, half), ALU.mult, ALU.mult)
        return v.tensor_tensor(oc, ztmp,
                               rstd_bc[:, half * 512:(half + 1) * 512],
                               ALU.mult)

    _emit_proj(P, C, wgate_wq, xn_own, TOWN, 16, gate_dst, AF.Sigmoid, 0,
               pre_waits=[("dve", xn_own_ready)],
               post=z_post, post_pre=[("dve", rms_done)])
    z_done = E.cnt["dve"]

    wout_wq = _WQ(E, dram["wout_t"], 16 * NCH, 16, wbuf)

    @block.gpsimd
    def _(g):
        wout_wq.emit_loads(g)

    def s_dst(mi, half):
        return s_sb[:, mi * TOWN + half * 512: mi * TOWN + half * 512 + 512]

    _emit_proj(P, C, wout_wq, o_sb, TOWN, 16, s_dst, AF.Copy, 0,
               pre_waits=[("dve", z_done)])
    s_act = E.cnt["act"]

    @block.sync
    def _(sy):
        E.wait(sy, "act", s_act)
        E.inc(sy.dma_start(out=s0_d, in_=s_sb), "oq", 16)
    s_spill_oq = E.cnt["oq"]

    rwE_bf = C["rwE_bf"]
    rwE_act = _emit_router(P, C, dram, xn_own, wr_sb, rwE_bf, R,
                           wq_ready=wq_p4)

    if debug == "p4":
        outs["s_d"] = nc.dram_tensor("s_d", [128, NCH * TOWN], BF16,
                                     kind="ExternalOutput").ap()
        outs["rw_d"] = nc.dram_tensor("rw_d", [8, TOWN], BF16,
                                      kind="ExternalOutput").ap()

        @block.sync
        def _(sy):
            E.wait(sy, "act", max(rwE_act, s_act))
            E.inc(sy.dma_start(out=outs["s_d"], in_=s_sb), "oq", 16)
            E.inc(sy.dma_start(out=outs["rw_d"], in_=rwE_bf[0:8, :]),
                  "oq", 16)
        s_spill_oq = E.cnt["oq"]

    ar.release()
    ar.release()

    ar.mark()
    accum = ar.f32(NCH * TOWN)
    ar.mark()
    h_sb = ar.bf(NCH * TOWN)
    xnst = ar.bf(NCH * 512)
    gsl = ar.bf(1024)
    usl = ar.bf(1024)
    wbc = ar.f32(1024)
    _emit_ffn0(P, C, dram, rwE_bf, rwE_act, accum, h_sb, xnst, wbuf,
               gsl, usl, wbc, xn0_d, guard_oq=s_spill_oq)
    ar.release()

    ar.mark()
    T5 = {"xs": ar.f32(2 * 1024), "ss": ar.bf(2 * 1024),
          "tb": ar.bf(2 * 1024), "upd": ar.f32(1024),
          "x1b": ar.bf(2 * 1024), "x1sq": ar.bf(2 * 1024),
          "xn1s": ar.bf(2 * 1024),
          "m1row": None, "v1row": None, "m1_bf": None, "v1_bf": None,
          "m1bc": ar.f32(1024), "r1bc": ar.f32(1024)}
    r8 = ar.f32(2048)
    T5["m1row"], T5["v1row"] = r8[0:1, 0:1024], r8[0:1, 1024:2048]
    rb8 = ar.bf(2048)
    T5["m1_bf"], T5["v1_bf"] = rb8[0:1, 0:1024], rb8[0:1, 1024:2048]

    ffn0_guard = 0
    if debug == "p4":
        outs["ffn0_d"] = nc.dram_tensor("ffn0_d", [128, NCH * TOWN], F32,
                                        kind="ExternalOutput").ap()
        dve_ff = E.cnt["dve"]

        @block.sync
        def _(sy):
            E.wait(sy, "dve", dve_ff)
            E.inc(sy.dma_start(out=outs["ffn0_d"], in_=accum), "oq", 16)
        ffn0_guard = E.cnt["oq"]

    @block.sync
    def _(sy):
        E.wait(sy, "oq", s_spill_oq)
        sy.nop()

    _emit_p5b(P, C, dram, accum, s0_d, x1_d, xn1_d, T5, guard_oq=ffn0_guard)
    ar.release()

    @block.sync
    def _(sy):
        E.wait(sy, "oq", E.cnt["oq"])
        sy.nop()
    ar.release()



def _build_B(debug=None):
    P = _Prog()
    _emit_partB(P, fused=False)
    return P.finish(), P.dram, P.outs


def _part_barrier(P):
    E, block = P.E, P.block
    waits = [(s, E.cnt[s]) for s in ("pe", "act", "dve", "oq", "xq", "wq",
                                     "pool", "sfree")]

    @block.gpsimd
    def _(g):
        for s, v in waits:
            E.wait(g, s, v)

    @block.sync
    def _(sy):
        for s, v in waits:
            E.wait(sy, s, v)


def _emit_partB(P, fused):
    nc, E, block, ar = P.nc, P.E, P.block, P.arena
    if fused:
        _part_barrier(P)
    ar.mark()
    dram = P.dram
    for n, shape, dt in [
        ("c_id", [128, 128], F32),
        ("c_bq", [128, 16], F32), ("c_bk", [128, 16], F32),
        ("c_bv", [128, 16], F32),
        ("c_cos", [128, TOWN], F32), ("c_sin", [128, TOWN], F32),
        ("wq_t", [16 * NCH * 128, 128], F32),
        ("wk_t", [16 * NCH * 128, 128], F32),
        ("wv_t", [16 * NCH * 128, 128], F32),
        ("xn1_d", [128, NCH * TOWN], BF16),
    ]:
        P.dcl(n, shape, dt)
    contrib_d = P.dcl("contrib_d", [TOWN, DM], BF16,
                      kind="Internal" if fused else "ExternalOutput")
    P.outs = {"contrib_d": contrib_d}
    oq_contrib = {}
    prev_tile = None

    wbuf = ar.bf(2 * 16 * 128)
    id_bf = ar.bf(128)
    bia = {k: ar.f32(16) for k in ("bq", "bk", "bv")}
    cosb = ar.f32(TOWN)
    sinb = ar.f32(TOWN)
    xn1 = ar.bf(NCH * TOWN)
    q1 = ar.bf(NCH * 512)
    k1 = ar.bf(NCH * 512)
    v1 = ar.bf(NCH * 512)
    rt = [ar.f32(512) for _ in range(4)]
    qt = ar.bf(NH * 128)
    kt = ar.bf(NH * 128)
    vt = ar.bf(NH * 128)
    prod = ar.bf(NH * 128)
    sc = ar.f32(256)
    rbuf = ar.f32(256)
    Ee = ar.f32(256)
    rmax = ar.f32(16)
    rs = ar.f32(16)
    acc = ar.f32(NH * 128)
    acc_bf = ar.bf(NH * 128)

    @block.gpsimd
    def _(g):
        E.inc(g.dma_start(out=id_bf, in_=dram["c_id"]), "wq", 16)

    @block.sync
    def _(sy):
        for k, src in (("bq", "c_bq"), ("bk", "c_bk"), ("bv", "c_bv")):
            E.inc(sy.dma_start(out=bia[k], in_=dram[src]), "xq", 16)
        E.inc(sy.dma_start(out=cosb, in_=dram["c_cos"]), "xq", 16)
        E.inc(sy.dma_start(out=sinb, in_=dram["c_sin"]), "xq", 16)
        E.inc(sy.dma_start(out=xn1, in_=dram["xn1_d"]), "xq", 16)
    xq_c = E.cnt["xq"]
    tp = P.ps["tp"].ap()

    for half in range(2):
        prev_half_done = E.cnt["dve"]
        xn_src = xn1[:, half * 512: half * 512 + 15 * TOWN + 512]
        for nm, buf, bk in (("wq_t", q1, "bq"), ("wk_t", k1, "bk"),
                            ("wv_t", v1, "bv")):
            wq = _WQ(E, dram[nm], 16 * NCH, 16, wbuf)

            @block.gpsimd
            def _(g, wq=wq):
                wq.emit_loads(g)

            def dstf(mi, h2, buf=buf):
                return buf[:, mi * 512:(mi + 1) * 512]
            _emit_proj(P, _BC, wq, xn_src, 512, 16, dstf,
                       AF.Identity, 0,
                       bias_of=lambda mi, bk=bk: bia[bk][:, mi:mi + 1],
                       pre_waits=[("xq", xq_c), ("dve", prev_half_done)],
                       xn_row=TOWN)
        qkv_act = E.cnt["act"]

        cs_a = cosb[0:64, half * 512: half * 512 + 512]
        sn_a = sinb[0:64, half * 512: half * 512 + 512]
        cs_b = cosb[64:128, half * 512: half * 512 + 512]
        sn_b = sinb[64:128, half * 512: half * 512 + 512]

        @block.vector
        def _(v, qkv_act=qkv_act):
            E.wait(v, "act", qkv_act)
            E.wait(v, "xq", xq_c)
            last = None
            for buf in (q1, k1):
                for h in range(NH):
                    A = buf[0:64, h * 512:(h + 1) * 512]
                    Br = buf[64:128, h * 512:(h + 1) * 512]
                    v.tensor_tensor(rt[0][0:64, :], A, cs_a, ALU.mult)
                    v.tensor_tensor(rt[1][0:64, :], Br, sn_b, ALU.mult)
                    v.tensor_tensor(rt[2][0:64, :], Br, cs_b, ALU.mult)
                    v.tensor_tensor(rt[3][0:64, :], A, sn_a, ALU.mult)
                    v.tensor_tensor(A, rt[0][0:64, :], rt[1][0:64, :],
                                    ALU.subtract)
                    last = v.tensor_tensor(Br, rt[2][0:64, :], rt[3][0:64, :],
                                           ALU.add)
            E.inc(last, "dve", 1)
        rope_dve = E.cnt["dve"]

        for tt in range(4):
            col = tt * 128
            prev_tile = (half, tt - 1) if tt else (half - 1, 3)
            dve_mix_prev = E.cnt["dve"]
            for rnd in range(6):
                items = []
                for i in range(8):
                    gidx = rnd * 8 + i
                    ten, h = divmod(gidx, 16)
                    src = (q1, k1, v1)[ten]
                    dstb = (qt, kt, vt)[ten]
                    items.append((src[:, h * 512 + col: h * 512 + col + 128],
                                  dstb[:, h * 128:(h + 1) * 128]))

                @block.tensor
                def _(t, items=items, rnd=rnd):
                    E.wait(t, "dve", rope_dve)
                    P.bw(t, "tp")
                    last = None
                    for i, (srcap, _) in enumerate(items):
                        last = t.transpose(tp[:, i * 128:(i + 1) * 128],
                                           srcap, id_bf)
                    E.inc(last, "pe", 1)
                pe_tr = E.cnt["pe"]

                @block.scalar
                def _(sc_, items=items, pe_tr=pe_tr):
                    E.wait(sc_, "pe", pe_tr)
                    E.wait(sc_, "dve", dve_mix_prev)
                    last = None
                    for i, (_, dstap) in enumerate(items):
                        last = sc_.copy(dstap, tp[:, i * 128:(i + 1) * 128])
                    E.inc(last, "act", 1)
                P.bs("tp", "act", E.cnt["act"])
            tm_act = E.cnt["act"]

            @block.vector
            def _(v, tm_act=tm_act):
                E.wait(v, "act", tm_act)
                for t_ in range(NH):
                    ktb = kt[:, t_ * 128:(t_ + 1) * 128].rearrange(
                        "p (o d) -> p o d", o=1)
                    ktb = bass.AP(tensor=ktb.tensor, offset=ktb.offset,
                                  ap=[[ktb.ap[0][0], 128], [0, 16], [1, 128]])
                    v.tensor_tensor(prod.rearrange("p (h d) -> p h d", h=16),
                                    qt.rearrange("p (h d) -> p h d", h=16),
                                    ktb, ALU.mult)
                    last = v.reduce_sum(
                        rbuf[:, t_ * 16:(t_ + 1) * 16],
                        prod.rearrange("p (h d) -> p h d", h=16),
                        axis=AX.X)
                E.ss(v, last, "dve")
                E.ss(v, v.tensor_copy(
                    sc, rbuf.rearrange("p (t h) -> p h t", h=16)), "dve")
                E.ss(v, v.reduce_max(rmax, sc.rearrange("p (h t) -> p h t",
                                                        t=16), axis=AX.X),
                     "dve")
                last = None
                for h in range(NH):
                    last = v.tensor_scalar(
                        Ee[:, h * 16:(h + 1) * 16], sc[:, h * 16:(h + 1) * 16],
                        rmax[:, h:h + 1], 1.0 / math.sqrt(HD),
                        ALU.subtract, ALU.mult)
                E.inc(last, "dve", 1)
            dve_sc = E.cnt["dve"]

            @block.scalar
            def _(sc_, dve_sc=dve_sc):
                E.wait(sc_, "dve", dve_sc)
                E.inc(sc_.activation(Ee, Ee, AF.Exp), "act", 1)
            act_e = E.cnt["act"]

            @block.vector
            def _(v, act_e=act_e, tt=tt, col=col):
                E.wait(v, "act", act_e)
                E.ss(v, v.reduce_sum(rs, Ee.rearrange("p (h t) -> p h t",
                                                      t=16), axis=AX.X),
                     "dve")
                E.ss(v, v.reciprocal(rs, rs), "dve")
                for t_ in range(NH):
                    for h in range(NH):
                        a = acc[:, h * 128:(h + 1) * 128]
                        if t_ == 0:
                            v.tensor_scalar_mul(a, vt[:, 0:128],
                                                Ee[:, h * 16:h * 16 + 1])
                        else:
                            v.scalar_tensor_tensor(
                                a, vt[:, t_ * 128:(t_ + 1) * 128],
                                Ee[:, h * 16 + t_:h * 16 + t_ + 1], a,
                                ALU.mult, ALU.add)
                last = None
                E.wait(v, "oq", oq_contrib.get(prev_tile, 0))
                for h in range(NH):
                    last = v.tensor_scalar_mul(
                        acc_bf[:, h * 128:(h + 1) * 128],
                        acc[:, h * 128:(h + 1) * 128], rs[:, h:h + 1])
                E.inc(last, "dve", 1)
            dve_mix = E.cnt["dve"]

            @block.sync
            def _(sy, tt=tt, half=half, dve_mix=dve_mix):
                E.wait(sy, "dve", dve_mix)
                u0 = 32 * half + 8 * tt
                dst = contrib_d.rearrange(
                    "(h u) (j d) -> h u j d", u=64, d=128)[:, u0:u0 + 8]
                dst = dst.rearrange("h u j d -> (u j) h d")
                srcp = acc_bf.rearrange("p (h d) -> p h d", d=128)
                E.inc(sy.dma_start(out=dst, in_=srcp), "oq", 16)
            oq_contrib[(half, tt)] = E.cnt["oq"]
    @block.sync
    def _(sy):
        E.wait(sy, "oq", E.cnt["oq"])
        sy.nop()
    ar.release()


_BC = None



def _emit_bfly(P, block, buf, n_chunks, cw, tmp2, pre=()):
    E = P.E
    stages = n_chunks.bit_length() - 1

    @block.vector
    def _(v):
        for sem, val in pre:
            E.wait(v, sem, val)
        ti = 0
        last = None
        for s in range(stages):
            step = 1 << s
            for base in range(0, n_chunks, step * 2):
                for off in range(step):
                    a = buf[:, (base + off) * cw:(base + off + 1) * cw]
                    b = buf[:, (base + off + step) * cw:
                            (base + off + step + 1) * cw]
                    t = tmp2[:, (ti % 2) * cw:(ti % 2 + 1) * cw]
                    ti += 1
                    v.tensor_tensor(t, a, b, ALU.subtract)
                    v.tensor_tensor(a, a, b, ALU.add)
                    last = v.tensor_copy(b, t)
        E.inc(last, "dve", 1)
    return E.cnt["dve"]


def _emit_h128(P, block, h128_bf, buf, n_chunks, cw, src_ready):
    E = P.E
    pe0 = E.cnt["pe"]
    act0 = E.cnt["act"]

    @block.tensor
    def _(t):
        for sem, val in src_ready:
            E.wait(t, sem, val)
        for c in range(n_chunks):
            bi = c % 4
            if c >= 4:
                E.wait(t, "act", act0 + c - 3)
            else:
                P.bw(t, ("g", bi))
            E.inc(t.matmul(P.gbanks[bi][:, 0:cw], h128_bf,
                           buf[:, c * cw:(c + 1) * cw],
                           start=True, stop=True), "pe", 1)

    @block.scalar
    def _(sc):
        for c in range(n_chunks):
            E.wait(sc, "pe", pe0 + c + 1)
            E.inc(sc.copy(buf[:, c * cw:(c + 1) * cw],
                          P.gbanks[c % 4][:, 0:cw]), "act", 1)
    for c in range(max(0, n_chunks - 4), n_chunks):
        P.bs(("g", c % 4), "act", act0 + c + 1)
    return E.cnt["act"]


def _build_C(debug=None, stage=None):
    P = _Prog()
    _emit_partC(P, fused=False, stage=stage)
    return P.finish(), P.dram, P.outs


def _emit_partC(P, fused, stage=None):
    nc, E, block, ar = P.nc, P.E, P.block, P.arena
    if fused:
        _part_barrier(P)
    ar.mark()
    dram = P.dram
    for n, shape, dt in [
        ("c_h128", [128, 128], F32), ("c_id", [128, 128], F32),
        ("c_suug", [128, 16], F32), ("c_sudp", [128, 64], F32),
        ("c_wsug", [128, 128], F32), ("c_wsdp", [128, 16], F32),
        ("c_bo", [128, 16], F32),
        ("wo_t", [16 * NCH * 128, 128], F32),
        ("wupg_t", [64 * 32 * 128, 128], F32),
        ("wdown_t", [16 * 64 * 128, 128], F32),
    ]:
        P.dcl(n, shape, dt)
    if fused:
        ag_d = P.dram["ag_d"]
        P.dcl("c_qh", [1, 4], mybir.dt.uint32)
    else:
        P.dcl("xn1_d", [128, NCH * TOWN], BF16)
        P.dcl("x1_d", [128, NCH * TOWN], F32)
        attn_d = P.dcl("attn_d", [TOWN, DM], BF16)
    outT_d = P.dcl("outT_d", [128, NCH * TOWN], F32, kind="ExternalOutput")
    P.outs = {"outT_d": outT_d}

    wbuf = ar.bf(2 * 16 * 128)
    wbuf_dn = ar.bf(2 * 32 * 128)
    h128_bf = ar.bf(128)
    id_bf = ar.bf(128)
    bo = ar.f32(16)
    suug = ar.f32(16)
    sudp = ar.f32(64)
    wsug = ar.f32(128)
    wsdp = ar.f32(16)
    wsu_eff = ar.f32(64)
    wsg_eff = ar.f32(64)
    wsd_eff = ar.f32(16)
    th = ar.bf(NCH * 512)
    h2 = ar.bf(64 * 512)
    tmp2 = ar.bf(2 * 512)
    xnsl = ar.bf(2 * 512)
    atile = ar.bf(2 * 2048)
    attn_fm = ar.bf(NCH * 512)
    s1h = ar.bf(NCH * 512)
    x1sl = ar.f32(2 * 512)
    outsl = ar.f32(2 * 512)
    g1sl = ar.bf(4 * 512)
    updf = ar.f32(512)
    qh_sb = ar.bf(16).bitcast(mybir.dt.uint32)

    @block.gpsimd
    def _(g):
        E.inc(g.dma_start(out=h128_bf, in_=dram["c_h128"]), "wq", 16)
        E.inc(g.dma_start(out=id_bf, in_=dram["c_id"]), "wq", 16)
    wq_c0 = E.cnt["wq"]
    if fused:
        @block.sync
        def _(sy):
            E.inc(sy.dma_start(out=qh_sb[0:1, 0:4], in_=dram["c_qh"]),
                  "xq", 16)

    @block.sync
    def _(sy):
        for dst, srcn in ((suug, "c_suug"), (sudp, "c_sudp"),
                          (wsug, "c_wsug"), (wsdp, "c_wsdp"),
                          (bo, "c_bo")):
            E.inc(sy.dma_start(out=dst, in_=dram[srcn]), "xq", 16)
    xq_c = E.cnt["xq"]

    @block.vector
    def _(v):
        E.wait(v, "xq", xq_c)
        v.scalar_tensor_tensor(wsu_eff, wsug[:, 0:64],
                               1.0 / math.sqrt(DM), sudp, ALU.mult, ALU.mult)
        v.tensor_scalar_mul(wsg_eff, wsug[:, 64:128], 1.0 / math.sqrt(DM))
        E.inc(v.tensor_scalar_mul(wsd_eff, wsdp, 1.0 / math.sqrt(INTER)),
              "dve", 1)
    scales_dve = E.cnt["dve"]
    tp = P.ps["tp"].ap()

    for half in range(2):
        dve_prev_half = E.cnt["dve"]
        xq_at = {}
        for tt in range(4):
            @block.sync
            def _(sy, tt=tt, half=half):
                E.wait(sy, "pe", E.cnt["pe"])
                t0 = half * 512 + tt * 128
                dst = atile[:, (tt % 2) * 2048:(tt % 2) * 2048 + 2048]
                if fused:
                    E.wait(sy, "pool", P.cc_mark)
                    hh = t0 // 256
                    r0 = (t0 % 256) // 64
                    idx = sy.value_load(qh_sb[0:1, hh:hh + 1])
                    ag4 = ag_d.rearrange("(r i z) e -> r i z e", i=16, z=64)
                    for k in range(2):
                        srcp = ag4[r0 + k:r0 + k + 1,
                                   bass.ts(idx, 1)].rearrange(
                            "r i z e -> (r i z) e")
                        E.inc(sy.dma_start(out=dst[k * 64:(k + 1) * 64, :],
                                           in_=srcp), "xq", 16)
                else:
                    E.inc(sy.dma_start(out=dst,
                                       in_=dram["attn_d"][t0:t0 + 128, :]),
                          "xq", 16)
            xq_at[tt] = E.cnt["xq"]

            for rnd in range(2):
                @block.tensor
                def _(t, tt=tt, rnd=rnd):
                    E.wait(t, "xq", xq_at[tt])
                    P.bw(t, "tp")
                    last = None
                    for i in range(8):
                        j = rnd * 8 + i
                        last = t.transpose(
                            tp[:, i * 128:(i + 1) * 128],
                            atile[:, (tt % 2) * 2048 + j * 128:
                                  (tt % 2) * 2048 + (j + 1) * 128], id_bf)
                    E.inc(last, "pe", 1)
                pe_tr = E.cnt["pe"]

                @block.scalar
                def _(sc_, tt=tt, rnd=rnd, pe_tr=pe_tr):
                    E.wait(sc_, "pe", pe_tr)
                    last = None
                    for i in range(8):
                        j = rnd * 8 + i
                        last = sc_.copy(
                            attn_fm[:, j * 512 + tt * 128:
                                    j * 512 + tt * 128 + 128],
                            tp[:, i * 128:(i + 1) * 128])
                    E.inc(last, "act", 1)
                P.bs("tp", "act", E.cnt["act"])
        attn_act = E.cnt["act"]

        wo_wq = _WQ(E, dram["wo_t"], 16 * NCH, 16, wbuf)

        @block.gpsimd
        def _(g, wo_wq=wo_wq):
            wo_wq.emit_loads(g)

        _emit_proj(P, _BC, wo_wq, attn_fm, 512, 16,
                   lambda mi, h_: s1h[:, mi * 512:(mi + 1) * 512],
                   AF.Identity, 0, bias_of=lambda mi: bo[:, mi:mi + 1],
                   pre_waits=[("act", attn_act), ("dve", dve_prev_half)],
                   xn_row=512)
        s1_act = E.cnt["act"]

        dve_base = E.cnt["dve"]

        @block.gpsimd
        def _(g, half=half, dve_base=dve_base):
            E.wait(g, "dve", dve_base)
            for m in range(NCH):
                if m >= 2:
                    E.wait(g, "dve", dve_base + m - 1)
                E.inc(g.dma_start(
                    out=xnsl[:, (m % 2) * 512:(m % 2) * 512 + 512],
                    in_=dram["xn1_d"][:, m * TOWN + half * 512:
                                      m * TOWN + half * 512 + 512]),
                    "pool", 16)
        pool0 = E.cnt["pool"] - 16 * NCH

        @block.vector
        def _(v, pool0=pool0):
            E.wait(v, "xq", xq_c)
            E.wait(v, "pe", E.cnt["pe"])
            for m in range(NCH):
                E.wait(v, "pool", pool0 + 16 * (m + 1))
                E.inc(v.tensor_scalar_mul(
                    th[:, m * 512:(m + 1) * 512],
                    xnsl[:, (m % 2) * 512:(m % 2) * 512 + 512],
                    suug[:, m:m + 1]), "dve", 1)
        th_dve = E.cnt["dve"]

        h128_act = _emit_h128(P, block, h128_bf, th, NCH, 512,
                              [("dve", th_dve), ("wq", wq_c0)])
        xh_dve = _emit_bfly(P, block, th, NCH, 512, tmp2,
                            pre=[("act", h128_act)])

        upg_wq = _WQ(E, dram["wupg_t"], 64 * 32, 16, wbuf)

        @block.gpsimd
        def _(g, upg_wq=upg_wq):
            upg_wq.emit_loads(g)

        pe0 = E.cnt["pe"]
        act0 = E.cnt["act"]
        dve0 = E.cnt["dve"]
        usl = g1sl

        @block.tensor
        def _(t, upg_wq=upg_wq, pe0=pe0, act0=act0):
            E.wait(t, "dve", xh_dve)
            for M in range(64):
                pair = M % 2
                if M >= 2:
                    E.wait(t, "act", act0 + M - 1)
                else:
                    P.bw(t, ("g", 2 * pair))
                    P.bw(t, ("g", 2 * pair + 1))
                for k in range(NCH):
                    t.matmul(P.gbanks[2 * pair], upg_wq.tile(t, M * 32 + k),
                             th[:, k * 512:(k + 1) * 512],
                             start=(k == 0), stop=(k == NCH - 1))
                last = None
                for k in range(NCH):
                    last = t.matmul(P.gbanks[2 * pair + 1],
                                    upg_wq.tile(t, M * 32 + 16 + k),
                                    th[:, k * 512:(k + 1) * 512],
                                    start=(k == 0), stop=(k == NCH - 1))
                E.inc(last, "pe", 1)

        @block.scalar
        def _(sc, pe0=pe0, dve0=dve0):
            E.wait(sc, "dve", scales_dve)
            for M in range(64):
                pair = M % 2
                E.wait(sc, "pe", pe0 + M + 1)
                if M >= 2:
                    E.wait(sc, "dve", dve0 + M - 1)
                sc.activation(usl[:, pair * 512:(pair + 1) * 512],
                              P.gbanks[2 * pair], AF.Identity,
                              scale=wsu_eff[:, M:M + 1])
                last = sc.activation(usl[:, (2 + pair) * 512:(3 + pair) * 512],
                                     P.gbanks[2 * pair + 1], AF.Silu,
                                     scale=wsg_eff[:, M:M + 1])
                E.inc(last, "act", 1)
                E.inc(sc.copy(P.dummy, P.dummy), "sfree", 2)

        @block.vector
        def _(v, act0=act0):
            for M in range(64):
                pair = M % 2
                E.wait(v, "act", act0 + M + 1)
                E.inc(v.tensor_tensor(
                    h2[:, M * 512:(M + 1) * 512],
                    usl[:, (2 + pair) * 512:(3 + pair) * 512],
                    usl[:, pair * 512:(pair + 1) * 512], ALU.mult), "dve", 1)
        for bi in range(4):
            P.bs(("g", bi), "act", E.cnt["act"])

        h128b_act = _emit_h128(P, block, h128_bf, h2, 64, 512,
                               [("dve", E.cnt["dve"])])
        h2_dve = _emit_bfly(P, block, h2, 64, 512, tmp2,
                            pre=[("act", h128b_act)])

        down_wq = _WQ(E, dram["wdown_t"], 16 * 64, 32, wbuf_dn)

        @block.gpsimd
        def _(g, down_wq=down_wq):
            down_wq.emit_loads(g)

        xq_x1b = E.cnt["xq"]
        oq_b = E.cnt["oq"]
        dve_dn = E.cnt["dve"]

        @block.sync
        def _(sy, half=half, xq_x1b=xq_x1b, dve_dn=dve_dn):
            for mi in range(NCH):
                if mi >= 2:
                    E.wait(sy, "dve", dve_dn + mi - 1)
                E.inc(sy.dma_start(
                    out=x1sl[:, (mi % 2) * 512:(mi % 2) * 512 + 512],
                    in_=dram["x1_d"][:, mi * TOWN + half * 512:
                                     mi * TOWN + half * 512 + 512]),
                    "xq", 16)
                if mi >= 1:
                    E.wait(sy, "dve", dve_dn + mi)
                    E.inc(sy.dma_start(
                        out=outT_d[:, (mi - 1) * TOWN + half * 512:
                                   (mi - 1) * TOWN + half * 512 + 512],
                        in_=outsl[:, ((mi - 1) % 2) * 512:
                                  ((mi - 1) % 2) * 512 + 512]), "oq", 16)
            E.wait(sy, "dve", dve_dn + NCH)
            E.inc(sy.dma_start(
                out=outT_d[:, 15 * TOWN + half * 512:
                           15 * TOWN + half * 512 + 512],
                in_=outsl[:, (15 % 2) * 512:(15 % 2) * 512 + 512]), "oq", 16)

        def d_post(v, mi, h_, half=half, xq_x1b=xq_x1b, oq_b=oq_b):
            E.wait(v, "xq", xq_x1b + 16 * (mi + 1))
            if mi >= 2:
                E.wait(v, "oq", oq_b + 16 * (mi - 1))
            v.tensor_tensor(updf, g1sl[:, (mi % 4) * 512:(mi % 4 + 1) * 512],
                            s1h[:, mi * 512:(mi + 1) * 512], ALU.mult)
            return v.tensor_tensor(
                outsl[:, (mi % 2) * 512:(mi % 2) * 512 + 512],
                x1sl[:, (mi % 2) * 512:(mi % 2) * 512 + 512], updf, ALU.add)

        _emit_proj(P, _BC, down_wq, h2, 512, 16,
                   lambda mi, h_: g1sl[:, (mi % 4) * 512:(mi % 4 + 1) * 512],
                   AF.Silu, 0, scale_of=lambda mi: wsd_eff[:, mi:mi + 1],
                   pre_waits=[("dve", h2_dve), ("act", s1_act)],
                   post=d_post, n_k=64, xn_row=512)

    @block.sync
    def _(sy):
        E.wait(sy, "oq", E.cnt["oq"])
        sy.nop()
    ar.release()


def _build_full():
    P = _Prog()
    _emit_partA(P, fused=True)
    _emit_partB(P, fused=True)
    E, block = P.E, P.block
    contrib_d = P.dram["contrib_d"]
    ag_d = P.dcl("ag_d", [4 * TOWN, DM], BF16, kind="Internal")

    @block.gpsimd
    def _(g):
        E.wait(g, "oq", E.cnt["oq"])
        E.inc(g.collective_compute(
            "AllGather", ALU.bypass,
            replica_groups=[[0, 1, 2, 3], [4, 5, 6, 7]],
            ins=[contrib_d[:, :]], outs=[ag_d[:, :]]), "pool", 1)
    P.cc_mark = E.cnt["pool"]
    _emit_partC(P, fused=True)
    return P.finish(), P.dram, P.outs


def _prep_C_inputs(inputs):
    d = {"c_id": np.eye(128, dtype=np.float32)}
    Wo = np.asarray(inputs["L1_attn_Wo"], np.float32)
    d["wo_t"] = _tiles_km(np.ascontiguousarray(Wo.T))
    d["c_bo"] = np.ascontiguousarray(
        np.asarray(inputs["L1_attn_bo"], np.float32).reshape(NCH, 128).T)
    H = np.array([[1.0]], dtype=np.float32)
    while H.shape[0] < 128:
        H = np.block([[H, H], [H, -H]])
    d["c_h128"] = H
    d["c_suug"] = np.ascontiguousarray(
        np.asarray(inputs["L1_ffn_SU_ug"], np.float32).reshape(NCH, 128).T)
    d["c_sudp"] = np.ascontiguousarray(
        np.asarray(inputs["L1_ffn_SU_dp"], np.float32).reshape(64, 128).T)
    ws = np.asarray(inputs["L1_ffn_Wscale_ug"], np.float32)
    d["c_wsug"] = np.ascontiguousarray(np.concatenate(
        [ws[:INTER].reshape(64, 128).T, ws[INTER:].reshape(64, 128).T],
        axis=1))
    d["c_wsdp"] = np.ascontiguousarray(
        np.asarray(inputs["L1_ffn_Wscale_dp"], np.float32).reshape(NCH, 128).T)
    W_up = np.asarray(inputs["L1_ffn_W_up"], np.float32)
    W_gate = np.asarray(inputs["L1_ffn_W_gate"], np.float32)
    up_t = _tiles_km(np.ascontiguousarray(W_up.T)).reshape(64, 16 * 128, 128)
    g_t = _tiles_km(np.ascontiguousarray(W_gate.T)).reshape(64, 16 * 128, 128)
    d["wupg_t"] = np.ascontiguousarray(
        np.concatenate([up_t, g_t], axis=1).reshape(-1, 128))
    d["wdown_t"] = _tiles_km(np.ascontiguousarray(
        np.asarray(inputs["L1_ffn_W_down"], np.float32).T))
    return d



def _core_slices(x):
    res = []
    for c in range(8):
        b, q = divmod(c, 4)
        own = np.ascontiguousarray(x[b, 1024 * q:1024 * (q + 1), :].T)
        pre = np.zeros((DM, TPRE), dtype=np.float32)
        lo = 1024 * q - TPRE
        seg = x[b, max(0, lo):1024 * q, :]
        if seg.shape[0]:
            pre[:, TPRE - seg.shape[0]:] = seg.T
        res.append((own, pre))
    return res


def _prep_A_inputs(inputs):
    d = {}
    d.update(_const_tiles())
    slope = np.asarray(inputs["L0_attn_slope_rate"], np.float32).reshape(1, NH)
    d["c_slope"] = np.broadcast_to(slope, (128, NH)).copy()
    d["c_temp"] = np.broadcast_to(
        np.asarray(inputs["L0_ffn_temperature"], np.float32).reshape(1, 1),
        (128, 1)).copy()
    d["c_nw"] = np.ascontiguousarray(
        np.asarray(inputs["L0_attn_norm_w"], np.float32).reshape(NCH, 128).T)
    wqkv = np.asarray(inputs["L0_attn_w_qkv"], np.float32)
    d["wqkv_t"] = _tiles_km(np.ascontiguousarray(wqkv.T))
    kv_rows = np.concatenate([np.arange(h * 384 + 128, h * 384 + 384)
                              for h in range(NH)])
    d["wkv_t"] = _tiles_km(np.ascontiguousarray(wqkv[kv_rows].T))
    d["wgate_t"] = _tiles_km(np.ascontiguousarray(
        np.asarray(inputs["L0_attn_w_gate"], np.float32).T))
    d["wout_t"] = _tiles_km(np.ascontiguousarray(
        np.asarray(inputs["L0_attn_w_out"], np.float32).T))
    d["wr_t"] = np.ascontiguousarray(np.asarray(inputs["L0_ffn_Wr"],
                                                np.float32))
    sel = np.zeros((8, 8 * 128), dtype=np.float32)
    for e in range(8):
        sel[e, e * 128:(e + 1) * 128] = 1.0
    d["c_sel8"] = sel
    Wg = np.asarray(inputs["L0_ffn_Wg"], np.float32)
    Wu = np.asarray(inputs["L0_ffn_Wu"], np.float32)
    Wd = np.asarray(inputs["L0_ffn_Wd"], np.float32)
    blocks = []
    for e in range(NEXP):
        wg_t = _tiles_km(Wg[e]).reshape(16, 16 * 128, 128)
        wu_t = _tiles_km(Wu[e]).reshape(16, 16 * 128, 128)
        inter = np.concatenate([wg_t, wu_t], axis=1).reshape(-1, 128)
        blocks.append(inter)
        blocks.append(_tiles_km(Wd[e]))
    d["ffn0_t"] = np.ascontiguousarray(np.concatenate(blocks, axis=0))
    return d


def _rope_rows():
    p = np.zeros(128, np.int64)
    p[0:64] = np.arange(64) * 2
    p[64:128] = np.arange(64) * 2 + 1
    return np.concatenate([h * 128 + p for h in range(NH)])


def _prep_B_inputs(inputs):
    d = {"c_id": np.eye(128, dtype=np.float32)}
    rows = _rope_rows()

    def chunked(v):
        return np.ascontiguousarray(
            np.asarray(v, np.float32).reshape(NCH, 128).T)

    for nm, wkey, bkey, permute in (
            ("q", "L1_attn_Wq", "L1_attn_bq", True),
            ("k", "L1_attn_Wk", "L1_attn_bk", True),
            ("v", "L1_attn_Wv", "L1_attn_bv", False),
            ("o", "L1_attn_Wo", "L1_attn_bo", False)):
        W = np.asarray(inputs[wkey], np.float32)
        b = np.asarray(inputs[bkey], np.float32)
        if permute:
            W = W[rows]
            b = b[rows]
        d[f"w{nm}_t"] = _tiles_km(np.ascontiguousarray(W.T))
        d[f"c_b{nm}"] = chunked(b)
    return d


def _rope_tables(core):
    b, q = divmod(core, 4)
    pos = (1024 * q + np.arange(TOWN, dtype=np.float64))
    inv = 1.0 / (10000.0 ** (np.arange(0, 128, 2, dtype=np.float64) / 128.0))
    fr = np.outer(inv, pos)
    c = np.cos(fr).astype(np.float32)
    s = np.sin(fr).astype(np.float32)
    return (np.concatenate([c, c], 0), np.concatenate([s, s], 0))


def _run_A_debug(inputs, debug="p2", cores=(0,)):
    nc, dram, outs = _build_A(debug=debug)
    shared = _prep_A_inputs(inputs)
    slices = _core_slices(np.asarray(inputs["L0_attn_hidden_states"],
                                     np.float32))
    in_maps = []
    for c in cores:
        m = dict(shared)
        m["xT"], m["xTpre"] = slices[c]
        in_maps.append(m)
    res = run_bass_kernel_spmd(nc, in_maps, core_ids=list(range(len(cores))))
    return res.results


def _filter(m, dram):
    return {k: m[k] for k in dram if k in m}


FUSED = True


def kernel(**inputs):
    if FUSED:
        return _kernel_fused(**inputs)
    return _kernel_unfused(**inputs)


def _kernel_fused(**inputs):
    inputs = {k: np.asarray(v) for k, v in inputs.items()}
    x = np.asarray(inputs["L0_attn_hidden_states"], np.float32)
    nc, dram, outs = _build_full()
    shared = {}
    shared.update(_prep_A_inputs(inputs))
    shared.update(_prep_B_inputs(inputs))
    shared.update(_prep_C_inputs(inputs))
    slices = _core_slices(x)
    in_maps = []
    for c in range(8):
        b, q = divmod(c, 4)
        m = dict(shared)
        m["xT"], m["xTpre"] = slices[c]
        m["c_cos"], m["c_sin"] = _rope_tables(c)
        m["c_qh"] = np.array([[q * 4 + hh for hh in range(4)]], np.uint32)
        in_maps.append(_filter(m, dram))
    res = run_bass_kernel_spmd(nc, in_maps, core_ids=list(range(8)))
    out = np.empty((2, 4096, 2048), np.float32)
    for c in range(8):
        b, q = divmod(c, 4)
        ot = np.asarray(res.results[c]["outT_d"], np.float32)
        out[b, 1024 * q:1024 * (q + 1), :] = ot.reshape(
            128, 16, 1024).transpose(1, 0, 2).reshape(2048, 1024).T
    return out


def _kernel_unfused(**inputs):
    inputs = {k: np.asarray(v) for k, v in inputs.items()}
    x = np.asarray(inputs["L0_attn_hidden_states"], np.float32)

    ncA, dramA, _ = _build_A()
    sharedA = _prep_A_inputs(inputs)
    slices = _core_slices(x)
    in_maps = []
    for c in range(8):
        m = dict(sharedA)
        m["xT"], m["xTpre"] = slices[c]
        in_maps.append(_filter(m, dramA))
    resA = run_bass_kernel_spmd(ncA, in_maps, core_ids=list(range(8)))

    ncB, dramB, _ = _build_B()
    sharedB = _prep_B_inputs(inputs)
    in_maps = []
    for c in range(8):
        m = dict(sharedB)
        m["c_cos"], m["c_sin"] = _rope_tables(c)
        m["xn1_d"] = np.asarray(resA.results[c]["xn1_d"])
        in_maps.append(_filter(m, dramB))
    resB = run_bass_kernel_spmd(ncB, in_maps, core_ids=list(range(8)))

    contribs = [np.asarray(resB.results[c]["contrib_d"]) for c in range(8)]
    t_idx = np.arange(1024)
    attn_new = []
    for c in range(8):
        b, q = divmod(c, 4)
        h = 4 * q + t_idx // 256
        r = (t_idx % 256) // 64
        ul = t_idx % 64
        arr = np.stack([contribs[4 * b + j] for j in range(4)])
        attn_new.append(np.ascontiguousarray(arr[r, h * 64 + ul, :]))

    ncC, dramC, _ = _build_C()
    sharedC = _prep_C_inputs(inputs)
    in_maps = []
    for c in range(8):
        m = dict(sharedC)
        m["xn1_d"] = np.asarray(resA.results[c]["xn1_d"])
        m["x1_d"] = np.asarray(resA.results[c]["x1_d"])
        m["attn_d"] = attn_new[c]
        in_maps.append(_filter(m, dramC))
    resC = run_bass_kernel_spmd(ncC, in_maps, core_ids=list(range(8)))

    out = np.empty((2, 4096, 2048), np.float32)
    for c in range(8):
        b, q = divmod(c, 4)
        ot = np.asarray(resC.results[c]["outT_d"], np.float32)
        out[b, 1024 * q:1024 * (q + 1), :] = ot.reshape(
            128, 16, 1024).transpose(1, 0, 2).reshape(2048, 1024).T
    return out


# version 04 (final)
# speedup vs baseline: 1.1378x; 1.1378x over previous
import math
import numpy as np
import ml_dtypes
import concourse.bass as bass
import concourse.mybir as mybir
from concourse.bass_utils import run_bass_kernel_spmd

BFNP = ml_dtypes.bfloat16
F32 = mybir.dt.float32
BF16 = mybir.dt.bfloat16
AF = mybir.ActivationFunctionType
ALU = mybir.AluOpType
AX = mybir.AxisListType

DM = 2048
NCH = DM // 128
NH, HD = 16, 128
TOWN = 1024
TPRE = 3072
SGTOK = 1024
B, SEQ = 2, 4096
EPS_LN = 1e-5
EPS_RMS = 1e-6
GLD_C = 1.0 / np.e
GLD_L = 0.5 - math.log(4.0 / 3.0)
GLD_U = 0.5
NEXP = 8
INTER = 8192
SCALE_H = 64.0


def _tiles_km(w_km, m_order=None):
    K, M = w_km.shape
    nK, nM = K // 128, M // 128
    t = w_km.reshape(nK, 128, nM, 128).transpose(2, 0, 1, 3)
    if m_order is not None:
        t = t[np.asarray(m_order)]
    return np.ascontiguousarray(t).reshape(-1, 128)


def _const_tiles():
    c = {}
    c["c_id"] = np.eye(128, dtype=np.float32)
    m = np.arange(128, dtype=np.float32)
    c["c_arrq"] = np.broadcast_to(m[None, :] + 1.0, (128, 128)).copy()
    c["c_arrk"] = np.broadcast_to(127.0 - m[None, :], (128, 128)).copy()
    d = m[None, :] - m[:, None]
    c["c_delta"] = np.where(d >= 0, d, 1e9).astype(np.float32)
    c["c_ones_col"] = np.ones((128, 1), dtype=np.float32)
    c["c_ones_row"] = np.ones((1, 128), dtype=np.float32)
    return c


class _E:

    def __init__(self, nc, sem_names):
        self.nc = nc
        self._cm = []
        self.sems = {}
        for n in sem_names:
            cm = nc.semaphore(n)
            self.sems[n] = cm.__enter__()
            self._cm.append(cm)
        self.cnt = {n: 0 for n in sem_names}
        self.hw = {}

    def close(self):
        for cm in reversed(self._cm):
            cm.__exit__(None, None, None)

    def inc(self, instr, sem, k=1):
        instr.then_inc(self.sems[sem], k)
        self.cnt[sem] += k
        return self.cnt[sem]

    def wait(self, eng, sem, val):
        if val <= 0:
            return
        key = (eng.engine, sem)
        if self.hw.get(key, 0) >= val:
            return
        eng.wait_ge(self.sems[sem], val)
        self.hw[key] = val

    def ss(self, eng, instr, sem):
        self.inc(instr, sem, 1)
        self.wait(eng, sem, self.cnt[sem])


class _Arena:

    ALIGN = 16

    def __init__(self, nc, cols):
        self._cm = nc.sbuf_tensor("arena", [128, cols], BF16)
        self.t = self._cm.__enter__()
        self.cols = cols
        self.off = 0
        self.marks = []
        self.peak = 0

    def close(self):
        self._cm.__exit__(None, None, None)

    def mark(self):
        self.marks.append(self.off)

    def release(self):
        self.off = self.marks.pop()

    def bf(self, cols):
        cols = -(-cols // self.ALIGN) * self.ALIGN
        assert self.off + cols <= self.cols, (self.off, cols, self.cols)
        ap = self.t.ap()[:, self.off:self.off + cols]
        self.off += cols
        self.peak = max(self.peak, self.off)
        return ap

    def f32(self, cols):
        return self.bf(2 * cols).bitcast(F32)


class _WQ:

    def __init__(self, E, src_ap, n_tiles, slab_tiles, buf_ap):
        self.E = E
        self.src = src_ap
        self.n = n_tiles
        self.st = slab_tiles
        self.buf = buf_ap
        self.nslab = -(-n_tiles // slab_tiles)
        self.wq_mark = []
        self.sfree0 = E.cnt["sfree"]
        self.cur = -1

    def emit_loads(self, g, nslab=None):
        E = self.E
        for s in range(len(self.wq_mark), min(x for x in [self.nslab, nslab]
                                              if x is not None)):
            E.wait(g, "sfree", self.sfree0 + max(0, s - 1))
            t0 = s * self.st
            nt = min(self.st, self.n - t0)
            src = self.src[t0 * 128:(t0 + nt) * 128, :].rearrange(
                "(t p) m -> p t m", p=128)
            dst = self.buf[:, (s % 2) * self.st * 128:
                           (s % 2) * self.st * 128 + nt * 128].rearrange(
                "p (t m) -> p t m", m=128)
            sem = "wq" if s % 2 == 0 else "wq2"
            E.inc(g.dma_start(out=dst, in_=src), sem, 16)
            self.wq_mark.append((sem, E.cnt[sem]))

    def tile(self, t_eng, i):
        s = i // self.st
        if s != self.cur:
            sem, val = self.wq_mark[s]
            self.E.wait(t_eng, sem, val)
            self.cur = s
        off = (s % 2) * self.st * 128 + (i - s * self.st) * 128
        return self.buf[:, off:off + 128]

    def ends_slab(self, i):
        s = i // self.st
        return i == min(self.n, (s + 1) * self.st) - 1



SEMS = ["wq", "wq2", "xq", "xq2", "oq", "oq2", "sfree", "pe", "act",
        "dve", "pool", "pool2", "cc"]


class _Prog:

    def __init__(self, arena_cols=100352):
        self.nc = bass.Bass("TRN2", target_bir_lowering=False, debug=False,
                            num_devices=8)
        self.dram = {}
        self.E = _E(self.nc, SEMS)
        self.arena = _Arena(self.nc, arena_cols)
        self.psum_cm = []
        self.ps = {}
        for name, shape, dt in [
            ("g0", [128, 512], F32), ("g1", [128, 512], F32),
            ("g2", [128, 512], F32), ("g3", [128, 512], F32),
            ("sm", [128, 512], F32), ("tp", [128, 1024], BF16),
            ("row", [1, 1024], F32),
        ]:
            cm = self.nc.psum_tensor(name, shape, dt)
            self.ps[name] = cm.__enter__()
            self.psum_cm.append(cm)
        self.block_cm = self.nc.Block()
        self.block = self.block_cm.__enter__()
        self.gbanks = [self.ps[k].ap() for k in ("g0", "g1", "g2", "g3")]
        self.bstate = {}
        self.dummy = self.arena.bf(16)[0:1, 0:1]

    def dcl(self, name, shape, dt, kind="ExternalInput"):
        if name not in self.dram:
            self.dram[name] = self.nc.dram_tensor(name, shape, dt,
                                                  kind=kind).ap()
        return self.dram[name]

    def bw(self, eng, key):
        st = self.bstate.get(key)
        if st:
            self.E.wait(eng, st[0], st[1])

    def bs(self, key, sem, val):
        self.bstate[key] = (sem, val)

    def finish(self):
        self.block_cm.__exit__(None, None, None)
        for cm in reversed(self.psum_cm):
            cm.__exit__(None, None, None)
        self.arena.close()
        self.E.close()
        return self.nc


def _mm_pipeline(P, n_groups, emit_mms, emit_evac, evac="act", depth=4):
    E, block = P.E, P.block
    pe0 = E.cnt["pe"]
    ev0 = E.cnt[evac]

    @block.tensor
    def _(t):
        for j in range(n_groups):
            bi = j % depth
            if j >= depth:
                E.wait(t, evac, ev0 + j - depth + 1)
            else:
                P.bw(t, ("g", bi))
            last = emit_mms(t, j, P.gbanks[bi])
            E.inc(last, "pe", 1)

    sec = block.scalar if evac == "act" else block.vector

    @sec
    def _(e):
        for j in range(n_groups):
            E.wait(e, "pe", pe0 + j + 1)
            last = emit_evac(e, j, P.gbanks[j % depth])
            E.inc(last, evac, 1)

    for j in range(max(0, n_groups - depth), n_groups):
        P.bs(("g", j % depth), evac, ev0 + j + 1)


def _emit_consts(P, dram):
    nc, E, block, ar = P.nc, P.E, P.block, P.arena
    C = {}
    C["id_bf"] = ar.bf(128)
    C["ones_col_bf"] = ar.bf(16)[:, 0:1]
    C["ones_row_bf"] = ar.bf(128)[0:1, :]
    C["arrq"] = ar.f32(128)
    C["arrk"] = ar.f32(128)
    C["delta"] = ar.f32(128)
    C["slope"] = ar.f32(16)
    C["sneg"] = ar.f32(16)
    C["temp"] = ar.f32(16)
    C["rtemp"] = ar.f32(16)
    C["nw"] = ar.f32(16)
    C["bdec"] = ar.f32(16)
    C["qdec_bf"] = ar.bf(NH * 128)
    C["kdec_bf"] = ar.bf(NH * 128)
    C["dt_bf"] = ar.bf(NH * 128)
    C["id_f32"] = ar.f32(128)
    C["sel8_bf"] = ar.bf(8 * 128)
    C["rwE_bf"] = ar.bf(1024)

    @block.gpsimd
    def _(g):
        E.inc(g.dma_start(out=C["id_bf"], in_=dram["c_id"]), "wq", 16)
        E.inc(g.dma_start(out=C["ones_col_bf"], in_=dram["c_ones_col"]), "wq", 16)
        E.inc(g.dma_start(out=C["ones_row_bf"], in_=dram["c_ones_row"]), "wq", 16)
    wq_c = E.cnt["wq"]

    @block.sync
    def _(sy):
        for k, src in [("arrq", "c_arrq"), ("arrk", "c_arrk"),
                       ("delta", "c_delta"), ("slope", "c_slope"),
                       ("nw", "c_nw")]:
            E.inc(sy.dma_start(out=C[k], in_=dram[src]), "xq", 16)
        E.inc(sy.dma_start(out=C["temp"][:, 0:1], in_=dram["c_temp"]), "xq", 16)
        E.inc(sy.dma_start(out=C["id_f32"], in_=dram["c_id"]), "xq", 16)
    xq_c = E.cnt["xq"]

    @block.vector
    def _(v):
        E.wait(v, "xq", xq_c)
        v.tensor_scalar_mul(C["sneg"], C["slope"], -1.0)
        v.tensor_scalar_mul(C["bdec"], C["slope"], -128.0)
        E.inc(v.reciprocal(C["rtemp"][:, 0:1], C["temp"][:, 0:1]), "dve", 1)
    dve_c = E.cnt["dve"]

    @block.scalar
    def _(sc):
        E.wait(sc, "dve", dve_c)
        sc.activation(C["bdec"], C["bdec"], AF.Exp)
        last = None
        for h in range(NH):
            s = C["sneg"][:, h:h + 1]
            sc.activation(C["qdec_bf"][:, h * 128:(h + 1) * 128], C["arrq"],
                          AF.Exp, scale=s)
            sc.activation(C["kdec_bf"][:, h * 128:(h + 1) * 128], C["arrk"],
                          AF.Exp, scale=s)
            last = sc.activation(C["dt_bf"][:, h * 128:(h + 1) * 128],
                                 C["delta"], AF.Exp, scale=s)
        E.inc(last, "act", 1)
    C["ready_act"] = E.cnt["act"]
    C["ready_wq"] = wq_c
    return C


def _emit_ln(P, C, x_src_dram, src_t0, t_total, xn_dst, dst_t_total, dst_t0, S):
    E, block = P.E, P.block
    xbf, x2b, tmpf = S["xbf"], S["x2b"], S["tmpf"]
    rowm, rowr = S["rowm"], S["rowr"]
    rowm_bf, rowr_bf = S["rowm_bf"], S["rowr_bf"]
    mbc, rbc = S["mbc"], S["rbc"]
    row = P.ps["row"].ap()

    for grp in range(t_total // 512):
        t0 = src_t0 + grp * 512
        dve_now = E.cnt["dve"]

        @block.gpsimd
        def _(g):
            E.wait(g, "dve", dve_now)
            for ch in range(NCH):
                E.inc(g.dma_start(
                    out=xbf[:, ch * 512:(ch + 1) * 512],
                    in_=x_src_dram[ch * 128:(ch + 1) * 128, t0:t0 + 512]),
                    "xq", 16)
        xq0 = E.cnt["xq"] - 16 * NCH

        pe0 = E.cnt["pe"]
        act0 = E.cnt["act"]

        @block.scalar
        def _(sc):
            E.wait(sc, "xq", xq0 + 16 * NCH)
            for ch in range(NCH):
                if ch < 2:
                    E.wait(sc, "pe", pe0)
                else:
                    E.wait(sc, "pe", pe0 + ch - 1)
                E.inc(sc.activation(x2b[:, (ch % 2) * 512:(ch % 2 + 1) * 512],
                                    xbf[:, ch * 512:(ch + 1) * 512],
                                    AF.Square), "act", 1)

        @block.tensor
        def _(t):
            P.bw(t, "row")
            for ch in range(NCH):
                E.wait(t, "act", act0 + ch + 1)
                t.matmul(row[0:1, 0:512], C["ones_col_bf"],
                         xbf[:, ch * 512:(ch + 1) * 512],
                         start=(ch == 0), stop=(ch == NCH - 1))
                last = t.matmul(row[0:1, 512:1024], C["ones_col_bf"],
                                x2b[:, (ch % 2) * 512:(ch % 2 + 1) * 512],
                                start=(ch == 0), stop=(ch == NCH - 1))
                E.inc(last, "pe", 1)
        pe_stats = E.cnt["pe"]

        act_now = E.cnt["act"]

        @block.vector
        def _(v):
            E.wait(v, "pe", pe_stats)
            E.wait(v, "act", act_now)
            E.ss(v, v.tensor_scalar_mul(rowm, row[0:1, 0:512], 1.0 / DM),
                 "dve")
            E.ss(v, v.tensor_tensor(rowr, rowm, rowm, ALU.mult), "dve")
            E.ss(v, v.scalar_tensor_tensor(rowr, row[0:1, 512:1024], 1.0 / DM,
                                           rowr, ALU.mult, ALU.subtract),
                 "dve")
            E.ss(v, v.tensor_scalar_add(rowr, rowr, EPS_LN), "dve")
            E.inc(v.reciprocal(rowr, rowr), "dve", 1)
        dve_rows = E.cnt["dve"]
        P.bs("row", "dve", dve_rows)

        @block.scalar
        def _(sc):
            E.wait(sc, "dve", dve_rows)
            sc.activation(rowr, rowr, AF.Sqrt)
            sc.copy(rowm_bf, rowm)
            E.inc(sc.copy(rowr_bf, rowr), "act", 1)
        act_rows = E.cnt["act"]

        @block.tensor
        def _(t):
            E.wait(t, "act", act_rows)
            P.bw(t, ("g", 0))
            P.bw(t, ("g", 1))
            t.matmul(P.gbanks[0], C["ones_row_bf"], rowm_bf,
                     start=True, stop=True)
            E.inc(t.matmul(P.gbanks[1], C["ones_row_bf"], rowr_bf,
                           start=True, stop=True), "pe", 1)
        pe_bc = E.cnt["pe"]

        @block.vector
        def _(v):
            E.wait(v, "pe", pe_bc)
            v.tensor_copy(mbc, P.gbanks[0])
            v.tensor_copy(rbc, P.gbanks[1])
            for ch in range(NCH):
                v.tensor_tensor(tmpf, xbf[:, ch * 512:(ch + 1) * 512],
                                mbc, ALU.subtract)
                last = v.tensor_tensor(
                    xn_dst[:, ch * dst_t_total + dst_t0 + grp * 512:
                           ch * dst_t_total + dst_t0 + grp * 512 + 512],
                    tmpf, rbc, ALU.mult)
            E.inc(last, "dve", 1)
        P.bs(("g", 0), "dve", E.cnt["dve"])
        P.bs(("g", 1), "dve", E.cnt["dve"])


def _emit_proj(P, C, wq, xn_src, t_total, n_m, dst_of, func, tile0,
               scale_of=None, bias_of=None, evac_cb=None, pre_waits=(),
               post=None, post_pre=(), n_k=NCH, xn_row=None):
    halves = t_total // 512
    groups = [(mi, h) for mi in range(n_m) for h in range(halves)]
    E = P.E
    dve0 = E.cnt["dve"]
    xn_row = xn_row if xn_row is not None else t_total

    frees = {}

    def mms(t, j, bank):
        if j == 0:
            for sem, val in pre_waits:
                P.E.wait(t, sem, val)
        mi, half = groups[j]
        last = None
        for k in range(n_k):
            idx = tile0 + mi * n_k + k
            w = wq.tile(t, idx)
            last = t.matmul(bank, w,
                            xn_src[:, k * xn_row + half * 512:
                                   k * xn_row + half * 512 + 512],
                            start=(k == 0), stop=(k == n_k - 1))
            if half == halves - 1 and wq.ends_slab(idx):
                frees[j] = frees.get(j, 0) + 1
        return last

    def evac(e, j, bank):
        mi, half = groups[j]
        if post is not None and j >= 4:
            E.wait(e, "dve", dve0 + j - 3)
        kw = {}
        if scale_of is not None:
            kw["scale"] = scale_of(mi)
        if bias_of is not None:
            kw["bias"] = bias_of(mi)
        last = e.activation(dst_of(mi, half), bank, func, **kw)
        if evac_cb is not None:
            last = evac_cb(e, mi, half, bank) or last
        if j in frees:
            P.E.inc(e.copy(P.dummy, P.dummy), "sfree", frees[j])
        return last

    _mm_pipeline(P, len(groups), mms, evac, evac="act")

    if post is not None:
        ev0 = E.cnt["act"] - len(groups)

        @P.block.vector
        def _(v):
            for j, (mi, half) in enumerate(groups):
                E.wait(v, "act", ev0 + j + 1)
                if j == 0:
                    for sem, val in post_pre:
                        E.wait(v, sem, val)
                last = post(v, mi, half)
                E.inc(last, "dve", 1)
    return tile0 + n_m * n_k


def _emit_kv_prefix_chunk(P, C, kp, vp, tcols, ch, kv, L):
    E, block = P.E, P.block
    tp = P.ps["tp"].ap()
    for w in range(4):
        hs = [4 * w + i for i in range(4)]

        @block.vector
        def _(v):
            E.wait(v, "act", L["src_act"])
            P.bw(v, "lt_tmpkp")
            last = None
            for i, h in enumerate(hs):
                last = v.tensor_tensor(
                    L["tmp_kp"][:, i * 128:(i + 1) * 128],
                    kp[:, h * tcols + ch * 128: h * tcols + ch * 128 + 128],
                    C["kdec_bf"][:, h * 128:(h + 1) * 128], ALU.mult)
            E.inc(last, "dve", 1)
        dve_kp = E.cnt["dve"]

        @block.tensor
        def _(t):
            E.wait(t, "dve", dve_kp)
            P.bw(t, "tp")
            last = None
            for i, h in enumerate(hs):
                t.transpose(tp[:, i * 256:i * 256 + 128],
                            L["tmp_kp"][:, i * 128:(i + 1) * 128], C["id_bf"])
                last = t.transpose(
                    tp[:, i * 256 + 128:i * 256 + 256],
                    vp[:, h * tcols + ch * 128: h * tcols + ch * 128 + 128],
                    C["id_bf"])
            E.inc(last, "pe", 1)
        P.bs("lt_tmpkp", "pe", E.cnt["pe"])
        pe_tr = E.cnt["pe"]

        @block.scalar
        def _(sc):
            E.wait(sc, "pe", pe_tr)
            P.bw(sc, "lt_ktvt")
            last = None
            for i in range(4):
                sc.copy(L["kT"][:, i * 128:(i + 1) * 128],
                        tp[:, i * 256:i * 256 + 128])
                last = sc.copy(L["vT"][:, i * 128:(i + 1) * 128],
                               tp[:, i * 256 + 128:i * 256 + 256])
            E.inc(last, "act", 1)
        P.bs("tp", "act", E.cnt["act"])
        act_ev = E.cnt["act"]

        @block.tensor
        def _(t):
            E.wait(t, "act", act_ev)
            P.bw(t, ("g", 3))
            last = None
            for i in range(4):
                last = t.matmul(P.gbanks[3][:, i * 128:(i + 1) * 128],
                                L["kT"][:, i * 128:(i + 1) * 128],
                                L["vT"][:, i * 128:(i + 1) * 128],
                                start=True, stop=True)
            E.inc(last, "pe", 1)
        P.bs("lt_ktvt", "pe", E.cnt["pe"])
        pe_u = E.cnt["pe"]

        @block.vector
        def _(v):
            E.wait(v, "pe", pe_u)
            last = None
            for i, h in enumerate(hs):
                last = v.scalar_tensor_tensor(
                    kv[:, h * 128:(h + 1) * 128], kv[:, h * 128:(h + 1) * 128],
                    C["bdec"][:, h:h + 1],
                    P.gbanks[3][:, i * 128:(i + 1) * 128], ALU.mult, ALU.add)
            E.inc(last, "dve", 1)
        P.bs(("g", 3), "dve", E.cnt["dve"])
    L["kv_dve"] = E.cnt["dve"]


def _emit_light_own_chunk(P, C, q_sb, k_sb, v_sb, hg, ch, kv, kv_bf, o_sb, L):
    E, block = P.E, P.block
    tp = P.ps["tp"].ap()
    row = P.ps["row"].ap()
    n_w = HG // 4
    for w in range(n_w):
        hls = [4 * w + i for i in range(4)]

        @block.tensor
        def _(t):
            E.wait(t, "act", L["src_act"])
            P.bw(t, ("g", 2))
            last = None
            for i, hl in enumerate(hls):
                last = t.matmul(
                    P.gbanks[2][:, i * 128:(i + 1) * 128],
                    k_sb[:, hl * TOWN + ch * 128: hl * TOWN + ch * 128 + 128],
                    q_sb[:, hl * TOWN + ch * 128: hl * TOWN + ch * 128 + 128],
                    start=True, stop=True)
            E.inc(last, "pe", 1)
        pe_s = E.cnt["pe"]

        @block.vector
        def _(v):
            E.wait(v, "pe", pe_s)
            P.bw(v, "lt_tmpkp")
            last = None
            for i, hl in enumerate(hls):
                h = hg * HG + hl
                v.tensor_tensor(L["E"][:, i * 128:(i + 1) * 128],
                                P.gbanks[2][:, i * 128:(i + 1) * 128],
                                C["dt_bf"][:, h * 128:(h + 1) * 128], ALU.mult)
                v.tensor_tensor(
                    L["tmp_kp"][:, i * 128:(i + 1) * 128],
                    k_sb[:, hl * TOWN + ch * 128: hl * TOWN + ch * 128 + 128],
                    C["kdec_bf"][:, h * 128:(h + 1) * 128], ALU.mult)
                last = v.tensor_tensor(
                    L["tmp_qp"][:, i * 128:(i + 1) * 128],
                    q_sb[:, hl * TOWN + ch * 128: hl * TOWN + ch * 128 + 128],
                    C["qdec_bf"][:, h * 128:(h + 1) * 128], ALU.mult)
            E.inc(last, "dve", 1)
        P.bs(("g", 2), "dve", E.cnt["dve"])
        dve_e = E.cnt["dve"]

        @block.tensor
        def _(t):
            E.wait(t, "dve", dve_e)
            P.bw(t, "tp")
            last = None
            for i, hl in enumerate(hls):
                t.transpose(tp[:, i * 256:i * 256 + 128],
                            L["tmp_kp"][:, i * 128:(i + 1) * 128], C["id_bf"])
                last = t.transpose(
                    tp[:, i * 256 + 128:i * 256 + 256],
                    v_sb[:, hl * TOWN + ch * 128: hl * TOWN + ch * 128 + 128],
                    C["id_bf"])
            E.inc(last, "pe", 1)
        P.bs("lt_tmpkp", "pe", E.cnt["pe"])
        pe_tr = E.cnt["pe"]

        @block.scalar
        def _(sc):
            E.wait(sc, "pe", pe_tr)
            E.wait(sc, "dve", L["kv_dve"])
            P.bw(sc, "lt_ktvt")
            P.bw(sc, "lt_kvbf")
            last = None
            for i, hl in enumerate(hls):
                h = hg * HG + hl
                sc.copy(L["kT"][:, i * 128:(i + 1) * 128],
                        tp[:, i * 256:i * 256 + 128])
                sc.copy(L["vT"][:, i * 128:(i + 1) * 128],
                        tp[:, i * 256 + 128:i * 256 + 256])
                last = sc.copy(kv_bf[:, i * 128:(i + 1) * 128],
                               kv[:, h * 128:(h + 1) * 128])
            E.inc(last, "act", 1)
        P.bs("tp", "act", E.cnt["act"])
        act_ev = E.cnt["act"]

        @block.tensor
        def _(t):
            E.wait(t, "act", act_ev)
            P.bw(t, ("g", 1))
            P.bw(t, ("g", 3))
            last = None
            for i, hl in enumerate(hls):
                t.matmul(P.gbanks[1][:, i * 128:(i + 1) * 128],
                         L["vT"][:, i * 128:(i + 1) * 128],
                         L["E"][:, i * 128:(i + 1) * 128],
                         start=True, stop=False)
                t.matmul(P.gbanks[1][:, i * 128:(i + 1) * 128],
                         kv_bf[:, i * 128:(i + 1) * 128],
                         L["tmp_qp"][:, i * 128:(i + 1) * 128],
                         start=False, stop=True)
                last = t.matmul(P.gbanks[3][:, i * 128:(i + 1) * 128],
                                L["kT"][:, i * 128:(i + 1) * 128],
                                L["vT"][:, i * 128:(i + 1) * 128],
                                start=True, stop=True)
            E.inc(last, "pe", 1)
        P.bs("lt_ktvt", "pe", E.cnt["pe"])
        P.bs("lt_kvbf", "pe", E.cnt["pe"])
        pe_o = E.cnt["pe"]

        @block.vector
        def _(v):
            E.wait(v, "pe", pe_o)
            last = None
            for i, hl in enumerate(hls):
                h = hg * HG + hl
                last = v.scalar_tensor_tensor(
                    kv[:, h * 128:(h + 1) * 128], kv[:, h * 128:(h + 1) * 128],
                    C["bdec"][:, h:h + 1],
                    P.gbanks[3][:, i * 128:(i + 1) * 128], ALU.mult, ALU.add)
            E.inc(last, "dve", 1)
        P.bs(("g", 3), "dve", E.cnt["dve"])
        if w == n_w - 1:
            L["kv_dve"] = E.cnt["dve"]

        @block.scalar
        def _(sc):
            E.wait(sc, "pe", pe_o)
            P.bw(sc, "lt_o2")
            last = None
            for i, hl in enumerate(hls):
                h = hg * HG + hl
                sc.copy(o_sb[:, h * TOWN + ch * 128: h * TOWN + ch * 128 + 128],
                        P.gbanks[1][:, i * 128:(i + 1) * 128])
                last = sc.activation(L["o2"][:, i * 128:(i + 1) * 128],
                                     P.gbanks[1][:, i * 128:(i + 1) * 128],
                                     AF.Square)
            E.inc(last, "act", 1)
        P.bs(("g", 1), "act", E.cnt["act"])
        act_o = E.cnt["act"]

        @block.tensor
        def _(t):
            E.wait(t, "act", act_o)
            P.bw(t, "row")
            last = None
            for i in range(4):
                last = t.matmul(row[0:1, ch * 128:(ch + 1) * 128],
                                C["ones_col_bf"],
                                L["o2"][:, i * 128:(i + 1) * 128],
                                start=(w == 0 and i == 0),
                                stop=(w == n_w - 1 and i == 3))
            E.inc(last, "pe", 1)
        P.bs("lt_o2", "pe", E.cnt["pe"])


def _emit_ffn0(P, C, dram, rwE_bf, rwE_act, accum, h_sb, xnst, wbuf, gsl, usl,
               wbc, xn0_d, guard_oq=0):
    E, block = P.E, P.block
    base = dram["ffn0_t"]
    for e in range(NEXP):
        ebase = e * 768 * 128
        dve_bc0 = E.cnt["dve"]

        @block.tensor
        def _(t, e=e):
            E.wait(t, "act", rwE_act)
            P.bw(t, ("g", 2))
            P.bw(t, ("g", 3))
            t.matmul(P.gbanks[2], C["sel8_bf"][0:8, e * 128:(e + 1) * 128],
                     rwE_bf[0:8, 0:512], start=True, stop=True)
            E.inc(t.matmul(P.gbanks[3],
                           C["sel8_bf"][0:8, e * 128:(e + 1) * 128],
                           rwE_bf[0:8, 512:1024], start=True, stop=True),
                  "pe", 1)
        pe_bc = E.cnt["pe"]

        @block.vector
        def _(v):
            E.wait(v, "pe", pe_bc)
            v.tensor_copy(wbc[:, 0:512], P.gbanks[2])
            E.inc(v.tensor_copy(wbc[:, 512:1024], P.gbanks[3]), "dve", 1)
        P.bs(("g", 2), "dve", E.cnt["dve"])
        P.bs(("g", 3), "dve", E.cnt["dve"])
        wbc_dve = E.cnt["dve"]

        for half in range(2):
            @block.sync
            def _(sy, half=half):
                E.wait(sy, "oq", guard_oq)
                E.wait(sy, "pe", E.cnt["pe"])
                src = xn0_d[:, :].rearrange(
                    "p (k t) -> p k t", t=TOWN)[:, :, half * 512:
                                                half * 512 + 512]
                E.inc(sy.dma_start(
                    out=xnst.rearrange("p (k t) -> p k t", t=512), in_=src),
                    "xq", 16)
            xq_xn = E.cnt["xq"]

            wq = _WQ(E, base[ebase:ebase + 512 * 128, :], 512, 16, wbuf)

            @block.gpsimd
            def _(g):
                wq.emit_loads(g)

            pe0 = E.cnt["pe"]
            act0 = E.cnt["act"]
            dve0 = E.cnt["dve"]

            @block.tensor
            def _(t, half=half, wq=wq, pe0=pe0, act0=act0):
                E.wait(t, "xq", xq_xn)
                for F in range(16):
                    pair = F % 2
                    if F >= 2:
                        E.wait(t, "act", act0 + F - 1)
                    else:
                        P.bw(t, ("g", 2 * pair))
                        P.bw(t, ("g", 2 * pair + 1))
                    for k in range(NCH):
                        t.matmul(P.gbanks[2 * pair], wq.tile(t, F * 32 + k),
                                 xnst[:, k * 512:(k + 1) * 512],
                                 start=(k == 0), stop=(k == NCH - 1))
                    last = None
                    for k in range(NCH):
                        last = t.matmul(P.gbanks[2 * pair + 1],
                                        wq.tile(t, F * 32 + 16 + k),
                                        xnst[:, k * 512:(k + 1) * 512],
                                        start=(k == 0), stop=(k == NCH - 1))
                    E.inc(last, "pe", 1)

            @block.scalar
            def _(sc, pe0=pe0, dve0=dve0, wq=wq):
                for F in range(16):
                    pair = F % 2
                    E.wait(sc, "pe", pe0 + F + 1)
                    if F >= 2:
                        E.wait(sc, "dve", dve0 + F - 1)
                    sc.activation(gsl[:, pair * 512:(pair + 1) * 512],
                                  P.gbanks[2 * pair], AF.Silu)
                    last = sc.activation(usl[:, pair * 512:(pair + 1) * 512],
                                         P.gbanks[2 * pair + 1], AF.Copy)
                    E.inc(last, "act", 1)
                    E.inc(sc.copy(P.dummy, P.dummy), "sfree", 2)

            @block.vector
            def _(v, half=half, act0=act0, wbc_dve=wbc_dve):
                E.wait(v, "oq", guard_oq)
                E.wait(v, "dve", wbc_dve)
                for F in range(16):
                    pair = F % 2
                    E.wait(v, "act", act0 + F + 1)
                    v.tensor_tensor(gsl[:, pair * 512:(pair + 1) * 512],
                                    gsl[:, pair * 512:(pair + 1) * 512],
                                    usl[:, pair * 512:(pair + 1) * 512],
                                    ALU.mult)
                    last = v.tensor_tensor(
                        h_sb[:, F * TOWN + half * 512: F * TOWN + half * 512 + 512],
                        gsl[:, pair * 512:(pair + 1) * 512],
                        wbc[:, half * 512:(half + 1) * 512], ALU.mult)
                    E.inc(last, "dve", 1)
            for bi in range(4):
                P.bs(("g", bi), "act", E.cnt["act"])

        h_dve = E.cnt["dve"]

        wdq = _WQ(E, base[ebase + 512 * 128:ebase + 768 * 128, :], 256, 16,
                  wbuf)

        @block.gpsimd
        def _(g):
            wdq.emit_loads(g)

        def mms(t, j, bank, wdq=wdq, h_dve=h_dve):
            if j == 0:
                E.wait(t, "dve", h_dve)
            mi, half = divmod(j, 2)
            last = None
            for k in range(NCH):
                idx = mi * NCH + k
                last = t.matmul(bank, wdq.tile(t, idx),
                                h_sb[:, k * TOWN + half * 512:
                                     k * TOWN + half * 512 + 512],
                                start=(k == 0), stop=(k == NCH - 1))
                if half == 1 and wdq.ends_slab(idx):
                    frees_b[j] = frees_b.get(j, 0) + 1
            return last

        def evac(v, j, bank, e=e):
            mi, half = divmod(j, 2)
            dst = accum[:, mi * TOWN + half * 512: mi * TOWN + half * 512 + 512]
            if e == 0:
                last = v.tensor_copy(dst, bank)
            else:
                last = v.tensor_tensor(dst, dst, bank, ALU.add)
            if j in frees_b:
                P.E.inc(v.tensor_copy(P.dummy, P.dummy), "sfree", frees_b[j])
            return last

        frees_b = {}
        _mm_pipeline(P, 32, mms, evac, evac="dve")


def _emit_p5b(P, C, dram, accum, s0_d, x1_d, xn1_d, T5, guard_oq=0):
    E, block = P.E, P.block
    row = P.ps["row"].ap()
    xs, ss, tb = T5["xs"], T5["ss"], T5["tb"]
    upd, x1b, x1sq, xn1s = T5["upd"], T5["x1b"], T5["x1sq"], T5["xn1s"]
    dve_a = E.cnt["dve"]
    dve_x1 = {}
    pe_st = {}
    acts = {}
    pe0 = E.cnt["pe"]

    for m in range(NCH):
        xsem = "xq" if m % 2 == 0 else "xq2"

        @block.sync
        def _(sy, m=m, xsem=xsem):
            if m < 2:
                E.wait(sy, "dve", dve_a)
            else:
                E.wait(sy, "dve", dve_x1[m - 2])
            sl = xs[:, (m % 2) * 1024:(m % 2) * 1024 + 1024]
            E.inc(sy.dma_start(out=sl, in_=dram["xT"][m * 128:(m + 1) * 128, :]),
                  xsem, 16)
            E.inc(sy.dma_start(out=ss[:, (m % 2) * 1024:(m % 2) * 1024 + 1024],
                               in_=s0_d[:, m * TOWN:(m + 1) * TOWN]), xsem, 16)
        xq_m = (xsem, E.cnt[xsem])

        @block.scalar
        def _(sc, m=m):
            E.wait(sc, "dve", dve_a)
            if m >= 2:
                E.wait(sc, "dve", dve_x1[m - 2])
            E.inc(sc.activation(tb[:, (m % 2) * 1024:(m % 2) * 1024 + 1024],
                                accum[:, m * TOWN:(m + 1) * TOWN], AF.Silu),
                  "act", 1)
        act_tb = E.cnt["act"]

        @block.vector
        def _(v, m=m, xq_m=xq_m, act_tb=act_tb):
            E.wait(v, "oq", guard_oq)
            E.wait(v, xq_m[0], xq_m[1])
            E.wait(v, "act", act_tb)
            v.tensor_tensor(upd, ss[:, (m % 2) * 1024:(m % 2) * 1024 + 1024],
                            tb[:, (m % 2) * 1024:(m % 2) * 1024 + 1024],
                            ALU.mult)
            E.inc(v.tensor_tensor(accum[:, m * TOWN:(m + 1) * TOWN],
                                  xs[:, (m % 2) * 1024:(m % 2) * 1024 + 1024],
                                  upd, ALU.add), "dve", 1)
        dve_x1[m] = E.cnt["dve"]

        @block.sync
        def _(sy, m=m):
            E.wait(sy, "dve", dve_x1[m])
            E.inc(sy.dma_start(out=x1_d[:, m * TOWN:(m + 1) * TOWN],
                               in_=accum[:, m * TOWN:(m + 1) * TOWN]), "oq", 16)

        @block.scalar
        def _(sc, m=m):
            E.wait(sc, "dve", dve_x1[m])
            if m >= 2:
                E.wait(sc, "pe", pe_st[m - 2])
            sc.copy(x1b[:, (m % 2) * 1024:(m % 2) * 1024 + 1024],
                    accum[:, m * TOWN:(m + 1) * TOWN])
            E.inc(sc.activation(x1sq[:, (m % 2) * 1024:(m % 2) * 1024 + 1024],
                                accum[:, m * TOWN:(m + 1) * TOWN], AF.Square),
                  "act", 1)
        acts[m] = E.cnt["act"]

        @block.tensor
        def _(t, m=m):
            E.wait(t, "act", acts[m])
            if m == 0:
                P.bw(t, "row")
                P.bw(t, ("g", 0))
                P.bw(t, ("g", 1))
            b0 = (m % 2) * 1024
            t.matmul(row[0:1, 0:512], C["ones_col_bf"], x1b[:, b0:b0 + 512],
                     start=(m == 0), stop=(m == NCH - 1))
            t.matmul(row[0:1, 512:1024], C["ones_col_bf"],
                     x1b[:, b0 + 512:b0 + 1024],
                     start=(m == 0), stop=(m == NCH - 1))
            t.matmul(P.gbanks[0][0:1, :], C["ones_col_bf"],
                     x1sq[:, b0:b0 + 512],
                     start=(m == 0), stop=(m == NCH - 1))
            last = t.matmul(P.gbanks[1][0:1, :], C["ones_col_bf"],
                            x1sq[:, b0 + 512:b0 + 1024],
                            start=(m == 0), stop=(m == NCH - 1))
            E.inc(last, "pe", 1)
        pe_st[m] = E.cnt["pe"]

    pe_stats = E.cnt["pe"]
    mrow = T5["m1row"]
    vrow = T5["v1row"]
    mrow_bf, vrow_bf = T5["m1_bf"], T5["v1_bf"]

    @block.vector
    def _(v):
        E.wait(v, "pe", pe_stats)
        E.ss(v, v.tensor_scalar_mul(mrow, row[0:1, 0:1024], 1.0 / DM), "dve")
        E.ss(v, v.tensor_tensor(vrow, mrow, mrow, ALU.mult), "dve")
        v.scalar_tensor_tensor(vrow[0:1, 0:512], P.gbanks[0][0:1, :],
                               1.0 / DM, vrow[0:1, 0:512],
                               ALU.mult, ALU.subtract)
        E.ss(v, v.scalar_tensor_tensor(vrow[0:1, 512:1024],
                                       P.gbanks[1][0:1, :],
                                       1.0 / DM, vrow[0:1, 512:1024],
                                       ALU.mult, ALU.subtract), "dve")
        E.ss(v, v.tensor_scalar_add(vrow, vrow, EPS_LN), "dve")
        E.inc(v.reciprocal(vrow, vrow), "dve", 1)
    dve_rows = E.cnt["dve"]
    P.bs("row", "dve", dve_rows)
    P.bs(("g", 0), "dve", dve_rows)
    P.bs(("g", 1), "dve", dve_rows)

    @block.scalar
    def _(sc):
        E.wait(sc, "dve", dve_rows)
        sc.activation(vrow, vrow, AF.Sqrt)
        sc.copy(mrow_bf, mrow)
        E.inc(sc.copy(vrow_bf, vrow), "act", 1)
    act_rows = E.cnt["act"]

    m1bc, r1bc = T5["m1bc"], T5["r1bc"]

    @block.tensor
    def _(t):
        E.wait(t, "act", act_rows)
        P.bw(t, ("g", 2))
        P.bw(t, ("g", 3))
        t.matmul(P.gbanks[2], C["ones_row_bf"], mrow_bf[0:1, 0:512],
                 start=True, stop=True)
        E.inc(t.matmul(P.gbanks[3], C["ones_row_bf"], mrow_bf[0:1, 512:1024],
                       start=True, stop=True), "pe", 1)
    pe_b1 = E.cnt["pe"]

    @block.vector
    def _(v):
        E.wait(v, "pe", pe_b1)
        v.tensor_copy(m1bc[:, 0:512], P.gbanks[2])
        E.inc(v.tensor_copy(m1bc[:, 512:1024], P.gbanks[3]), "dve", 1)
    dve_m1 = E.cnt["dve"]
    P.bs(("g", 2), "dve", dve_m1)
    P.bs(("g", 3), "dve", dve_m1)

    @block.tensor
    def _(t):
        E.wait(t, "dve", dve_m1)
        t.matmul(P.gbanks[2], C["ones_row_bf"], vrow_bf[0:1, 0:512],
                 start=True, stop=True)
        E.inc(t.matmul(P.gbanks[3], C["ones_row_bf"], vrow_bf[0:1, 512:1024],
                       start=True, stop=True), "pe", 1)
    pe_b2 = E.cnt["pe"]

    @block.vector
    def _(v):
        E.wait(v, "pe", pe_b2)
        v.tensor_copy(r1bc[:, 0:512], P.gbanks[2])
        E.inc(v.tensor_copy(r1bc[:, 512:1024], P.gbanks[3]), "dve", 1)
    P.bs(("g", 2), "dve", E.cnt["dve"])
    P.bs(("g", 3), "dve", E.cnt["dve"])

    oq_xn = {}
    for m in range(NCH):
        osem = "oq" if m % 2 == 0 else "oq2"

        @block.vector
        def _(v, m=m):
            if m >= 2:
                E.wait(v, oq_xn[m - 2][0], oq_xn[m - 2][1])
            v.tensor_tensor(upd, accum[:, m * TOWN:(m + 1) * TOWN], m1bc,
                            ALU.subtract)
            E.inc(v.tensor_tensor(xn1s[:, (m % 2) * 1024:(m % 2) * 1024 + 1024],
                                  upd, r1bc, ALU.mult), "dve", 1)
        dve_xn = E.cnt["dve"]

        @block.sync
        def _(sy, m=m, dve_xn=dve_xn, osem=osem):
            E.wait(sy, "dve", dve_xn)
            E.inc(sy.dma_start(out=xn1_d[:, m * TOWN:(m + 1) * TOWN],
                               in_=xn1s[:, (m % 2) * 1024:(m % 2) * 1024 + 1024]),
                  osem, 16)
        oq_xn[m] = (osem, E.cnt[osem])



NPRECH = TPRE // 128
PSG = 512
NSG = TPRE // PSG
HG = 4


def _emit_rms_rows(P, C, var_acc, rstd_bc, rb_row):
    E, block = P.E, P.block
    dve_v = E.cnt["dve"]

    @block.vector
    def _(v):
        E.ss(v, v.tensor_scalar(var_acc[0:1, :], var_acc[0:1, :], 1.0 / DM,
                                EPS_RMS, ALU.mult, ALU.add), "dve")
        E.inc(v.reciprocal(var_acc[0:1, :], var_acc[0:1, :]), "dve", 1)
    dve_r = E.cnt["dve"]

    @block.scalar
    def _(sc):
        E.wait(sc, "dve", dve_r)
        sc.activation(var_acc[0:1, :], var_acc[0:1, :], AF.Sqrt)
        E.inc(sc.copy(rb_row, var_acc[0:1, :]), "act", 1)
    act_r = E.cnt["act"]

    @block.tensor
    def _(t):
        E.wait(t, "act", act_r)
        P.bw(t, ("g", 0))
        P.bw(t, ("g", 1))
        t.matmul(P.gbanks[0], C["ones_row_bf"], rb_row[0:1, 0:512],
                 start=True, stop=True)
        E.inc(t.matmul(P.gbanks[1], C["ones_row_bf"], rb_row[0:1, 512:1024],
                       start=True, stop=True), "pe", 1)
    pe_b = E.cnt["pe"]

    @block.vector
    def _(v):
        E.wait(v, "pe", pe_b)
        v.tensor_copy(rstd_bc[:, 0:512], P.gbanks[0])
        E.inc(v.tensor_copy(rstd_bc[:, 512:1024], P.gbanks[1]), "dve", 1)
    P.bs(("g", 0), "dve", E.cnt["dve"])
    P.bs(("g", 1), "dve", E.cnt["dve"])
    return E.cnt["dve"]


def _emit_router(P, C, dram, xn_own, wr_sb, rwE_bf, R, wq_ready=0):
    E, block = P.E, P.block
    row = P.ps["row"]

    pe0 = E.cnt["pe"]

    @block.tensor
    def _(t):
        E.wait(t, "wq", wq_ready)
        P.bw(t, ("g", 2))
        P.bw(t, ("g", 3))
        for half in range(2):
            for k in range(NCH):
                last = t.matmul(P.gbanks[2 + half][0:8, 0:512],
                                wr_sb[:, k * 8:(k + 1) * 8],
                                xn_own[:, k * TOWN + half * 512:
                                       k * TOWN + half * 512 + 512],
                                start=(k == 0), stop=(k == NCH - 1))
        E.inc(last, "pe", 1)
    pe_l = E.cnt["pe"]

    @block.vector
    def _(v):
        E.wait(v, "pe", pe_l)
        v.tensor_scalar_mul(R["inh"][0:8, 0:512], P.gbanks[2][0:8, 0:512],
                            C["rtemp"][0:8, 0:1])
        E.inc(v.tensor_scalar_mul(R["inh"][0:8, 512:1024],
                                  P.gbanks[3][0:8, 0:512],
                                  C["rtemp"][0:8, 0:1]), "dve", 1)
    dve_l = E.cnt["dve"]
    P.bs(("g", 2), "dve", dve_l)
    P.bs(("g", 3), "dve", dve_l)

    @block.scalar
    def _(sc):
        E.wait(sc, "dve", dve_l)
        E.inc(sc.activation(R["inh"], R["inh"], AF.Sigmoid), "act", 1)
    act_l = E.cnt["act"]

    @block.tensor
    def _(t):
        E.wait(t, "act", act_l)
        P.bw(t, ("g", 3))
        for tt in range(8):
            last = t.transpose(P.gbanks[3][:, tt * 8:(tt + 1) * 8],
                               R["inh"][0:8, tt * 128:(tt + 1) * 128],
                               C["id_f32"][0:8, 0:8])
        E.inc(last, "pe", 1)
    pe_t = E.cnt["pe"]

    @block.vector
    def _(v):
        E.wait(v, "pe", pe_t)
        E.inc(v.tensor_copy(R["tok"], P.gbanks[3][:, 0:64]), "dve", 1)
    dve_t = E.cnt["dve"]
    P.bs(("g", 3), "dve", dve_t)

    @block.scalar
    def _(sc):
        E.wait(sc, "dve", dve_t)
        E.inc(sc.activation(R["dist"], R["tok"], AF.Abs,
                            bias=R["negC"][:, 0:1]), "act", 1)
    act_d = E.cnt["act"]

    @block.vector
    def _(v):
        E.wait(v, "act", act_d)
        v.tensor_scalar_mul(R["e1"], R["dist"], -10.0)
        E.inc(v.tensor_scalar_mul(R["e2"], R["dist"], -10.0 / 3.0), "dve", 1)
    dve_d = E.cnt["dve"]

    @block.scalar
    def _(sc):
        E.wait(sc, "dve", dve_d)
        sc.activation(R["e1"], R["e1"], AF.Exp)
        E.inc(sc.activation(R["e2"], R["e2"], AF.Exp), "act", 1)
    act_e = E.cnt["act"]

    @block.vector
    def _(v):
        E.wait(v, "act", act_e)
        tok, e1, e2 = R["tok"], R["e1"], R["e2"]
        t1, t2, t3 = R["t1"], R["t2"], R["t3"]
        cw, cne, cni = R["cw"], R["cne"], R["cni"]
        cm, cs, cr = R["cm"], R["cs"], R["cr"]

        def r3(x):
            return x.rearrange("p (t e) -> p t e", e=8)

        v.tensor_single_scalar(t1, tok, GLD_L, ALU.is_ge)
        E.ss(v, v.tensor_single_scalar(t2, tok, GLD_U, ALU.is_le), "dve")
        E.ss(v, v.tensor_tensor(t1, t1, t2, ALU.mult), "dve")
        E.ss(v, v.tensor_tensor(t1, t1, e1, ALU.mult), "dve")
        E.ss(v, v.reduce_sum(cw, r3(t1), axis=AX.X), "dve")
        v.tensor_single_scalar(cne, cw, 1e-8, ALU.is_lt)
        E.ss(v, v.reduce_max(cm, r3(e2), axis=AX.X), "dve")
        for tt in range(8):
            v.tensor_single_scalar(t2[:, tt * 8:(tt + 1) * 8],
                                   e2[:, tt * 8:(tt + 1) * 8],
                                   cm[:, tt:tt + 1], ALU.is_ge)
        E.ss(v, v.tensor_scalar(cni, cne, -1.0, 1.0, ALU.mult, ALU.add),
             "dve")
        E.ss(v, v.scalar_tensor_tensor(t3, t2, -1e30, e2, ALU.mult, ALU.add),
             "dve")
        E.ss(v, v.reduce_max(cm, r3(t3), axis=AX.X), "dve")
        for tt in range(8):
            v.tensor_single_scalar(t3[:, tt * 8:(tt + 1) * 8],
                                   t3[:, tt * 8:(tt + 1) * 8],
                                   cm[:, tt:tt + 1], ALU.is_ge)
        E.ss(v, v.tensor_copy(P.dummy.bitcast(F32), P.dummy.bitcast(F32))
             if False else v.tensor_scalar_mul(cs, cw, 1.0), "dve")
        E.ss(v, v.tensor_tensor(t2, t2, t3, ALU.add), "dve")
        E.ss(v, v.tensor_tensor(t2, e2, t2, ALU.mult), "dve")
        E.ss(v, v.reduce_sum(cs, r3(t2), axis=AX.X), "dve")
        E.ss(v, v.tensor_single_scalar(cs, cs, 1e-8, ALU.max), "dve")
        E.ss(v, v.reciprocal(cr, cs), "dve")
        for tt in range(8):
            v.tensor_single_scalar(t2[:, tt * 8:(tt + 1) * 8],
                                   t2[:, tt * 8:(tt + 1) * 8],
                                   cr[:, tt:tt + 1], ALU.mult)
        for tt in range(8):
            v.tensor_single_scalar(t2[:, tt * 8:(tt + 1) * 8],
                                   t2[:, tt * 8:(tt + 1) * 8],
                                   cne[:, tt:tt + 1], ALU.mult)
        for tt in range(8):
            v.tensor_single_scalar(t1[:, tt * 8:(tt + 1) * 8],
                                   t1[:, tt * 8:(tt + 1) * 8],
                                   cni[:, tt:tt + 1], ALU.mult)
        E.ss(v, v.tensor_tensor(t1, t1, t2, ALU.add), "dve")
        E.ss(v, v.reduce_sum(cs, r3(t1), axis=AX.X), "dve")
        E.ss(v, v.tensor_single_scalar(cs, cs, 1e-8, ALU.max), "dve")
        E.ss(v, v.reciprocal(cr, cs), "dve")
        last = None
        for tt in range(8):
            last = v.tensor_single_scalar(R["wf"][:, tt * 8:(tt + 1) * 8],
                                          t1[:, tt * 8:(tt + 1) * 8],
                                          cr[:, tt:tt + 1], ALU.mult)
        E.inc(last, "dve", 1)
    dve_w = E.cnt["dve"]

    for rnd in range(2):
        @block.tensor
        def _(t, rnd=rnd):
            E.wait(t, "dve", dve_w)
            P.bw(t, ("g", 3))
            for i in range(4):
                tt = rnd * 4 + i
                last = t.transpose(P.gbanks[3][0:8, i * 128:(i + 1) * 128],
                                   R["wf"][:, tt * 8:(tt + 1) * 8],
                                   C["id_f32"])
            E.inc(last, "pe", 1)
        pe_w = E.cnt["pe"]

        @block.scalar
        def _(sc, rnd=rnd, pe_w=pe_w):
            E.wait(sc, "pe", pe_w)
            last = None
            for i in range(4):
                tt = rnd * 4 + i
                last = sc.copy(rwE_bf[0:8, tt * 128:(tt + 1) * 128],
                               P.gbanks[3][0:8, i * 128:(i + 1) * 128])
            E.inc(last, "act", 1)
        P.bs(("g", 3), "act", E.cnt["act"])
    return E.cnt["act"]


def _build_A(debug=None):
    P = _Prog()
    _emit_partA(P, fused=False, debug=debug)
    return P.finish(), P.dram, P.outs


def _emit_partA(P, fused, debug=None):
    nc, E, block, ar = P.nc, P.E, P.block, P.arena
    ar.mark()

    dram = P.dram
    for n, shape, dt in [
        ("c_id", [128, 128], F32), ("c_arrq", [128, 128], F32),
        ("c_arrk", [128, 128], F32), ("c_delta", [128, 128], F32),
        ("c_ones_col", [128, 1], F32), ("c_ones_row", [1, 128], F32),
        ("c_slope", [128, 16], F32), ("c_temp", [128, 1], F32),
        ("c_nw", [128, 16], F32),
        ("xT", [DM, TOWN], F32), ("xTpre", [DM, TPRE], F32),
        ("wqkv_t", [48 * NCH * 128, 128], F32),
        ("wkv_t", [32 * NCH * 128, 128], F32),
        ("wgate_t", [16 * NCH * 128, 128], F32),
        ("wout_t", [16 * NCH * 128, 128], F32),
        ("wr_t", [NCH * 128, 8], F32),
        ("c_sel8", [8, 8 * 128], F32),
        ("ffn0_t", [NEXP * 768 * 128, 128], F32),
    ]:
        P.dcl(n, shape, dt)

    bkind = "Internal" if fused else "ExternalOutput"
    xn0_d = nc.dram_tensor("xn0_d", [128, NCH * TOWN], BF16).ap()
    s0_d = nc.dram_tensor("s0_d", [128, NCH * TOWN], BF16).ap()
    x1_d = P.dcl("x1_d", [128, NCH * TOWN], F32, kind=bkind)
    xn1_d = P.dcl("xn1_d", [128, NCH * TOWN], BF16, kind=bkind)
    outs = P.outs = {}
    outs["x1_d"] = x1_d
    outs["xn1_d"] = xn1_d
    if debug == "p3":
        outs["o_d"] = nc.dram_tensor("o_d", [128, NH * TOWN], BF16,
                                     kind="ExternalOutput").ap()
        outs["var_d"] = nc.dram_tensor("var_d", [1, TOWN], F32,
                                       kind="ExternalOutput").ap()
        outs["kv_d"] = nc.dram_tensor("kv_d", [128, NH * 128], F32,
                                      kind="ExternalOutput").ap()
        for n in ("q_d", "k_d", "v_d"):
            outs[n] = nc.dram_tensor(n, [128, NH * TOWN], BF16,
                                     kind="ExternalOutput").ap()

    C = _emit_consts(P, dram)

    wbuf = ar.bf(2 * 16 * 128)
    ar.mark()
    xn_own = ar.bf(NCH * TOWN)
    kv = ar.f32(NH * 128)
    kv_bf = ar.bf(4 * 128)
    o_sb = ar.bf(NH * TOWN)
    var_acc = ar.f32(1024)
    L = {"tmp_kp": ar.bf(512), "tmp_qp": ar.bf(512), "E": ar.bf(512),
         "kT": ar.bf(512), "vT": ar.bf(512), "o2": ar.bf(512),
         "kv_dve": 0}
    S = {}
    S["xbf"] = ar.bf(NCH * 512)
    S["x2b"] = ar.bf(2 * 512)
    S["tmpf"] = ar.f32(512)
    r4 = ar.f32(1024)
    S["rowm"], S["rowr"] = r4[0:1, 0:512], r4[0:1, 512:1024]
    rb = ar.bf(1024)
    S["rowm_bf"], S["rowr_bf"] = rb[0:1, 0:512], rb[0:1, 512:1024]
    S["mbc"] = ar.f32(512)
    S["rbc"] = ar.f32(512)

    @block.vector
    def _(v):
        E.inc(v.memset(kv, 0.0), "dve", 1)

    ar.mark()
    xn_pre = ar.bf(NCH * PSG)
    kp = ar.bf(NH * PSG)
    vp = ar.bf(NH * PSG)
    for sg in range(NSG):
        _emit_ln(P, C, dram["xTpre"], sg * PSG, PSG, xn_pre, PSG, 0, S)
        xnp_ready = E.cnt["dve"]
        wkv_wq = _WQ(E, dram["wkv_t"], 32 * NCH, 16, wbuf)

        @block.gpsimd
        def _(g):
            if sg == 0:
                E.wait(g, "act", C["ready_act"])
            wkv_wq.emit_loads(g)

        def dst_kv(mi, half, kp=kp, vp=vp):
            h, typ = divmod(mi, 2)
            buf = (kp, vp)[typ]
            return buf[:, h * PSG + half * 512: h * PSG + half * 512 + 512]
        _emit_proj(P, C, wkv_wq, xn_pre, PSG, 32, dst_kv, AF.Silu, 0,
                   pre_waits=[("dve", xnp_ready)])
        L["src_act"] = E.cnt["act"]
        for ch in range(PSG // 128):
            _emit_kv_prefix_chunk(P, C, kp, vp, PSG, ch, kv, L)
    ar.release()

    _emit_ln(P, C, dram["xT"], 0, TOWN, xn_own, TOWN, 0, S)
    xn_own_ready = E.cnt["dve"]

    @block.sync
    def _(sy):
        E.wait(sy, "dve", xn_own_ready)
        E.inc(sy.dma_start(out=xn0_d, in_=xn_own), "oq", 16)

    ar.mark()
    q_sb = ar.bf(HG * TOWN)
    k_sb = ar.bf(HG * TOWN)
    v_sb = ar.bf(HG * TOWN)

    qkv_wq = _WQ(E, dram["wqkv_t"], 48 * NCH, 16, wbuf)

    @block.gpsimd
    def _(g):
        qkv_wq.emit_loads(g)

    tile0 = 0
    n_hg = NH // HG
    for hg in range(n_hg):
        def dst_of(mi, half, q_sb=q_sb, k_sb=k_sb, v_sb=v_sb):
            h_local, typ = divmod(mi, 3)
            buf = (q_sb, k_sb, v_sb)[typ]
            return buf[:, h_local * TOWN + half * 512:
                       h_local * TOWN + half * 512 + 512]
        pw = [("dve", xn_own_ready)]
        if debug == "p3":
            pw.append(("oq", E.cnt["oq"]))
        tile0 = _emit_proj(P, C, qkv_wq, xn_own, TOWN, 3 * HG, dst_of,
                           AF.Silu, tile0, pre_waits=pw)
        L["src_act"] = E.cnt["act"]
        if debug == "p3":
            act_q = E.cnt["act"]

            @block.sync
            def _(sy):
                E.wait(sy, "act", act_q)
                for nm, buf in (("q_d", q_sb), ("k_d", k_sb), ("v_d", v_sb)):
                    E.inc(sy.dma_start(
                        out=outs[nm][:, hg * HG * TOWN:(hg + 1) * HG * TOWN],
                        in_=buf), "oq", 16)
        for ch in range(TOWN // 128):
            _emit_light_own_chunk(P, C, q_sb, k_sb, v_sb, hg, ch,
                                  kv, kv_bf, o_sb, L)
        pe_var = E.cnt["pe"]

        @block.vector
        def _(v):
            E.wait(v, "pe", pe_var)
            if hg == 0:
                last = v.tensor_copy(var_acc[0:1, :], P.ps["row"].ap()[0:1, :])
            else:
                last = v.tensor_tensor(var_acc[0:1, :], var_acc[0:1, :],
                                       P.ps["row"].ap()[0:1, :], ALU.add)
            E.inc(last, "dve", 1)
        P.bs("row", "dve", E.cnt["dve"])
    ar.release()

    if debug == "p3":
        dve_f = E.cnt["dve"]
        act_f = E.cnt["act"]

        @block.sync
        def _(sy):
            E.wait(sy, "dve", dve_f)
            E.wait(sy, "act", act_f)
            E.inc(sy.dma_start(out=outs["o_d"], in_=o_sb), "oq", 16)
            E.inc(sy.dma_start(out=outs["var_d"], in_=var_acc[0:1, :]),
                  "oq", 16)
            E.inc(sy.dma_start(out=outs["kv_d"], in_=kv), "oq", 16)
            E.wait(sy, "oq", E.cnt["oq"])
        ar.release()
        return

    ar.mark()
    rb_row = ar.bf(1024)[0:1, :]
    rstd_bc = ar.f32(1024)
    gsl4 = ar.bf(4 * 512)
    ztmp = ar.f32(512)
    s_sb = ar.bf(NCH * TOWN)
    wr_sb = ar.bf(NCH * 8)
    R = {"inh": ar.f32(1024)[0:8, :], "tok": ar.f32(64), "dist": ar.f32(64),
         "e1": ar.f32(64), "e2": ar.f32(64), "t1": ar.f32(64),
         "t2": ar.f32(64), "t3": ar.f32(64), "cw": ar.f32(8),
         "cne": ar.f32(8), "cni": ar.f32(8), "cm": ar.f32(8),
         "cs": ar.f32(8), "cr": ar.f32(8), "wf": ar.f32(64),
         "negC": ar.f32(16)}

    @block.vector
    def _(v):
        E.inc(v.memset(R["negC"][:, 0:1], -GLD_C), "dve", 1)

    rms_done = _emit_rms_rows(P, C, var_acc, rstd_bc, rb_row)

    wgate_wq = _WQ(E, dram["wgate_t"], 16 * NCH, 16, wbuf)

    @block.gpsimd
    def _(g):
        wgate_wq.emit_loads(g)
        E.inc(g.dma_start(out=wr_sb.rearrange("p (k m) -> p k m", m=8),
                          in_=dram["wr_t"].rearrange("(k p) m -> p k m",
                                                     p=128)), "wq", 16)
        E.inc(g.dma_start(out=C["sel8_bf"][0:8, :], in_=dram["c_sel8"]),
              "wq", 16)
    wq_p4 = E.cnt["wq"]

    def gate_dst(mi, half):
        j = (2 * mi + half) % 4
        return gsl4[:, j * 512:(j + 1) * 512]

    def z_post(v, mi, half):
        oc = o_sb[:, mi * TOWN + half * 512: mi * TOWN + half * 512 + 512]
        v.scalar_tensor_tensor(ztmp, oc, C["nw"][:, mi:mi + 1],
                               gate_dst(mi, half), ALU.mult, ALU.mult)
        return v.tensor_tensor(oc, ztmp,
                               rstd_bc[:, half * 512:(half + 1) * 512],
                               ALU.mult)

    _emit_proj(P, C, wgate_wq, xn_own, TOWN, 16, gate_dst, AF.Sigmoid, 0,
               pre_waits=[("dve", xn_own_ready)],
               post=z_post, post_pre=[("dve", rms_done)])
    z_done = E.cnt["dve"]

    wout_wq = _WQ(E, dram["wout_t"], 16 * NCH, 16, wbuf)

    @block.gpsimd
    def _(g):
        wout_wq.emit_loads(g)

    def s_dst(mi, half):
        return s_sb[:, mi * TOWN + half * 512: mi * TOWN + half * 512 + 512]

    _emit_proj(P, C, wout_wq, o_sb, TOWN, 16, s_dst, AF.Copy, 0,
               pre_waits=[("dve", z_done)])
    s_act = E.cnt["act"]

    @block.sync
    def _(sy):
        E.wait(sy, "act", s_act)
        E.inc(sy.dma_start(out=s0_d, in_=s_sb), "oq", 16)
    s_spill_oq = E.cnt["oq"]

    rwE_bf = C["rwE_bf"]
    rwE_act = _emit_router(P, C, dram, xn_own, wr_sb, rwE_bf, R,
                           wq_ready=wq_p4)

    if debug == "p4":
        outs["s_d"] = nc.dram_tensor("s_d", [128, NCH * TOWN], BF16,
                                     kind="ExternalOutput").ap()
        outs["rw_d"] = nc.dram_tensor("rw_d", [8, TOWN], BF16,
                                      kind="ExternalOutput").ap()

        @block.sync
        def _(sy):
            E.wait(sy, "act", max(rwE_act, s_act))
            E.inc(sy.dma_start(out=outs["s_d"], in_=s_sb), "oq", 16)
            E.inc(sy.dma_start(out=outs["rw_d"], in_=rwE_bf[0:8, :]),
                  "oq", 16)
        s_spill_oq = E.cnt["oq"]

    ar.release()
    ar.release()

    ar.mark()
    accum = ar.f32(NCH * TOWN)
    ar.mark()
    h_sb = ar.bf(NCH * TOWN)
    xnst = ar.bf(NCH * 512)
    gsl = ar.bf(1024)
    usl = ar.bf(1024)
    wbc = ar.f32(1024)
    _emit_ffn0(P, C, dram, rwE_bf, rwE_act, accum, h_sb, xnst, wbuf,
               gsl, usl, wbc, xn0_d, guard_oq=s_spill_oq)
    ar.release()

    ar.mark()
    T5 = {"xs": ar.f32(2 * 1024), "ss": ar.bf(2 * 1024),
          "tb": ar.bf(2 * 1024), "upd": ar.f32(1024),
          "x1b": ar.bf(2 * 1024), "x1sq": ar.bf(2 * 1024),
          "xn1s": ar.bf(2 * 1024),
          "m1row": None, "v1row": None, "m1_bf": None, "v1_bf": None,
          "m1bc": ar.f32(1024), "r1bc": ar.f32(1024)}
    r8 = ar.f32(2048)
    T5["m1row"], T5["v1row"] = r8[0:1, 0:1024], r8[0:1, 1024:2048]
    rb8 = ar.bf(2048)
    T5["m1_bf"], T5["v1_bf"] = rb8[0:1, 0:1024], rb8[0:1, 1024:2048]

    ffn0_guard = 0
    if debug == "p4":
        outs["ffn0_d"] = nc.dram_tensor("ffn0_d", [128, NCH * TOWN], F32,
                                        kind="ExternalOutput").ap()
        dve_ff = E.cnt["dve"]

        @block.sync
        def _(sy):
            E.wait(sy, "dve", dve_ff)
            E.inc(sy.dma_start(out=outs["ffn0_d"], in_=accum), "oq", 16)
        ffn0_guard = E.cnt["oq"]

    @block.sync
    def _(sy):
        E.wait(sy, "oq", s_spill_oq)
        sy.nop()

    _emit_p5b(P, C, dram, accum, s0_d, x1_d, xn1_d, T5, guard_oq=ffn0_guard)
    ar.release()

    @block.sync
    def _(sy):
        E.wait(sy, "oq", E.cnt["oq"])
        E.wait(sy, "oq2", E.cnt["oq2"])
        sy.nop()
    ar.release()



def _build_B(debug=None):
    P = _Prog()
    _emit_partB(P, fused=False)
    return P.finish(), P.dram, P.outs


C_FENCES = set()


def _fence(P, tag):
    if tag not in C_FENCES:
        return
    E, block = P.E, P.block
    waits = [(s, E.cnt[s]) for s in SEMS]
    for secname in ("tensor", "scalar", "vector", "gpsimd", "sync"):
        sec = getattr(block, secname)

        @sec
        def _(e):
            for s, v in waits:
                E.wait(e, s, v)


def _part_barrier(P):
    E, block = P.E, P.block
    waits = [(s, E.cnt[s]) for s in SEMS]

    @block.gpsimd
    def _(g):
        for s, v in waits:
            E.wait(g, s, v)

    @block.sync
    def _(sy):
        for s, v in waits:
            E.wait(sy, s, v)


def _emit_partB(P, fused):
    nc, E, block, ar = P.nc, P.E, P.block, P.arena
    if fused:
        _part_barrier(P)
    ar.mark()
    dram = P.dram
    for n, shape, dt in [
        ("c_id", [128, 128], F32),
        ("c_bq", [128, 16], F32), ("c_bk", [128, 16], F32),
        ("c_bv", [128, 16], F32),
        ("c_cos", [128, TOWN], F32), ("c_sin", [128, TOWN], F32),
        ("wq_t", [16 * NCH * 128, 128], F32),
        ("wk_t", [16 * NCH * 128, 128], F32),
        ("wv_t", [16 * NCH * 128, 128], F32),
        ("xn1_d", [128, NCH * TOWN], BF16),
    ]:
        P.dcl(n, shape, dt)
    contrib_d = P.dcl("contrib_d", [TOWN, DM], BF16,
                      kind="Internal" if fused else "ExternalOutput")
    P.outs = {"contrib_d": contrib_d}
    oq_contrib = {}
    prev_tile = None

    wbuf = ar.bf(2 * 16 * 128)
    id_bf = ar.bf(128)
    bia = {k: ar.f32(16) for k in ("bq", "bk", "bv")}
    cosb = ar.f32(TOWN)
    sinb = ar.f32(TOWN)
    xn1 = ar.bf(NCH * TOWN)
    q1 = ar.bf(NCH * 512)
    k1 = ar.bf(NCH * 512)
    v1 = ar.bf(NCH * 512)
    rt = [ar.f32(512) for _ in range(4)]
    qt = ar.bf(NH * 128)
    kt = ar.bf(NH * 128)
    vt = ar.bf(NH * 128)
    prod = ar.bf(NH * 128)
    sc = ar.f32(256)
    rbuf = ar.f32(256)
    Ee = ar.f32(256)
    rmax = ar.f32(16)
    rs = ar.f32(16)
    acc = ar.f32(NH * 128)
    acc_bf = ar.bf(NH * 128)

    @block.gpsimd
    def _(g):
        E.inc(g.dma_start(out=id_bf, in_=dram["c_id"]), "wq", 16)

    @block.sync
    def _(sy):
        for k, src in (("bq", "c_bq"), ("bk", "c_bk"), ("bv", "c_bv")):
            E.inc(sy.dma_start(out=bia[k], in_=dram[src]), "xq", 16)
        E.inc(sy.dma_start(out=cosb, in_=dram["c_cos"]), "xq", 16)
        E.inc(sy.dma_start(out=sinb, in_=dram["c_sin"]), "xq", 16)
        E.inc(sy.dma_start(out=xn1, in_=dram["xn1_d"]), "xq", 16)
    xq_c = E.cnt["xq"]
    tp = P.ps["tp"].ap()

    for half in range(2):
        prev_half_done = E.cnt["dve"]
        xn_src = xn1[:, half * 512: half * 512 + 15 * TOWN + 512]
        for nm, buf, bk in (("wq_t", q1, "bq"), ("wk_t", k1, "bk"),
                            ("wv_t", v1, "bv")):
            wq = _WQ(E, dram[nm], 16 * NCH, 16, wbuf)

            @block.gpsimd
            def _(g, wq=wq):
                wq.emit_loads(g)

            def dstf(mi, h2, buf=buf):
                return buf[:, mi * 512:(mi + 1) * 512]
            _emit_proj(P, _BC, wq, xn_src, 512, 16, dstf,
                       AF.Identity, 0,
                       bias_of=lambda mi, bk=bk: bia[bk][:, mi:mi + 1],
                       pre_waits=[("xq", xq_c), ("dve", prev_half_done)],
                       xn_row=TOWN)
        qkv_act = E.cnt["act"]

        cs_a = cosb[0:64, half * 512: half * 512 + 512]
        sn_a = sinb[0:64, half * 512: half * 512 + 512]
        cs_b = cosb[64:128, half * 512: half * 512 + 512]
        sn_b = sinb[64:128, half * 512: half * 512 + 512]

        @block.vector
        def _(v, qkv_act=qkv_act):
            E.wait(v, "act", qkv_act)
            E.wait(v, "xq", xq_c)
            last = None
            for buf in (q1, k1):
                for h in range(NH):
                    A = buf[0:64, h * 512:(h + 1) * 512]
                    Br = buf[64:128, h * 512:(h + 1) * 512]
                    v.tensor_tensor(rt[0][0:64, :], A, cs_a, ALU.mult)
                    v.tensor_tensor(rt[1][0:64, :], Br, sn_b, ALU.mult)
                    v.tensor_tensor(rt[2][0:64, :], Br, cs_b, ALU.mult)
                    v.tensor_tensor(rt[3][0:64, :], A, sn_a, ALU.mult)
                    v.tensor_tensor(A, rt[0][0:64, :], rt[1][0:64, :],
                                    ALU.subtract)
                    last = v.tensor_tensor(Br, rt[2][0:64, :], rt[3][0:64, :],
                                           ALU.add)
            E.inc(last, "dve", 1)
        rope_dve = E.cnt["dve"]

        for tt in range(4):
            col = tt * 128
            prev_tile = (half, tt - 1) if tt else (half - 1, 3)
            dve_mix_prev = E.cnt["dve"]
            for rnd in range(6):
                items = []
                for i in range(8):
                    gidx = rnd * 8 + i
                    ten, h = divmod(gidx, 16)
                    src = (q1, k1, v1)[ten]
                    dstb = (qt, kt, vt)[ten]
                    items.append((src[:, h * 512 + col: h * 512 + col + 128],
                                  dstb[:, h * 128:(h + 1) * 128]))

                @block.tensor
                def _(t, items=items, rnd=rnd):
                    E.wait(t, "dve", rope_dve)
                    P.bw(t, "tp")
                    last = None
                    for i, (srcap, _) in enumerate(items):
                        last = t.transpose(tp[:, i * 128:(i + 1) * 128],
                                           srcap, id_bf)
                    E.inc(last, "pe", 1)
                pe_tr = E.cnt["pe"]

                @block.scalar
                def _(sc_, items=items, pe_tr=pe_tr):
                    E.wait(sc_, "pe", pe_tr)
                    E.wait(sc_, "dve", dve_mix_prev)
                    last = None
                    for i, (_, dstap) in enumerate(items):
                        last = sc_.copy(dstap, tp[:, i * 128:(i + 1) * 128])
                    E.inc(last, "act", 1)
                P.bs("tp", "act", E.cnt["act"])
            tm_act = E.cnt["act"]

            @block.vector
            def _(v, tm_act=tm_act):
                E.wait(v, "act", tm_act)
                for t_ in range(NH):
                    ktb = kt[:, t_ * 128:(t_ + 1) * 128].rearrange(
                        "p (o d) -> p o d", o=1)
                    ktb = bass.AP(tensor=ktb.tensor, offset=ktb.offset,
                                  ap=[[ktb.ap[0][0], 128], [0, 16], [1, 128]])
                    v.tensor_tensor(prod.rearrange("p (h d) -> p h d", h=16),
                                    qt.rearrange("p (h d) -> p h d", h=16),
                                    ktb, ALU.mult)
                    last = v.reduce_sum(
                        rbuf[:, t_ * 16:(t_ + 1) * 16],
                        prod.rearrange("p (h d) -> p h d", h=16),
                        axis=AX.X)
                E.ss(v, last, "dve")
                E.ss(v, v.tensor_copy(
                    sc, rbuf.rearrange("p (t h) -> p h t", h=16)), "dve")
                E.ss(v, v.reduce_max(rmax, sc.rearrange("p (h t) -> p h t",
                                                        t=16), axis=AX.X),
                     "dve")
                last = None
                for h in range(NH):
                    last = v.tensor_scalar(
                        Ee[:, h * 16:(h + 1) * 16], sc[:, h * 16:(h + 1) * 16],
                        rmax[:, h:h + 1], 1.0 / math.sqrt(HD),
                        ALU.subtract, ALU.mult)
                E.inc(last, "dve", 1)
            dve_sc = E.cnt["dve"]

            @block.scalar
            def _(sc_, dve_sc=dve_sc):
                E.wait(sc_, "dve", dve_sc)
                E.inc(sc_.activation(Ee, Ee, AF.Exp), "act", 1)
            act_e = E.cnt["act"]

            @block.vector
            def _(v, act_e=act_e, tt=tt, col=col):
                E.wait(v, "act", act_e)
                E.ss(v, v.reduce_sum(rs, Ee.rearrange("p (h t) -> p h t",
                                                      t=16), axis=AX.X),
                     "dve")
                E.ss(v, v.reciprocal(rs, rs), "dve")
                for t_ in range(NH):
                    for h in range(NH):
                        a = acc[:, h * 128:(h + 1) * 128]
                        if t_ == 0:
                            v.tensor_scalar_mul(a, vt[:, 0:128],
                                                Ee[:, h * 16:h * 16 + 1])
                        else:
                            v.scalar_tensor_tensor(
                                a, vt[:, t_ * 128:(t_ + 1) * 128],
                                Ee[:, h * 16 + t_:h * 16 + t_ + 1], a,
                                ALU.mult, ALU.add)
                last = None
                if prev_tile in oq_contrib:
                    E.wait(v, oq_contrib[prev_tile][0],
                           oq_contrib[prev_tile][1])
                for h in range(NH):
                    last = v.tensor_scalar_mul(
                        acc_bf[:, h * 128:(h + 1) * 128],
                        acc[:, h * 128:(h + 1) * 128], rs[:, h:h + 1])
                E.inc(last, "dve", 1)
            dve_mix = E.cnt["dve"]

            gi = half * 4 + tt
            osem = "oq" if gi % 2 == 0 else "oq2"

            @block.sync
            def _(sy, tt=tt, half=half, dve_mix=dve_mix, osem=osem):
                E.wait(sy, "dve", dve_mix)
                u0 = 32 * half + 8 * tt
                dst = contrib_d.rearrange(
                    "(h u) (j d) -> h u j d", u=64, d=128)[:, u0:u0 + 8]
                dst = dst.rearrange("h u j d -> (u j) h d")
                srcp = acc_bf.rearrange("p (h d) -> p h d", d=128)
                E.inc(sy.dma_start(out=dst, in_=srcp), osem, 16)
            oq_contrib[(half, tt)] = (osem, E.cnt[osem])
    @block.sync
    def _(sy):
        E.wait(sy, "oq", E.cnt["oq"])
        E.wait(sy, "oq2", E.cnt["oq2"])
        sy.nop()
    ar.release()


_BC = None



def _emit_bfly(P, block, buf, n_chunks, cw, tmp2, pre=()):
    E = P.E
    stages = n_chunks.bit_length() - 1

    @block.vector
    def _(v):
        for sem, val in pre:
            E.wait(v, sem, val)
        ti = 0
        last = None
        for s in range(stages):
            step = 1 << s
            for base in range(0, n_chunks, step * 2):
                for off in range(step):
                    a = buf[:, (base + off) * cw:(base + off + 1) * cw]
                    b = buf[:, (base + off + step) * cw:
                            (base + off + step + 1) * cw]
                    t = tmp2[:, (ti % 2) * cw:(ti % 2 + 1) * cw]
                    ti += 1
                    v.tensor_tensor(t, a, b, ALU.subtract)
                    v.tensor_tensor(a, a, b, ALU.add)
                    last = v.tensor_copy(b, t)
        E.inc(last, "dve", 1)
    return E.cnt["dve"]


def _emit_h128(P, block, h128_bf, buf, n_chunks, cw, src_ready):
    E = P.E
    pe0 = E.cnt["pe"]
    act0 = E.cnt["act"]

    @block.tensor
    def _(t):
        for sem, val in src_ready:
            E.wait(t, sem, val)
        for c in range(n_chunks):
            bi = c % 4
            if c >= 4:
                E.wait(t, "act", act0 + c - 3)
            else:
                P.bw(t, ("g", bi))
            E.inc(t.matmul(P.gbanks[bi][:, 0:cw], h128_bf,
                           buf[:, c * cw:(c + 1) * cw],
                           start=True, stop=True), "pe", 1)

    @block.scalar
    def _(sc):
        for c in range(n_chunks):
            E.wait(sc, "pe", pe0 + c + 1)
            E.inc(sc.copy(buf[:, c * cw:(c + 1) * cw],
                          P.gbanks[c % 4][:, 0:cw]), "act", 1)
    for c in range(max(0, n_chunks - 4), n_chunks):
        P.bs(("g", c % 4), "act", act0 + c + 1)
    return E.cnt["act"]


def _build_C(debug=None, stage=None):
    P = _Prog()
    _emit_partC(P, fused=False, stage=stage)
    return P.finish(), P.dram, P.outs


def _emit_partC(P, fused, stage=None):
    nc, E, block, ar = P.nc, P.E, P.block, P.arena
    if fused:
        _part_barrier(P)
    ar.mark()
    dram = P.dram
    for n, shape, dt in [
        ("c_h128", [128, 128], F32), ("c_id", [128, 128], F32),
        ("c_suug", [128, 16], F32), ("c_sudp", [128, 64], F32),
        ("c_wsug", [128, 128], F32), ("c_wsdp", [128, 16], F32),
        ("c_bo", [128, 16], F32),
        ("wo_t", [16 * NCH * 128, 128], F32),
        ("wupg_t", [64 * 32 * 128, 128], F32),
        ("wdown_t", [16 * 64 * 128, 128], F32),
    ]:
        P.dcl(n, shape, dt)
    if fused:
        ag_d = P.dram["ag_d"]
        P.dcl("c_qh", [1, 4], mybir.dt.uint32)
    else:
        P.dcl("xn1_d", [128, NCH * TOWN], BF16)
        P.dcl("x1_d", [128, NCH * TOWN], F32)
        attn_d = P.dcl("attn_d", [TOWN, DM], BF16)
    outT_d = P.dcl("outT_d", [128, NCH * TOWN], F32, kind="ExternalOutput")
    P.outs = {"outT_d": outT_d}

    wbuf = ar.bf(2 * 16 * 128)
    wbuf_dn = ar.bf(2 * 32 * 128)
    h128_bf = ar.bf(128)
    id_bf = ar.bf(128)
    bo = ar.f32(16)
    suug = ar.f32(16)
    sudp = ar.f32(64)
    wsug = ar.f32(128)
    wsdp = ar.f32(16)
    wsu_eff = ar.f32(64)
    wsg_eff = ar.f32(64)
    wsd_eff = ar.f32(16)
    th = ar.bf(NCH * 512)
    h2 = ar.bf(64 * 512)
    tmp2 = ar.bf(2 * 512)
    xnsl = ar.bf(2 * 512)
    atile = ar.bf(2 * 2048)
    attn_fm = ar.bf(NCH * 512)
    s1h = ar.bf(NCH * 512)
    x1sl = ar.f32(2 * 512)
    outsl = ar.f32(2 * 512)
    g1sl = ar.bf(4 * 512)
    updf = ar.f32(512)
    qh_sb = ar.bf(16).bitcast(mybir.dt.uint32)

    @block.gpsimd
    def _(g):
        E.inc(g.dma_start(out=h128_bf, in_=dram["c_h128"]), "cc", 16)
        E.inc(g.dma_start(out=id_bf, in_=dram["c_id"]), "cc", 16)
    if fused:
        @block.sync
        def _(sy):
            E.inc(sy.dma_start(out=qh_sb[0:1, 0:4], in_=dram["c_qh"]),
                  "cc", 16)
    cc_all = E.cnt["cc"]

    @block.sync
    def _(sy):
        for dst, srcn in ((suug, "c_suug"), (sudp, "c_sudp"),
                          (wsug, "c_wsug"), (wsdp, "c_wsdp"),
                          (bo, "c_bo")):
            E.inc(sy.dma_start(out=dst, in_=dram[srcn]), "xq", 16)
    xq_c = E.cnt["xq"]

    @block.vector
    def _(v):
        E.wait(v, "xq", xq_c)
        v.scalar_tensor_tensor(wsu_eff, wsug[:, 0:64],
                               1.0 / math.sqrt(DM), sudp, ALU.mult, ALU.mult)
        v.tensor_scalar_mul(wsg_eff, wsug[:, 64:128], 1.0 / math.sqrt(DM))
        E.inc(v.tensor_scalar_mul(wsd_eff, wsdp, 1.0 / math.sqrt(INTER)),
              "dve", 1)
    scales_dve = E.cnt["dve"]
    tp = P.ps["tp"].ap()

    for half in range(2):
        dve_prev_half = E.cnt["dve"]
        xq_at = {}
        for tt in range(4):
            @block.sync
            def _(sy, tt=tt, half=half, dve_gate=E.cnt["dve"]):
                E.wait(sy, "pe", E.cnt["pe"])
                if tt == 0:
                    E.wait(sy, "dve", dve_gate)
                    E.wait(sy, "cc", cc_all)
                t0 = half * 512 + tt * 128
                dst = atile[:, (tt % 2) * 2048:(tt % 2) * 2048 + 2048]
                if fused:
                    hh = t0 // 256
                    r0 = (t0 % 256) // 64
                    idx = sy.value_load(qh_sb[0:1, hh:hh + 1])
                    ag4 = ag_d.rearrange("(r i z) e -> r i z e", i=16, z=64)
                    for k in range(2):
                        srcp = ag4[r0 + k:r0 + k + 1,
                                   bass.ts(idx, 1)].rearrange(
                            "r i z e -> (r i z) e")
                        E.inc(sy.dma_start(out=dst[k * 64:(k + 1) * 64, :],
                                           in_=srcp), "xq", 16)
                else:
                    E.inc(sy.dma_start(out=dst,
                                       in_=dram["attn_d"][t0:t0 + 128, :]),
                          "xq", 16)
            xq_at[tt] = E.cnt["xq"]

            for rnd in range(2):
                @block.tensor
                def _(t, tt=tt, rnd=rnd):
                    E.wait(t, "xq", xq_at[tt])
                    P.bw(t, "tp")
                    last = None
                    for i in range(8):
                        j = rnd * 8 + i
                        last = t.transpose(
                            tp[:, i * 128:(i + 1) * 128],
                            atile[:, (tt % 2) * 2048 + j * 128:
                                  (tt % 2) * 2048 + (j + 1) * 128], id_bf)
                    E.inc(last, "pe", 1)
                pe_tr = E.cnt["pe"]

                @block.scalar
                def _(sc_, tt=tt, rnd=rnd, pe_tr=pe_tr):
                    E.wait(sc_, "pe", pe_tr)
                    last = None
                    for i in range(8):
                        j = rnd * 8 + i
                        last = sc_.copy(
                            attn_fm[:, j * 512 + tt * 128:
                                    j * 512 + tt * 128 + 128],
                            tp[:, i * 128:(i + 1) * 128])
                    E.inc(last, "act", 1)
                P.bs("tp", "act", E.cnt["act"])
        attn_act = E.cnt["act"]

        wo_wq = _WQ(E, dram["wo_t"], 16 * NCH, 16, wbuf)

        @block.gpsimd
        def _(g, wo_wq=wo_wq):
            wo_wq.emit_loads(g)

        _emit_proj(P, _BC, wo_wq, attn_fm, 512, 16,
                   lambda mi, h_: s1h[:, mi * 512:(mi + 1) * 512],
                   AF.Identity, 0, bias_of=lambda mi: bo[:, mi:mi + 1],
                   pre_waits=[("act", attn_act), ("dve", dve_prev_half)],
                   xn_row=512)
        s1_act = E.cnt["act"]
        _fence(P, "wo")

        dve_base = E.cnt["dve"]

        pool_plan = {}

        @block.gpsimd
        def _(g, half=half, dve_base=dve_base):
            E.wait(g, "dve", dve_base)
            for m in range(NCH):
                if m >= 2:
                    E.wait(g, "dve", dve_base + m - 1)
                ps_ = "pool" if m % 2 == 0 else "pool2"
                E.inc(g.dma_start(
                    out=xnsl[:, (m % 2) * 512:(m % 2) * 512 + 512],
                    in_=dram["xn1_d"][:, m * TOWN + half * 512:
                                      m * TOWN + half * 512 + 512]),
                    ps_, 16)
                pool_plan[m] = (ps_, E.cnt[ps_])

        @block.vector
        def _(v):
            E.wait(v, "xq", xq_c)
            E.wait(v, "pe", E.cnt["pe"])
            for m in range(NCH):
                E.wait(v, pool_plan[m][0], pool_plan[m][1])
                E.inc(v.tensor_scalar_mul(
                    th[:, m * 512:(m + 1) * 512],
                    xnsl[:, (m % 2) * 512:(m % 2) * 512 + 512],
                    suug[:, m:m + 1]), "dve", 1)
        th_dve = E.cnt["dve"]

        h128_act = _emit_h128(P, block, h128_bf, th, NCH, 512,
                              [("dve", th_dve), ("cc", cc_all)])
        xh_dve = _emit_bfly(P, block, th, NCH, 512, tmp2,
                            pre=[("act", h128_act)])
        if stage == "dumps" and half == 1:
            dxh = P.dcl("dxh", [128, NCH * 512], BF16, kind="ExternalOutput")

            @block.sync
            def _(sy):
                E.wait(sy, "dve", xh_dve)
                E.inc(sy.dma_start(out=dxh, in_=th), "oq", 16)
        _fence(P, "xh")

        upg_wq = _WQ(E, dram["wupg_t"], 64 * 32, 16, wbuf)

        @block.gpsimd
        def _(g, upg_wq=upg_wq):
            upg_wq.emit_loads(g)

        pe0 = E.cnt["pe"]
        act0 = E.cnt["act"]
        dve0 = E.cnt["dve"]
        usl = g1sl

        @block.tensor
        def _(t, upg_wq=upg_wq, pe0=pe0, act0=act0):
            E.wait(t, "dve", xh_dve)
            for M in range(64):
                pair = M % 2
                if M >= 2:
                    E.wait(t, "act", act0 + M - 1)
                else:
                    P.bw(t, ("g", 2 * pair))
                    P.bw(t, ("g", 2 * pair + 1))
                for k in range(NCH):
                    t.matmul(P.gbanks[2 * pair], upg_wq.tile(t, M * 32 + k),
                             th[:, k * 512:(k + 1) * 512],
                             start=(k == 0), stop=(k == NCH - 1))
                last = None
                for k in range(NCH):
                    last = t.matmul(P.gbanks[2 * pair + 1],
                                    upg_wq.tile(t, M * 32 + 16 + k),
                                    th[:, k * 512:(k + 1) * 512],
                                    start=(k == 0), stop=(k == NCH - 1))
                E.inc(last, "pe", 1)

        @block.scalar
        def _(sc, pe0=pe0, dve0=dve0):
            E.wait(sc, "dve", scales_dve)
            for M in range(64):
                pair = M % 2
                E.wait(sc, "pe", pe0 + M + 1)
                if M >= 2:
                    E.wait(sc, "dve", dve0 + M - 1)
                sc.activation(usl[:, pair * 512:(pair + 1) * 512],
                              P.gbanks[2 * pair], AF.Identity,
                              scale=wsu_eff[:, M:M + 1])
                last = sc.activation(usl[:, (2 + pair) * 512:(3 + pair) * 512],
                                     P.gbanks[2 * pair + 1], AF.Silu,
                                     scale=wsg_eff[:, M:M + 1])
                E.inc(last, "act", 1)
                E.inc(sc.copy(P.dummy, P.dummy), "sfree", 2)

        @block.vector
        def _(v, act0=act0):
            for M in range(64):
                pair = M % 2
                E.wait(v, "act", act0 + M + 1)
                E.inc(v.tensor_tensor(
                    h2[:, M * 512:(M + 1) * 512],
                    usl[:, (2 + pair) * 512:(3 + pair) * 512],
                    usl[:, pair * 512:(pair + 1) * 512], ALU.mult), "dve", 1)
        for bi in range(4):
            P.bs(("g", bi), "act", E.cnt["act"])

        _fence(P, "upg")

        h128b_act = _emit_h128(P, block, h128_bf, h2, 64, 512,
                               [("dve", E.cnt["dve"])])
        h2_dve = _emit_bfly(P, block, h2, 64, 512, tmp2,
                            pre=[("act", h128b_act)])
        if stage == "dumps" and half == 1:
            dh2 = P.dcl("dh2", [128, 64 * 512], BF16, kind="ExternalOutput")
            dh2s = P.dcl("dh2s", [128, NCH * 512], BF16,
                         kind="ExternalOutput")

            @block.sync
            def _(sy):
                E.wait(sy, "dve", h2_dve)
                E.inc(sy.dma_start(out=dh2, in_=h2), "oq", 16)
                E.inc(sy.dma_start(out=dh2s, in_=s1h), "oq", 16)

        _fence(P, "fw8")

        down_wq = _WQ(E, dram["wdown_t"], 16 * 64, 32, wbuf_dn)

        @block.gpsimd
        def _(g, down_wq=down_wq):
            down_wq.emit_loads(g)

        xq_x1b = E.cnt["xq"]
        oq_b = E.cnt["oq"]
        dve_dn = E.cnt["dve"]

        xq_plan = {}
        oq_plan = {}

        @block.sync
        def _(sy, half=half, dve_dn=dve_dn):
            for mi in range(NCH):
                if mi >= 2:
                    E.wait(sy, "dve", dve_dn + mi - 1)
                xs_ = "xq" if mi % 2 == 0 else "xq2"
                E.inc(sy.dma_start(
                    out=x1sl[:, (mi % 2) * 512:(mi % 2) * 512 + 512],
                    in_=dram["x1_d"][:, mi * TOWN + half * 512:
                                     mi * TOWN + half * 512 + 512]),
                    xs_, 16)
                xq_plan[mi] = (xs_, E.cnt[xs_])
                if mi >= 1:
                    E.wait(sy, "dve", dve_dn + mi)
                    os_ = "oq" if (mi - 1) % 2 == 0 else "oq2"
                    E.inc(sy.dma_start(
                        out=outT_d[:, (mi - 1) * TOWN + half * 512:
                                   (mi - 1) * TOWN + half * 512 + 512],
                        in_=outsl[:, ((mi - 1) % 2) * 512:
                                  ((mi - 1) % 2) * 512 + 512]), os_, 16)
                    oq_plan[mi - 1] = (os_, E.cnt[os_])
            E.wait(sy, "dve", dve_dn + NCH)
            os_ = "oq" if 15 % 2 == 0 else "oq2"
            E.inc(sy.dma_start(
                out=outT_d[:, 15 * TOWN + half * 512:
                           15 * TOWN + half * 512 + 512],
                in_=outsl[:, (15 % 2) * 512:(15 % 2) * 512 + 512]), os_, 16)
            oq_plan[15] = (os_, E.cnt[os_])

        def d_post(v, mi, h_, half=half):
            E.wait(v, xq_plan[mi][0], xq_plan[mi][1])
            if mi >= 2:
                E.wait(v, oq_plan[mi - 2][0], oq_plan[mi - 2][1])
            v.tensor_tensor(updf, g1sl[:, (mi % 4) * 512:(mi % 4 + 1) * 512],
                            s1h[:, mi * 512:(mi + 1) * 512], ALU.mult)
            return v.tensor_tensor(
                outsl[:, (mi % 2) * 512:(mi % 2) * 512 + 512],
                x1sl[:, (mi % 2) * 512:(mi % 2) * 512 + 512], updf, ALU.add)

        _emit_proj(P, _BC, down_wq, h2, 512, 16,
                   lambda mi, h_: g1sl[:, (mi % 4) * 512:(mi % 4 + 1) * 512],
                   AF.Silu, 0, scale_of=lambda mi: wsd_eff[:, mi:mi + 1],
                   pre_waits=[("dve", h2_dve), ("act", s1_act)],
                   post=d_post, n_k=64, xn_row=512)

    @block.sync
    def _(sy):
        E.wait(sy, "oq", E.cnt["oq"])
        E.wait(sy, "oq2", E.cnt["oq2"])
        sy.nop()
    ar.release()


def _build_full():
    P = _Prog()
    _emit_partA(P, fused=True)
    _emit_partB(P, fused=True)
    E, block = P.E, P.block
    contrib_d = P.dram["contrib_d"]
    ag_d = P.dcl("ag_d", [4 * TOWN, DM], BF16, kind="Internal")

    @block.gpsimd
    def _(g):
        E.wait(g, "oq", E.cnt["oq"])
        E.wait(g, "oq2", E.cnt["oq2"])
        E.inc(g.collective_compute(
            "AllGather", ALU.bypass,
            replica_groups=[[0, 1, 2, 3], [4, 5, 6, 7]],
            ins=[contrib_d[:, :]], outs=[ag_d[:, :]]), "cc", 1)
    _emit_partC(P, fused=True)
    return P.finish(), P.dram, P.outs


def _prep_C_inputs(inputs):
    d = {"c_id": np.eye(128, dtype=np.float32)}
    Wo = np.asarray(inputs["L1_attn_Wo"], np.float32)
    d["wo_t"] = _tiles_km(np.ascontiguousarray(Wo.T))
    d["c_bo"] = np.ascontiguousarray(
        np.asarray(inputs["L1_attn_bo"], np.float32).reshape(NCH, 128).T)
    H = np.array([[1.0]], dtype=np.float32)
    while H.shape[0] < 128:
        H = np.block([[H, H], [H, -H]])
    d["c_h128"] = H
    d["c_suug"] = np.ascontiguousarray(
        np.asarray(inputs["L1_ffn_SU_ug"], np.float32).reshape(NCH, 128).T)
    d["c_sudp"] = np.ascontiguousarray(
        np.asarray(inputs["L1_ffn_SU_dp"], np.float32).reshape(64, 128).T)
    ws = np.asarray(inputs["L1_ffn_Wscale_ug"], np.float32)
    d["c_wsug"] = np.ascontiguousarray(np.concatenate(
        [ws[:INTER].reshape(64, 128).T, ws[INTER:].reshape(64, 128).T],
        axis=1))
    d["c_wsdp"] = np.ascontiguousarray(
        np.asarray(inputs["L1_ffn_Wscale_dp"], np.float32).reshape(NCH, 128).T)
    W_up = np.asarray(inputs["L1_ffn_W_up"], np.float32)
    W_gate = np.asarray(inputs["L1_ffn_W_gate"], np.float32)
    up_t = _tiles_km(np.ascontiguousarray(W_up.T)).reshape(64, 16 * 128, 128)
    g_t = _tiles_km(np.ascontiguousarray(W_gate.T)).reshape(64, 16 * 128, 128)
    d["wupg_t"] = np.ascontiguousarray(
        np.concatenate([up_t, g_t], axis=1).reshape(-1, 128))
    d["wdown_t"] = _tiles_km(np.ascontiguousarray(
        np.asarray(inputs["L1_ffn_W_down"], np.float32).T))
    return d



def _core_slices(x):
    res = []
    for c in range(8):
        b, q = divmod(c, 4)
        own = np.ascontiguousarray(x[b, 1024 * q:1024 * (q + 1), :].T)
        pre = np.zeros((DM, TPRE), dtype=np.float32)
        lo = 1024 * q - TPRE
        seg = x[b, max(0, lo):1024 * q, :]
        if seg.shape[0]:
            pre[:, TPRE - seg.shape[0]:] = seg.T
        res.append((own, pre))
    return res


def _prep_A_inputs(inputs):
    d = {}
    d.update(_const_tiles())
    slope = np.asarray(inputs["L0_attn_slope_rate"], np.float32).reshape(1, NH)
    d["c_slope"] = np.broadcast_to(slope, (128, NH)).copy()
    d["c_temp"] = np.broadcast_to(
        np.asarray(inputs["L0_ffn_temperature"], np.float32).reshape(1, 1),
        (128, 1)).copy()
    d["c_nw"] = np.ascontiguousarray(
        np.asarray(inputs["L0_attn_norm_w"], np.float32).reshape(NCH, 128).T)
    wqkv = np.asarray(inputs["L0_attn_w_qkv"], np.float32)
    d["wqkv_t"] = _tiles_km(np.ascontiguousarray(wqkv.T))
    kv_rows = np.concatenate([np.arange(h * 384 + 128, h * 384 + 384)
                              for h in range(NH)])
    d["wkv_t"] = _tiles_km(np.ascontiguousarray(wqkv[kv_rows].T))
    d["wgate_t"] = _tiles_km(np.ascontiguousarray(
        np.asarray(inputs["L0_attn_w_gate"], np.float32).T))
    d["wout_t"] = _tiles_km(np.ascontiguousarray(
        np.asarray(inputs["L0_attn_w_out"], np.float32).T))
    d["wr_t"] = np.ascontiguousarray(np.asarray(inputs["L0_ffn_Wr"],
                                                np.float32))
    sel = np.zeros((8, 8 * 128), dtype=np.float32)
    for e in range(8):
        sel[e, e * 128:(e + 1) * 128] = 1.0
    d["c_sel8"] = sel
    Wg = np.asarray(inputs["L0_ffn_Wg"], np.float32)
    Wu = np.asarray(inputs["L0_ffn_Wu"], np.float32)
    Wd = np.asarray(inputs["L0_ffn_Wd"], np.float32)
    blocks = []
    for e in range(NEXP):
        wg_t = _tiles_km(Wg[e]).reshape(16, 16 * 128, 128)
        wu_t = _tiles_km(Wu[e]).reshape(16, 16 * 128, 128)
        inter = np.concatenate([wg_t, wu_t], axis=1).reshape(-1, 128)
        blocks.append(inter)
        blocks.append(_tiles_km(Wd[e]))
    d["ffn0_t"] = np.ascontiguousarray(np.concatenate(blocks, axis=0))
    return d


def _rope_rows():
    p = np.zeros(128, np.int64)
    p[0:64] = np.arange(64) * 2
    p[64:128] = np.arange(64) * 2 + 1
    return np.concatenate([h * 128 + p for h in range(NH)])


def _prep_B_inputs(inputs):
    d = {"c_id": np.eye(128, dtype=np.float32)}
    rows = _rope_rows()

    def chunked(v):
        return np.ascontiguousarray(
            np.asarray(v, np.float32).reshape(NCH, 128).T)

    for nm, wkey, bkey, permute in (
            ("q", "L1_attn_Wq", "L1_attn_bq", True),
            ("k", "L1_attn_Wk", "L1_attn_bk", True),
            ("v", "L1_attn_Wv", "L1_attn_bv", False),
            ("o", "L1_attn_Wo", "L1_attn_bo", False)):
        W = np.asarray(inputs[wkey], np.float32)
        b = np.asarray(inputs[bkey], np.float32)
        if permute:
            W = W[rows]
            b = b[rows]
        d[f"w{nm}_t"] = _tiles_km(np.ascontiguousarray(W.T))
        d[f"c_b{nm}"] = chunked(b)
    return d


def _rope_tables(core):
    b, q = divmod(core, 4)
    pos = (1024 * q + np.arange(TOWN, dtype=np.float64))
    inv = 1.0 / (10000.0 ** (np.arange(0, 128, 2, dtype=np.float64) / 128.0))
    fr = np.outer(inv, pos)
    c = np.cos(fr).astype(np.float32)
    s = np.sin(fr).astype(np.float32)
    return (np.concatenate([c, c], 0), np.concatenate([s, s], 0))


def _run_A_debug(inputs, debug="p2", cores=(0,)):
    nc, dram, outs = _build_A(debug=debug)
    shared = _prep_A_inputs(inputs)
    slices = _core_slices(np.asarray(inputs["L0_attn_hidden_states"],
                                     np.float32))
    in_maps = []
    for c in cores:
        m = dict(shared)
        m["xT"], m["xTpre"] = slices[c]
        in_maps.append(m)
    res = run_bass_kernel_spmd(nc, in_maps, core_ids=list(range(len(cores))))
    return res.results


def _filter(m, dram):
    return {k: m[k] for k in dram if k in m}


FUSED = True


def kernel(**inputs):
    if FUSED:
        return _kernel_fused(**inputs)
    return _kernel_unfused(**inputs)


def _kernel_fused(**inputs):
    inputs = {k: np.asarray(v) for k, v in inputs.items()}
    x = np.asarray(inputs["L0_attn_hidden_states"], np.float32)
    nc, dram, outs = _build_full()
    shared = {}
    shared.update(_prep_A_inputs(inputs))
    shared.update(_prep_B_inputs(inputs))
    shared.update(_prep_C_inputs(inputs))
    slices = _core_slices(x)
    in_maps = []
    for c in range(8):
        b, q = divmod(c, 4)
        m = dict(shared)
        m["xT"], m["xTpre"] = slices[c]
        m["c_cos"], m["c_sin"] = _rope_tables(c)
        m["c_qh"] = np.array([[q * 4 + hh for hh in range(4)]], np.uint32)
        in_maps.append(_filter(m, dram))
    res = run_bass_kernel_spmd(nc, in_maps, core_ids=list(range(8)))
    out = np.empty((2, 4096, 2048), np.float32)
    for c in range(8):
        b, q = divmod(c, 4)
        ot = np.asarray(res.results[c]["outT_d"], np.float32)
        out[b, 1024 * q:1024 * (q + 1), :] = ot.reshape(
            128, 16, 1024).transpose(1, 0, 2).reshape(2048, 1024).T
    return out


def _kernel_unfused(**inputs):
    inputs = {k: np.asarray(v) for k, v in inputs.items()}
    x = np.asarray(inputs["L0_attn_hidden_states"], np.float32)

    ncA, dramA, _ = _build_A()
    sharedA = _prep_A_inputs(inputs)
    slices = _core_slices(x)
    in_maps = []
    for c in range(8):
        m = dict(sharedA)
        m["xT"], m["xTpre"] = slices[c]
        in_maps.append(_filter(m, dramA))
    resA = run_bass_kernel_spmd(ncA, in_maps, core_ids=list(range(8)))

    ncB, dramB, _ = _build_B()
    sharedB = _prep_B_inputs(inputs)
    in_maps = []
    for c in range(8):
        m = dict(sharedB)
        m["c_cos"], m["c_sin"] = _rope_tables(c)
        m["xn1_d"] = np.asarray(resA.results[c]["xn1_d"])
        in_maps.append(_filter(m, dramB))
    resB = run_bass_kernel_spmd(ncB, in_maps, core_ids=list(range(8)))

    contribs = [np.asarray(resB.results[c]["contrib_d"]) for c in range(8)]
    t_idx = np.arange(1024)
    attn_new = []
    for c in range(8):
        b, q = divmod(c, 4)
        h = 4 * q + t_idx // 256
        r = (t_idx % 256) // 64
        ul = t_idx % 64
        arr = np.stack([contribs[4 * b + j] for j in range(4)])
        attn_new.append(np.ascontiguousarray(arr[r, h * 64 + ul, :]))

    ncC, dramC, _ = _build_C()
    sharedC = _prep_C_inputs(inputs)
    in_maps = []
    for c in range(8):
        m = dict(sharedC)
        m["xn1_d"] = np.asarray(resA.results[c]["xn1_d"])
        m["x1_d"] = np.asarray(resA.results[c]["x1_d"])
        m["attn_d"] = attn_new[c]
        in_maps.append(_filter(m, dramC))
    resC = run_bass_kernel_spmd(ncC, in_maps, core_ids=list(range(8)))

    out = np.empty((2, 4096, 2048), np.float32)
    for c in range(8):
        b, q = divmod(c, 4)
        ot = np.asarray(resC.results[c]["outT_d"], np.float32)
        out[b, 1024 * q:1024 * (q + 1), :] = ot.reshape(
            128, 16, 1024).transpose(1, 0, 2).reshape(2048, 1024).T
    return out
